# version 26
# baseline (speedup 1.0000x reference)
import numpy as np
from contextlib import ExitStack

import concourse.bass as bass
import concourse.mybir as mybir
from concourse.bass_utils import run_bass_kernel_spmd

F32 = mybir.dt.float32
BF16 = mybir.dt.bfloat16
I16 = mybir.dt.int16
I8 = mybir.dt.int8
I32 = mybir.dt.int32
AF = mybir.ActivationFunctionType
ALU = mybir.AluOpType
AX = mybir.AxisListType

B, L, D, H, HD, INNER = 16, 2048, 1024, 16, 64, 4096
NC = 8
SEQ = 2
NT = L // 128
NG = L // 512
NP = H // 2
TB = 512
NB = SEQ * L // TB
EPS_QK = float(1024 * 1e-6)
EPS_LN = 1e-5
EPS_OP = float(1024 * 1e-5)

VB0 = 0
LF0 = 16640
QB0 = 33024
KB0 = 35072
R10 = 37120
OS0 = 39168
ZB0 = 39680
JK0 = 41728
BIGN = 42752
WV0 = QB0
XQ0 = LF0
L2T0 = LF0 + 4096
XT0 = LF0 + 6144
M10 = LF0 + 8192
CI0 = LF0 + 8704
CR0 = LF0 + 9728
JF0 = LF0 + 11264
OT0 = LF0 + 12288
PW0 = 0
PO0 = 8192
PT0 = 12288


class Plan:
    def __init__(self):
        self.ops = []
        self.cnt = {}
        self.pending = {}

    def n(self, sem):
        return self.cnt.get(sem, 0)

    def bar(self, engines, sems_):
        snap = [(s, self.n(s)) for s in sems_ if self.n(s) > 0]
        for e in engines:
            self.pending.setdefault(e, []).extend(snap)

    def op(self, eng, fn, waits=(), inc=None, k=1):
        incs = ()
        if inc is not None:
            self.cnt[inc] = self.cnt.get(inc, 0) + k
            incs = ((inc, k),)
        w = list(self.pending.pop(eng, []))
        w += [(s, v) for (s, v) in waits if v > 0]
        best = {}
        for s, v in w:
            best[s] = max(best.get(s, 0), v)
        self.ops.append((eng, fn, tuple(best.items()), incs))
        return self.cnt.get(inc, 0)

    def dma(self, eng, fn, waits=(), queue="y", nrot=8):
        key = ("dmaidx", queue)
        i = self.cnt.get(key, 0)
        self.cnt[key] = i + 1
        semn = "d%s%d" % (queue, i % nrot)
        v = self.n(semn) + 16
        w = list(waits)
        if v > 16:
            w.append((semn, v - 16))
        self.op(eng, fn, w, inc=semn, k=16)
        return (semn, v)


def build():
    nc = bass.Bass("TRN2", target_bir_lowering=False)
    es = ExitStack()

    dtn = lambda n, s, d, k: nc.dram_tensor(n, s, d, kind=k).ap()
    x_in = dtn("x_in", [SEQ * L, D], F32, "ExternalInput")
    wqk = dtn("wqk", [D, 2 * D], F32, "ExternalInput")
    wv = dtn("wv", [D, D], F32, "ExternalInput")
    wout = dtn("wout", [D, D], F32, "ExternalInput")
    qs_in = dtn("qs", [1, HD], F32, "ExternalInput")
    qb_in = dtn("qb", [1, HD], F32, "ExternalInput")
    ks_in = dtn("ks", [1, HD], F32, "ExternalInput")
    kb_in = dtn("kb", [1, HD], F32, "ExternalInput")
    wg_in = dtn("wg", [INNER, D], F32, "ExternalInput")
    wu_in = dtn("wu", [INNER, D], F32, "ExternalInput")
    wd_in = dtn("wd", [D, INNER], F32, "ExternalInput")
    out_hbm = dtn("out", [SEQ * L, D], F32, "ExternalOutput")
    s_wqk = dtn("s_wqk", [D, 2 * D], BF16, "Internal")
    s_wg = dtn("s_wg", [INNER, D], I8, "Internal")
    s_wu = dtn("s_wu", [INNER, D], I8, "Internal")
    s_wd = dtn("s_wd", [INNER, D], I8, "Internal")
    s_x1 = dtn("s_x1", [SEQ * L, D], F32, "Internal")

    sb = lambda n, shape, d: es.enter_context(nc.sbuf_tensor(n, shape, d))
    ident = sb("ident", [128, 128], BF16)
    onesb = sb("onesb", [128, 128], BF16)
    opair = sb("opair", [128, 128], BF16)
    cst = sb("cst", [128, 16], BF16)
    ltri = sb("ltri", [128, 128], BF16)
    masks = sb("masks", [128, 1024], BF16)
    iot = sb("iot", [128, 512], I32)
    iotb = iot.bitcast(BF16)
    sQ = sb("sQ", [128, 1], F32); bQ = sb("bQ", [128, 1], F32)
    sK = sb("sK", [128, 1], F32); bK = sb("bK", [128, 1], F32)
    shiftp = sb("shiftp", [128, 1], F32)
    mbias = sb("mbias", [128, 1], F32)
    nshift2 = sb("nshift2", [128, 1], F32)
    shiftW = sb("shiftW", [128, 1], F32)
    nshiftW = sb("nshiftW", [128, 1], F32)
    iswg = sb("iswg", [128, 1], F32)
    iswu = sb("iswu", [128, 1], F32)
    kapp = sb("kapp", [128, 1], F32)
    swq = {"wg": sb("swgq", [128, 1], F32), "wu": sb("swuq", [128, 1], F32),
           "wd": sb("swdq", [128, 1], F32)}
    smal = sb("smal", [128, 128], F32)
    statq = sb("statq", [128, 32], F32)
    bn6 = sb("bn6", [128, 16], F32)
    mv2 = sb("mv2", [128, 8], F32)
    big = sb("big", [128, BIGN], BF16)
    wsA = sb("wsA", [128, 8192], BF16)
    wsB = sb("wsB", [128, 4096], F32)
    _i16 = sb("i16a", [128, 1024], I16)
    i16d = (_i16, _i16)
    obufT = sb("obufT", [128, 16384], BF16)
    wqk2 = sb("wqk2", [128, 4096], BF16)
    qkbuf2 = sb("qkbuf2", [128, 8192], BF16)
    jF32 = sb("jF32", [128, 1024], F32)
    ln2f = sb("ln2f", [128, 2048], F32)
    crepf = sb("crepf", [128, 1536], F32)
    rrow = sb("rrow", [128, 1536], F32)
    lstag = rrow[:, 1024:1536]
    l_all = sb("l_all", [128, 512], F32)
    rlb = sb("rlb", [128, 512], BF16)
    rl2 = sb("rl2", [128, 512], BF16)

    ps_ = lambda n, shape: es.enter_context(nc.psum_tensor(n, shape, F32))
    pS = ps_("pS", [128, 1024])
    pS2 = ps_("pS2", [128, 1024])
    pT = ps_("pT", [128, 1024])
    pU = ps_("pU", [128, 1024])

    semnames = ["pt", "pv", "ps", "pg"]
    for q in ("y", "s", "g"):
        semnames += ["d%s%d" % (q, i) for i in range(8)]
    sems = {s: es.enter_context(nc.semaphore(s)) for s in semnames}

    P = Plan()
    OP = P.op
    DMA = P.dma
    ALL_ENG = ("y", "s", "v", "t", "g")
    COMPUTE_SEMS = tuple(semnames)

    def BAR():
        P.bar(ALL_ENG, COMPUTE_SEMS)

    def mm(out_ap, lhsT, rhs, start, stop, waits=(), tp=None):
        return OP("t", lambda e, o=out_ap, l=lhsT, r=rhs, a=start, b=stop, t_=tp:
                  e.matmul(o, l, r, start=a, stop=b, tile_position=t_,
                           skip_group_check=True),
                  waits, "pt")

    def rsq(dst, src, waits, scale_half=True):
        OP("s", lambda e, d=dst, s_=src: e.activation(d, s_, AF.Ln), waits, "ps")
        sc = -0.5 if scale_half else -1.0
        OP("s", lambda e, d=dst, sc=sc: e.activation(d, d, AF.Exp, scale=sc),
           [("ps", P.n("ps"))], "ps")

    tap_outs = []

    def tap(name, ap_fn, shape, dtype=BF16):
        if not TAP.get(name):
            return
        t = nc.dram_tensor("tap_" + name, shape, dtype, kind="ExternalOutput").ap()
        P.bar(("y",), COMPUTE_SEMS)
        tok = DMA("y", lambda e, t=t, ap_fn=ap_fn: e.dma_start(t, ap_fn()), ())
        for e_ in ALL_ENG:
            P.pending.setdefault(e_, []).append(tok)
        tap_outs.append(name)

    bigf = lambda o, n: big[:, o:o + 2 * n].bitcast(F32)
    vbuf = lambda c: big[:, VB0 + c * 1040: VB0 + (c + 1) * 1040]
    ln1f = lambda ic: big[:, LF0 + ic * 2048: LF0 + (ic + 1) * 2048]
    qbp = qkbuf2[:, 0:4096]
    kbp = qkbuf2[:, 4096:8192]
    wqkp = big[:, R10:R10 + 2048]
    pbuf = lambda j: big[:, R10 + j * 1024: R10 + (j + 1) * 1024]
    ostag = big[:, OS0:OS0 + 512]
    lnt = lambda j: big[:, ZB0 + j * 1024: ZB0 + (j + 1) * 1024]
    junkA = big[:, JK0:JK0 + 1024]
    wvo = lambda ic: big[:, WV0 + ic * 1024: WV0 + (ic + 1) * 1024]
    tblk = lambda c: big[:, VB0 + c * 512: VB0 + (c + 1) * 512]
    xqf = lambda hc: big[:, XQ0 + hc * 512: XQ0 + (hc + 1) * 512]
    ln2t = lambda j: big[:, L2T0 + j * 1024: L2T0 + (j + 1) * 1024]
    xqtm = lambda j: big[:, XT0 + j * 1024: XT0 + (j + 1) * 1024]
    M1 = big[:, M10:M10 + 512]
    cin = lambda j: big[:, CI0 + j * 128: CI0 + (j + 1) * 128]
    cstage = big[:, CI0 + 512:CI0 + 640]
    stin = big[:, CI0 + 640:CI0 + 768]
    ststage = big[:, CI0 + 768:CI0 + 896]
    crep = lambda j: big[:, CR0 + j * 512: CR0 + (j + 1) * 512]
    junkF = big[:, JF0:JF0 + 1024]
    outst = lambda j: big[:, OT0 + j * 2048: OT0 + (j + 1) * 2048].bitcast(F32)

    obuf = lambda s, p: obufT[:, p * 2048:(p + 1) * 2048]

    OP("g", lambda e: e.iota(iot[:], [[-1, 512]], base=0, channel_multiplier=1), (), "pg")
    gi = P.n("pg")
    OP("v", lambda e: e.tensor_scalar(ident[:], iot[:, 0:128], 0.0, None, ALU.is_equal),
       [("pg", gi)], "pv")
    OP("v", lambda e: e.tensor_scalar(ltri[:], iot[:, 0:128], 0.0, None, ALU.is_le),
       [("pg", gi)], "pv")
    OP("v", lambda e: e.tensor_scalar(masks[:, 512:1024], iot[:], 1.0, -1e30,
                                      ALU.is_equal, ALU.mult), [("pg", gi)], "pv")
    OP("v", lambda e: e.memset(masks[:, 0:512], 0.0), (), "pv")
    OP("v", lambda e: e.memset(masks[0:1, 0:512], -1e30), [("pv", P.n("pv"))], "pv")
    OP("v", lambda e: e.memset(onesb[:], 1.0), (), "pv")
    OP("v", lambda e: e.memset(opair[:], 0.0), (), "pv")
    OP("v", lambda e: e.memset(opair[0:1, 0:64], 1.0), [("pv", P.n("pv"))], "pv")
    OP("v", lambda e: e.memset(big[0:1, 0:64], 0.0), (), "pv")
    OP("v", lambda e: e.memset(big[0:1, 64:128], 1.0), [("pv", P.n("pv"))], "pv")
    vop = P.n("pv")
    DMA("y", lambda e: e.dma_start(opair[1:2, 0:128], big[0:1, 0:128]),
        [("pv", vop)])
    OP("v", lambda e: e.memset(smal[:, 0:1], 1.5), (), "pv")
    OP("v", lambda e: e.tensor_scalar(i16d[0][:, 0:1], smal[:, 0:1], 1.0, None, ALU.mult),
       [("pv", P.n("pv"))], "pv")
    OP("v", lambda e: e.tensor_scalar(shiftp[:], i16d[0][:, 0:1], -0.5, 1.0, ALU.mult, ALU.add),
       [("pv", P.n("pv"))], "pv")
    OP("v", lambda e: e.tensor_scalar(nshiftW[:], shiftp[:], -32.0, None, ALU.mult),
       [("pv", P.n("pv"))], "pv")
    OP("v", lambda e: e.tensor_scalar(shiftW[:], shiftp[:], 33.0, None, ALU.mult),
       [("pv", P.n("pv"))], "pv")
    OP("v", lambda e: e.tensor_scalar(nshift2[:], shiftp[:], -512.0, None, ALU.mult),
       [("pv", P.n("pv"))], "pv")
    OP("v", lambda e: e.tensor_scalar(shiftp[:], shiftp[:], 513.0, None, ALU.mult),
       [("pv", P.n("pv"))], "pv")
    toks = []
    for r, src in enumerate((qs_in, qb_in, ks_in, kb_in)):
        toks.append(DMA("y", lambda e, r=r, src=src: e.dma_start(smal[r:r + 1, 64:128], src)))
    for dst, src in ((sQ, qs_in), (bQ, qb_in), (sK, ks_in), (bK, kb_in)):
        toks.append(DMA("y", lambda e, d=dst, s_=src: e.dma_start(d[0:64, 0:1], s_.transpose([1, 0]))))
        toks.append(DMA("y", lambda e, d=dst, s_=src: e.dma_start(d[64:128, 0:1], s_.transpose([1, 0]))))
    OP("v", lambda e: e.reduce_max(smal[0:4, 1:2], smal[0:4, 64:128], AX.X,
                                   apply_absolute_value=True), toks, "pv")
    OP("v", lambda e: e.tensor_copy(cst[0:4, 0:1], smal[0:4, 1:2]), [("pv", P.n("pv"))], "pv")
    mm(pU[0:1, 0:1], cst[0:4, 0:1], onesb[0:4, 0:1], True, True, [("pv", P.n("pv"))])
    OP("s", lambda e: e.activation(smal[0:1, 2:3], pU[0:1, 0:1], AF.Square, scale=0.5),
       [("pt", P.n("pt"))], "ps")
    OP("v", lambda e: e.tensor_scalar(cst[0:1, 1:2], smal[0:1, 2:3], -1.0, None, ALU.mult),
       [("ps", P.n("ps"))], "pv")
    mm(pU[:, 1:2], onesb[0:1, :], cst[0:1, 1:2], True, True, [("pv", P.n("pv"))])
    OP("v", lambda e: e.tensor_copy(mbias[:], pU[:, 1:2]), [("pt", P.n("pt"))], "pv")
    BAR()

    pwf = lambda sl: bigf(PW0 + sl * 4096, 2048)
    pwo = lambda sl: big[:, PO0 + sl * 2048: PO0 + (sl + 1) * 2048]
    g_rd = {}
    g_st = {}
    for ic in range(8):
        sl = ic % 2
        w = [("pv", g_rd[ic - 2])] if ic >= 2 else []
        tok = DMA("y", lambda e, ic=ic, sl=sl: e.dma_start(pwf(sl),
                                                           wqk[ic * 128:(ic + 1) * 128, :]), w)
        OP("v", lambda e, sl=sl: e.reduce_sum(
            smal[:, 32:64], pwf(sl).rearrange("p (b j) -> p b j", j=64), AX.X),
           [tok, ("pv", P.n("pv"))], "pv")
        OP("v", lambda e: e.tensor_scalar(smal[:, 32:64], smal[:, 32:64], 1.0 / 64, None,
                                          ALU.mult), [("pv", P.n("pv"))], "pv")
        vme = P.n("pv")
        wst = [g_st[ic - 2]] if ic >= 2 else []
        for e2 in range(2):
            for half in range(2):
                def fprm(e, sl=sl, e2=e2, half=half):
                    i_ap = pwf(sl).rearrange("p (pp e h j) -> p pp e h j",
                                             e=2, h=2, j=64)[:, :, e2, half, :]
                    o_ap = pwo(sl).rearrange("p (pp h e j) -> p pp h e j",
                                             h=2, e=2, j=64)[:, :, half, e2, :]
                    m_ap = smal[:, 32:64].rearrange("p (pp e h) -> p pp e h",
                                                    e=2, h=2)[:, :, e2, half]
                    m_ap = m_ap.unsqueeze(2).broadcast_to([128, 8, 64])
                    return e.tensor_tensor(o_ap, i_ap, m_ap, ALU.subtract)
                OP("v", fprm, [("pv", vme)] + wst, "pv")
        g_rd[ic] = P.n("pv")
        g_st[ic] = DMA("g", lambda e, ic=ic, sl=sl: e.dma_start(
            s_wqk[ic * 128:(ic + 1) * 128, :], pwo(sl)), [("pv", P.n("pv"))], queue="g")
    wqk_stoks = dict(g_st)

    w1f = lambda sl: wsB[:, sl * 1024:(sl + 1) * 1024]
    p2o = lambda sl: wsA[:, sl * 1024:(sl + 1) * 1024]
    i8ev = obufT.bitcast(I8)
    ptst = lambda sl: i8ev[:, sl * 1024:(sl + 1) * 1024]
    dstb_wg = lambda rc, cc: s_wg[rc * 128:(rc + 1) * 128, :].rearrange(
        "p (a x) -> p a x", x=128)
    dstb_wu = lambda rc, cc: s_wu[rc * 128:(rc + 1) * 128, :].rearrange(
        "p (a x) -> p a x", x=128)
    dstb_wd = lambda rc, cc: s_wd[cc * 1024:(cc + 1) * 1024, rc * 128:(rc + 1) * 128].rearrange(
        "(a p) x -> p a x", p=128)
    wmeta = []
    for name, w_hbm, dstb in (("wg", wg_in, dstb_wg), ("wu", wu_in, dstb_wu),
                              ("wd", wd_in, dstb_wd)):
        R, C = w_hbm.shape
        wmeta.append((name, w_hbm, dstb, R // 128, C // 1024))
    gslot = [0]
    cons = {}

    def slot_load(src, eng, queue, extra=()):
        gi_ = gslot[0]
        sl = gi_ % 4
        w = list(extra)
        if gi_ >= 4:
            w.append(("pv", cons[gi_ - 4]))
        tok = DMA(eng, lambda e, sl=sl, src=src: e.dma_start(w1f(sl), src), w, queue=queue)
        gslot[0] = gi_ + 1
        return gi_, sl, tok

    pcol = {"wg": 64, "wu": 32, "wd": 0}
    for (name, w_hbm, dstb, nr, ncc) in wmeta:
        base = pcol[name]
        for idx in range(nr * ncc):
            rc, cc = divmod(idx, ncc)
            gi_, sl, tok = slot_load(
                w_hbm[rc * 128:(rc + 1) * 128, cc * 1024:(cc + 1) * 1024], "s", "s")
            OP("v", lambda e, sl=sl, co=base + idx: e.reduce_sum(
                smal[:, co:co + 1], w1f(sl), AX.X, apply_absolute_value=True), [tok], "pv")
            cons[gi_] = P.n("pv")
    for (name, w_hbm, dstb, nr, ncc) in wmeta:
        base = pcol[name]
        R, C = w_hbm.shape
        OP("v", lambda e, base=base: e.reduce_sum(smal[:, 96:97], smal[:, base:base + 32],
                                                  AX.X), [("pv", P.n("pv"))], "pv")
        tokg = DMA("g", lambda e: e.dma_start(rrow[0:1, 0:128], smal[:, 96:97]),
                   [("pv", P.n("pv"))], queue="g")
        OP("v", lambda e: e.reduce_sum(smal[0:1, 97:98], rrow[0:1, 0:128], AX.X),
           [tokg], "pv")
        OP("v", lambda e, sc=1.0 / (R * C): e.tensor_scalar(
            smal[0:1, 97:98], smal[0:1, 97:98], sc, 1e-5, ALU.mult, ALU.max),
           [("pv", P.n("pv"))], "pv")
        OP("v", lambda e: e.reciprocal(smal[0:1, 98:99], smal[0:1, 97:98]),
           [("pv", P.n("pv"))], "pv")
        OP("v", lambda e: e.tensor_copy(rrow[0:1, 0:128],
                                        smal[0:1, 98:99].broadcast_to([1, 128])),
           [("pv", P.n("pv"))], "pv")
        tokr = DMA("g", lambda e, d=swq[name]: e.dma_start(d[:], rrow[0:1, 0:128]),
                   [("pv", P.n("pv"))], queue="g")
        for e_ in ALL_ENG:
            P.pending.setdefault(e_, []).append(tokr)
        if name in ("wg", "wu"):
            dstc = iswg if name == "wg" else iswu
            OP("v", lambda e: e.tensor_scalar(smal[0:1, 99:100], smal[0:1, 97:98], 1.0 / 127,
                                              None, ALU.mult), [("pv", P.n("pv"))], "pv")
        else:
            dstc = kapp
            OP("v", lambda e: e.tensor_tensor(smal[0:1, 99:100], smal[0:1, 98:99],
                                              smal[0:1, 98:99], ALU.mult),
               [("pv", P.n("pv"))], "pv")
            OP("v", lambda e: e.tensor_scalar(smal[0:1, 99:100], smal[0:1, 99:100],
                                              1e-5 * 127.0 * 127.0, None, ALU.mult),
               [("pv", P.n("pv"))], "pv")
        OP("v", lambda e: e.tensor_copy(rrow[0:1, 0:128],
                                        smal[0:1, 99:100].broadcast_to([1, 128])),
           [("pv", P.n("pv"))], "pv")
        tokc = DMA("g", lambda e, d=dstc: e.dma_start(d[:], rrow[0:1, 0:128]),
                   [("pv", P.n("pv"))], queue="g")
        for e_ in ALL_ENG:
            P.pending.setdefault(e_, []).append(tokc)
    evacv = {}
    trv = {}
    stok = {}
    p2i = [0]
    for (name, w_hbm, dstb, nr, ncc) in wmeta:
        for idx in range(nr * ncc):
            rc, cc = divmod(idx, ncc)
            gi_, sl, tok = slot_load(
                w_hbm[rc * 128:(rc + 1) * 128, cc * 1024:(cc + 1) * 1024], "y", "y")
            i2_ = p2i[0]
            p2i[0] += 1
            OP("v", lambda e, sl=sl, n_=name: e.tensor_scalar(
                i16d[0][:], w1f(sl), swq[n_][:], shiftW[:], ALU.mult, ALU.add), [tok], "pv")
            wp2 = [("pv", P.n("pv"))]
            if (i2_ - 4) in trv:
                wp2.append(("pt", trv[i2_ - 4]))
            OP("v", lambda e, sl=sl: e.tensor_scalar(
                p2o(sl), i16d[0][:], nshiftW[:], -1.0, ALU.add, ALU.max),
               wp2, "pv")
            OP("v", lambda e, sl=sl: e.tensor_scalar(
                p2o(sl), p2o(sl), 1.0, None, ALU.min), [("pv", P.n("pv"))], "pv")
            vq2 = P.n("pv")
            cons[gi_] = vq2
            poff = (i2_ % 2) * 512
            sl2 = i2_ % 2
            for jb in range(2):
                pb = pT if jb == 0 else pU
                wtr = [("pv", vq2)] if jb == 0 else []
                if jb == 0 and (i2_ - 2) in evacv:
                    wtr.append(("ps", evacv[i2_ - 2]))
                for j in range(4):
                    jj = jb * 4 + j
                    mm(pb[:, poff + j * 128: poff + (j + 1) * 128],
                       p2o(sl)[:, jj * 128:(jj + 1) * 128],
                       ident[:], True, True, wtr if j == 0 else ())
                ttx = P.n("pt")
                ws_ = [("pt", ttx)]
                if jb == 0 and sl2 in stok:
                    ws_.append(stok[sl2])
                OP("s", lambda e, jb=jb, pb=pb, sl2=sl2, poff=poff: e.activation(
                    ptst(sl2)[:, jb * 512:(jb + 1) * 512], pb[:, poff:poff + 512], AF.Copy),
                   ws_, "ps")
            evacv[i2_] = P.n("ps")
            trv[i2_] = P.n("pt")

            def fst(e, rc=rc, cc=cc, sl2=sl2, dstb=dstb):
                return e.dma_start(dstb(rc, cc),
                                   ptst(sl2).rearrange("p (a x) -> p a x", x=128))
            stok[sl2] = DMA("g", fst, [("ps", evacv[i2_])], queue="g")
    g_prep_v = P.n("pv")
    g_prep_ps = P.n("ps")
    prep_stoks = [stok[0], stok[1]]
    tap("mbias", lambda: mbias[:, :], [128, 1], F32)
    tap("shiftp", lambda: shiftp[:, :], [128, 1], F32)
    tap("iswg", lambda: iswg[:, :], [128, 1], F32)
    tap("kapp", lambda: kapp[:, :], [128, 1], F32)

    for s in range(SEQ):
        xs = x_in[s * L:(s + 1) * L, :]
        g_a1 = {}
        for tt in range(NT):
            sl = tt % 2
            if tt >= 2:
                w = [("pv", g_a1[tt - 2])]
            else:
                w = [("pv", g_prep_v)] + prep_stoks if s == 0 else []
            tok = DMA("y", lambda e, sl=sl, src=xs[tt * 128:(tt + 1) * 128, :]: e.dma_start(
                wsB[:, sl * 1024:(sl + 1) * 1024], src), w)
            OP("v", lambda e, sl=sl: e.bn_stats(bn6[:, 0:6], wsB[:, sl * 1024:sl * 1024 + 512]),
               [tok], "pv")
            OP("v", lambda e, sl=sl: e.bn_stats(bn6[:, 6:12],
                                                wsB[:, sl * 1024 + 512:(sl + 1) * 1024]),
               [("pv", P.n("pv"))], "pv")
            OP("v", lambda e: e.bn_aggr(mv2[:, 0:2], bn6[:, 0:12]), [("pv", P.n("pv"))], "pv")
            OP("v", lambda e: e.tensor_scalar(mv2[:, 2:3], mv2[:, 1:2], 1.0, EPS_LN,
                                              ALU.mult, ALU.add), [("pv", P.n("pv"))], "pv")
            rsq(mv2[:, 3:4], mv2[:, 2:3], [("pv", P.n("pv"))])
            sv = P.n("ps")
            OP("v", lambda e, sl=sl: e.tensor_scalar(lnt(sl), wsB[:, sl * 1024:(sl + 1) * 1024],
                                                     mv2[:, 0:1], mv2[:, 3:4],
                                                     ALU.subtract, ALU.mult), [("ps", sv)], "pv")
            vln = P.n("pv")
            for jb in range(2):
                pb = pT if jb == 0 else pU
                for j in range(4):
                    ic = jb * 4 + j
                    wtr0 = [("pv", vln)]
                    if s == 0 and tt == 0 and jb == 0:
                        wtr0.append(("ps", g_prep_ps))
                    mm(pb[:, j * 128:(j + 1) * 128], lnt(sl)[:, ic * 128:(ic + 1) * 128],
                       ident[:], True, True, wtr0 if ic == 0 else ())
                tv = P.n("pt")
                def fev(e, tt=tt, jb=jb, pb=pb):
                    reg = big[:, LF0 + jb * 8192: LF0 + (jb + 1) * 8192]
                    dst = reg.rearrange("p (a x) -> p a x", x=2048)[
                        :, :, tt * 128:(tt + 1) * 128]
                    return e.activation(dst, pb[:, 0:512].rearrange("p (a x) -> p a x", x=128),
                                        AF.Copy)
                OP("s", fev, [("pt", tv), ("ps", P.n("ps"))], "ps")
            g_a1[tt] = P.n("pv")
            if tt == 0:
                P.bar(("y",), COMPUTE_SEMS)
                tap("ln1f_t0_s%d" % s, lambda: big[:, LF0:LF0 + 128], [128, 128])
                tap("lnt0_s%d" % s, lambda: lnt(0), [128, 1024])
                tap("wsB0_s%d" % s, lambda: wsB[:, 0:1024], [128, 1024], F32)
                tap("mv20_s%d" % s, lambda: mv2[:, 0:4], [128, 4], F32)
                tap("bn60_s%d" % s, lambda: bn6[:, 0:12], [128, 12], F32)
        BAR()
        tap("ln1f_s%d" % s, lambda: big[:, LF0:LF0 + 16384], [128, 16384])

        wvtoks = []
        for ic in range(8):
            wvtoks.append(DMA("g", lambda e, ic=ic: e.dma_start(
                wvo(ic), wv[ic * 128:(ic + 1) * 128, :]), (), queue="g"))
        g_v = {}
        for tt in range(NT):
            pb = pS if tt % 2 == 0 else pS2
            w0 = list(wvtoks)
            if tt >= 2:
                w0.append(("ps", g_v[tt - 2]))
            for oc in range(2):
                for ic in range(8):
                    mm(pb[:, oc * 512:(oc + 1) * 512], ln1f(ic)[:, tt * 128:(tt + 1) * 128],
                       wvo(ic)[:, oc * 512:(oc + 1) * 512], ic == 0, ic == 7,
                       w0 if (oc == 0 and ic == 0) else ())
            tv = P.n("pt")
            OP("v", lambda e, tt=tt: e.memset(
                vbuf(tt).rearrange("p (h x) -> p h x", x=65)[:, :, 64:65], 1.0), (), "pv")
            vo = P.n("pv")
            for oc in range(2):
                def fvev(e, tt=tt, oc=oc, pb=pb):
                    dst = vbuf(tt).rearrange("p (h x) -> p h x", x=65)[
                        :, oc * 8:(oc + 1) * 8, 0:64]
                    return e.activation(dst, pb[:, oc * 512:(oc + 1) * 512].rearrange(
                        "p (h x) -> p h x", x=64), AF.Copy, scale=1.0 / 32)
                OP("s", fvev, [("pt", tv), ("pv", vo)], "ps")
            g_v[tt] = P.n("ps")
        BAR()
        tap("vbuf_s%d" % s, lambda: big[:, VB0:VB0 + 16640], [128, 16640])

        g_omove = []
        g_lmove = []
        ltoks = []
        g_attn_done = {}
        g_qk_ready = {}
        g_exp_ps = {(0, 0): 0, (0, 1): 0, (1, 0): 0, (1, 1): 0}
        g_pbuf_pt = {(0, 0): 0, (0, 1): 0, (1, 0): 0, (1, 1): 0}
        g_oacc = {0: (0, 0), 1: (0, 0)}
        g_projslot = [0]
        g_proj_state = {}

        def proj_partA(p, rnd):
            par = p % 2
            ttg = rnd // 2
            hf = rnd % 2
            if rnd == 0:
                toks = []
                for ic in range(8):
                    wld = [("pt", g_attn_done.get(p - 2, 0))] if ic == 0 else []
                    if p == 0:
                        wld.append(wqk_stoks[ic])
                    toks.append(DMA("y", lambda e, p=p, ic=ic, par=par: e.dma_start(
                        wqk2[:, par * 2048 + ic * 256: par * 2048 + (ic + 1) * 256],
                        s_wqk[ic * 128:(ic + 1) * 128, p * 256:(p + 1) * 256]), wld))
                g_qk_ready[(p, "wtok")] = toks
            toks = g_qk_ready[(p, "wtok")]
            slots = (pS[:, 0:256], pS[:, 512:768])
            w0 = list(toks) + [("pv", g_projslot[0])]
            if rnd == 0 and p >= 2:
                w0.append(("pt", g_attn_done.get(p - 2, 0)))
            for ic in range(8):
                for tj in range(2):
                    tt = ttg * 4 + hf * 2 + tj
                    mm(slots[tj], ln1f(ic)[:, tt * 128:(tt + 1) * 128],
                       wqk2[:, par * 2048 + ic * 256: par * 2048 + (ic + 1) * 256],
                       ic == 0, ic == 7, w0 if (ic == 0 and tj == 0) else ())
            tprj = P.n("pt")
            for tj in range(2):
                OP("s", lambda e, tj=tj, slots=slots: e.activation(
                    junkA[:, tj * 256:(tj + 1) * 256], slots[tj], AF.Square),
                   [("pt", tprj), ("ps", P.n("ps"))], "ps")
            g_proj_state[(p, rnd, "sq")] = P.n("ps")

        def proj_partA2(p, rnd):
            slots = (pS[:, 0:256], pS[:, 512:768])
            ssq = g_proj_state[(p, rnd, "sq")]
            OP("v", lambda e: e.reduce_sum(statq[:, 0:8],
                                           junkA[:, 0:512].rearrange("p (b j) -> p b j", j=64),
                                           AX.X), [("ps", ssq)], "pv")
            OP("v", lambda e: e.tensor_scalar(statq[:, 8:16], statq[:, 0:8], 1.0 / 64,
                                              EPS_QK, ALU.mult, ALU.add),
               [("pv", P.n("pv"))], "pv")
            rsq(statq[:, 16:24], statq[:, 8:16], [("pv", P.n("pv"))])
            srr = P.n("ps")
            zsl = rnd % 2
            for tj in range(2):
                OP("v", lambda e, tj=tj, zsl=zsl, slots=slots: e.tensor_tensor(
                    lnt(zsl)[:, tj * 256:(tj + 1) * 256].rearrange("p (b j) -> p b j", j=64),
                    slots[tj].rearrange("p (b j) -> p b j", j=64),
                    statq[:, 16 + tj * 4:16 + (tj + 1) * 4].unsqueeze(2).broadcast_to(
                        [128, 4, 64]), ALU.mult), [("ps", srr)], "pv")
            g_proj_state[(p, rnd)] = P.n("pv")

        def proj_partB(p, rnd):
            par = p % 2
            ttg = rnd // 2
            hf = rnd % 2
            vz1 = g_proj_state[(p, rnd)]
            zsl = rnd % 2
            for cls in range(2):
                for tj in range(2):
                    mm(pS[:, cls * 512 + tj * 128: cls * 512 + (tj + 1) * 128],
                       lnt(zsl)[:, (2 * tj + cls) * 128:(2 * tj + cls + 1) * 128],
                       ident[:], True, True,
                       [("pv", vz1)] if (cls == 0 and tj == 0) else ())
            ttr = P.n("pt")
            for cls in range(2):
                buf = qbp if cls == 0 else kbp
                sc = sQ if cls == 0 else sK
                bi = bQ if cls == 0 else bK
                OP("v", lambda e, buf=buf, sc=sc, bi=bi, par=par, ttg=ttg, hf=hf, cls=cls:
                   e.tensor_scalar(buf[:, par * 2048 + ttg * 512 + hf * 256:
                                       par * 2048 + ttg * 512 + (hf + 1) * 256],
                                   pS[:, cls * 512:cls * 512 + 256], sc[:], bi[:],
                                   ALU.mult, ALU.add), [("pt", ttr)], "pv")
            g_projslot[0] = P.n("pv")
            if rnd == 7:
                g_qk_ready[p] = P.n("pv")

        def emit_attn_pairgroup(p, g, hooks, cb0):
            par = p % 2
            nchunk = 4 * (g + 1)
            bufs = (pT, pS2)

            def emit_pv(ci):
                par2 = ci % 2
                for e2 in range(2):
                    h = 2 * p + e2
                    pOb = pU[:, e2 * 512:(e2 + 1) * 512]
                    wPV = [("ps", g_exp_ps[(e2, par2)])]
                    if ci == 0:
                        wPV.append(("pv", g_oacc[e2][0]))
                        wPV.append(("ps", g_oacc[e2][1]))
                    mm(pOb[0:65, :], vbuf(ci)[:, h * 65:(h + 1) * 65],
                       pbuf(e2)[:, par2 * 512:(par2 + 1) * 512],
                       ci == 0, ci == nchunk - 1, wPV)
                    g_pbuf_pt[(e2, par2)] = P.n("pt")

            for ci in range(nchunk):
                par2 = ci % 2
                off = 512 * g - 128 * ci
                diag = -384 <= off <= 0
                for e2 in range(2):
                    base = e2 * 64
                    tpos = (64, 0) if e2 == 1 else None
                    pb = bufs[e2]
                    wS = [("ps", g_exp_ps[(e2, par2)])]
                    if ci == 0 and e2 == 0:
                        wS.append(("pv", g_qk_ready[p]))
                    mm(pb[:, par2 * 512:(par2 + 1) * 512],
                       kbp[base:base + 64, par * 2048 + ci * 128:
                           par * 2048 + (ci + 1) * 128],
                       qbp[base:base + 64, par * 2048 + g * 512:
                           par * 2048 + (g + 1) * 512],
                       True, not diag, wS, tp=tpos)
                    if diag:
                        jm = (-off) // 128
                        mm(pb[:, par2 * 512:(par2 + 1) * 512], ltri[:],
                           masks[:, 512 - 128 * jm: 1024 - 128 * jm], False, True, ())
                    texp = P.n("pt")
                    OP("s", lambda e, pb=pb, e2=e2, par2=par2: e.activation(
                        pbuf(e2)[:, par2 * 512:(par2 + 1) * 512],
                        pb[:, par2 * 512:(par2 + 1) * 512],
                        AF.Exp, bias=mbias[:], scale=1.0 / 64),
                       [("pt", texp), ("pt", g_pbuf_pt[(e2, par2)])], "ps")
                    g_exp_ps[(e2, par2)] = P.n("ps")
                if ci > 0:
                    emit_pv(ci - 1)
                if ci % 2 == 1:
                    for fn in hooks.get(cb0 + ci // 2, ()):
                        fn()
            emit_pv(nchunk - 1)
            tPV = P.n("pt")
            for e2 in range(2):
                h = 2 * p + e2
                pOb = pU[:, e2 * 512:(e2 + 1) * 512]
                r = h * 4 + g
                if e2 == 0:
                    OP("v", lambda e, s=s, p=p, g=g, pOb=pOb: e.tensor_copy(
                        obuf(s, p)[0:64, g * 512:(g + 1) * 512], pOb[0:64, 0:512]),
                       [("pt", tPV)], "pv")
                else:
                    OP("v", lambda e, pOb=pOb: e.tensor_copy(ostag[0:64, :], pOb[0:64, 0:512]),
                       [("pt", tPV)] + g_omove, "pv")
                    g_omove[:] = [DMA("g", lambda e, s=s, p=p, g=g: e.dma_start(
                        obuf(s, p)[64:128, g * 512:(g + 1) * 512], ostag[0:64, :]),
                        [("pv", P.n("pv"))], queue="g")]
                OP("s", lambda e, pOb=pOb: e.copy(lstag[64:65, :], pOb[64:65, 0:512]),
                   [("pt", tPV)] + g_lmove, "ps")
                g_lmove[:] = [DMA("g", lambda e, r=r: e.dma_start(
                    l_all[r:r + 1, :], lstag[64:65, :]),
                    [("ps", P.n("ps"))], queue="g")]
                ltoks.extend(g_lmove)
                g_oacc[e2] = (P.n("pv"), P.n("ps"))
            g_attn_done[p] = P.n("pt")

        for rnd in range(8):
            proj_partA(0, rnd)
            proj_partA2(0, rnd)
            proj_partB(0, rnd)
        A_SLOTS = (0, 3, 5, 8, 10, 12, 15, 17)
        for p in range(NP):
            hooks = {}
            if p + 1 < NP:
                for rnd in range(8):
                    hooks.setdefault(A_SLOTS[rnd], []).append(
                        lambda rnd=rnd: proj_partA(p + 1, rnd))
                    hooks.setdefault(A_SLOTS[rnd] + 1, []).append(
                        lambda rnd=rnd: proj_partA2(p + 1, rnd))
                    hooks.setdefault(A_SLOTS[rnd] + 2, []).append(
                        lambda rnd=rnd: proj_partB(p + 1, rnd))
            cb0 = 0
            for g in range(NG):
                emit_attn_pairgroup(p, g, hooks, cb0)
                cb0 += 2 * (g + 1)
        BAR()
        OP("s", lambda e: e.activation(l_all[0:64, :], l_all[0:64, :], AF.Ln), ltoks, "ps")
        OP("s", lambda e: e.activation(rlb[0:64, :], l_all[0:64, :], AF.Exp, scale=-1.0),
           [("ps", P.n("ps"))], "ps")
        srl = P.n("ps")
        rlslot = (rl2[0:2, 0:512], iotb[0:2, 0:512], iotb[0:2, 512:1024])
        rl_mm = {}
        rl_mult = {}
        for idx in range(NP * NG):
            p, g = divmod(idx, NG)
            r0 = 8 * p + g
            sl3 = rlslot[idx % 3]
            wga = [("ps", srl)]
            if idx - 3 in rl_mm:
                wga.append(("pt", rl_mm[idx - 3]))
            tokA = DMA("g", lambda e, r0=r0, sl3=sl3: e.dma_start(sl3[0:1, :],
                                                                 rlb[r0:r0 + 1, :]),
                       wga, queue="g")
            tokB = DMA("g", lambda e, r0=r0, sl3=sl3: e.dma_start(sl3[1:2, :],
                                                                  rlb[r0 + 4:r0 + 5, :]),
                       (), queue="g")
            poff = (idx % 2) * 512
            wmm_ = [tokA, tokB]
            if idx - 2 in rl_mult:
                wmm_.append(("pv", rl_mult[idx - 2]))
            mm(pS[:, poff:poff + 512], opair[0:2, :], sl3[0:2, :], True, True, wmm_)
            rl_mm[idx] = P.n("pt")
            OP("v", lambda e, s=s, p=p, g=g, poff=poff: e.tensor_tensor(
                obuf(s, p)[:, g * 512:(g + 1) * 512], obuf(s, p)[:, g * 512:(g + 1) * 512],
                pS[:, poff:poff + 512], ALU.mult), [("pt", rl_mm[idx])], "pv")
            rl_mult[idx] = P.n("pv")
        BAR()
        tap("lall_s%d" % s, lambda: l_all[0:64, :], [64, 512], F32)
        tap("onorm_s%d" % s, lambda s=s: obuf(s, 0), [128, 2048])

        wotoks = []
        g_x1st = {0: [], 1: []}
        for ic in range(8):
            wotoks.append(DMA("g", lambda e, ic=ic: e.dma_start(
                wvo(ic), wout[ic * 128:(ic + 1) * 128, :]), (), queue="g"))
        g_a4 = {}
        for tt in range(NT):
            slx = tt % 2
            wx = [("pv", g_a4[tt - 2][1])] if tt >= 2 else []
            dyx = DMA("y", lambda e, slx=slx, src=xs[tt * 128:(tt + 1) * 128, :]: e.dma_start(
                wsB[:, slx * 1024:(slx + 1) * 1024], src), wx)
            pb = pS if tt % 2 == 0 else pS2
            w0 = list(wotoks)
            if tt >= 2:
                w0.append(("pv", g_a4[tt - 2][0]))
            for oc in range(2):
                for ic in range(8):
                    mm(pb[:, oc * 512:(oc + 1) * 512],
                       obuf(s, ic)[:, tt * 128:(tt + 1) * 128],
                       wvo(ic)[:, oc * 512:(oc + 1) * 512], ic == 0, ic == 7,
                       w0 if (oc == 0 and ic == 0) else ())
            ta4 = P.n("pt")
            OP("v", lambda e, pb=pb: e.bn_stats(bn6[:, 0:6], pb[:, 0:512]), [("pt", ta4)], "pv")
            OP("v", lambda e, pb=pb: e.bn_stats(bn6[:, 6:12], pb[:, 512:1024]),
               [("pv", P.n("pv"))], "pv")
            OP("v", lambda e: e.bn_aggr(mv2[:, 0:2], bn6[:, 0:12]), [("pv", P.n("pv"))], "pv")
            OP("v", lambda e: e.tensor_scalar(mv2[:, 2:3], mv2[:, 1:2], 1.0, EPS_OP,
                                              ALU.mult, ALU.add), [("pv", P.n("pv"))], "pv")
            rsq(mv2[:, 3:4], mv2[:, 2:3], [("pv", P.n("pv"))])
            sr4 = P.n("ps")
            OP("v", lambda e, pb=pb: e.tensor_scalar(jF32[:], pb[:], mv2[:, 0:1], mv2[:, 3:4],
                                                     ALU.subtract, ALU.mult), [("ps", sr4)], "pv")
            vstt = P.n("pv")
            slo = tt % 2
            OP("v", lambda e, slo=slo, slx=slx: e.tensor_tensor(
                outst(slo), wsB[:, slx * 1024:(slx + 1) * 1024], jF32[:], ALU.add),
               [("pv", vstt), dyx] + g_x1st[slo], "pv")
            g_x1st[slo] = [DMA("g", lambda e, dst=s_x1[(s * 16 + tt) * 128:
                                                       (s * 16 + tt + 1) * 128, :],
                               slo=slo: e.dma_start(dst, outst(slo)),
                               [("pv", P.n("pv"))], queue="g")]
            g_a4[tt] = (P.n("pv"), P.n("pv"))
        BAR()


    g_ost = {0: [], 1: []}
    g_strep_w = [("pv", P.n("pv")), ("pt", P.n("pt")), ("ps", P.n("ps"))]
    g_xw_free = list(g_strep_w)
    g_xqf_t = 0
    g_f1pg = [0]
    g_cin_t = 0
    f0_state = {}

    def emit_F0_dve(b, j, st, pw=()):
        sl = j % 2
        if True:
            w0 = list(st.get(("ld", j - 2), [])) + list(pw)
            x1tok = DMA(
                "y", lambda e, sl=sl, src=s_x1[(b * 4 + j) * 128:(b * 4 + j + 1) * 128, :]:
                e.dma_start(xw(sl), src), w0)
            OP("v", lambda e, sl=sl: e.bn_stats(bn6[:, 0:6], xw(sl)[:, 0:512]),
               [x1tok] + ([("pt", st[("pe", j - 2)])] if ("pe", j - 2) in st else []), "pv")
            OP("v", lambda e, sl=sl: e.bn_stats(bn6[:, 6:12], xw(sl)[:, 512:1024]),
               [("pv", P.n("pv"))], "pv")
            OP("v", lambda e: e.bn_aggr(mv2[:, 0:2], bn6[:, 0:12]), [("pv", P.n("pv"))], "pv")
            OP("v", lambda e: e.tensor_scalar(mv2[:, 2:3], mv2[:, 1:2], 1.0, EPS_LN,
                                              ALU.mult, ALU.add), [("pv", P.n("pv"))], "pv")
            rsq(mv2[:, 3:4], mv2[:, 2:3], [("pv", P.n("pv"))])
            s0 = P.n("ps")
            OP("v", lambda e, sl=sl: e.tensor_scalar(
                ln2f[:, sl * 1024:(sl + 1) * 1024], xw(sl),
                mv2[:, 0:1], mv2[:, 3:4], ALU.subtract, ALU.mult),
               [("ps", s0), ("pg", g_f1pg[0])], "pv")
            OP("v", lambda e, sl=sl: e.reduce_max(mv2[:, 4:5], ln2f[:, sl * 1024:(sl + 1) * 1024],
                                                  AX.X, apply_absolute_value=True),
               [("pv", P.n("pv"))], "pv")
            OP("v", lambda e: e.tensor_scalar(mv2[:, 5:6], mv2[:, 4:5], 1e-5, None, ALU.max),
               [("pv", P.n("pv"))], "pv")
            OP("v", lambda e: e.reciprocal(mv2[:, 6:7], mv2[:, 5:6]), [("pv", P.n("pv"))], "pv")
            OP("v", lambda e: e.tensor_scalar(mv2[:, 7:8], mv2[:, 6:7], 127.0, None, ALU.mult),
               [("pv", P.n("pv"))], "pv")
            vq0 = P.n("pv")
            OP("v", lambda e, sl=sl: e.tensor_scalar(i16d[sl][:],
                                                     ln2f[:, sl * 1024:(sl + 1) * 1024],
                                                     mv2[:, 7:8], shiftp[:],
                                                     ALU.mult, ALU.add), [("pv", vq0)], "pv")
            OP("v", lambda e, sl=sl: e.tensor_scalar(xqtm(sl), i16d[sl][:], nshift2[:], None,
                                                     ALU.add), [("pv", P.n("pv"))], "pv")
            wcin = [("pv", vq0)]
            if j == 0:
                wcin.append(("pt", g_cin_t))
            OP("v", lambda e, j=j: e.memset(cin(j)[:], 0.0), wcin, "pv")
            OP("v", lambda e, j=j: e.tensor_tensor(cin(j)[:, 0:1], mv2[:, 5:6], iswg[:],
                                                   ALU.mult), [("pv", P.n("pv"))], "pv")
            OP("v", lambda e, j=j: e.tensor_tensor(cin(j)[:, 32:33], mv2[:, 5:6], iswu[:],
                                                   ALU.mult), [("pv", P.n("pv"))], "pv")
            st[("vc", j)] = P.n("pv")
            st[("ld", j)] = [("pv", P.n("pv"))]

    def emit_F0_pe(b, j, st):
        sl = j % 2
        for jb in range(2):
            pb = pT if jb == 0 else pU
            wtr = [("pv", st[("vc", j)])]
            if jb == 0 and ("ev", j - 1) in st:
                wtr.append(("ps", st[("ev", j - 1)]))
            if j == 0 and jb == 0:
                wtr.append(("pt", g_xqf_t))
            for jj in range(4):
                hc = jb * 4 + jj
                mm(pb[:, jj * 128:(jj + 1) * 128], xqtm(sl)[:, hc * 128:(hc + 1) * 128],
                   ident[:], True, True, wtr if jj == 0 else ())
            ttx = P.n("pt")
            def fex(e, j=j, jb=jb, pb=pb):
                reg = big[:, XQ0 + jb * 2048: XQ0 + (jb + 1) * 2048]
                dst = reg.rearrange("p (a x) -> p a x", x=512)[
                    :, :, j * 128:(j + 1) * 128]
                return e.activation(dst, pb[:, 0:512].rearrange("p (a x) -> p a x", x=128),
                                    AF.Copy)
            OP("s", fex, [("pt", ttx), ("ps", P.n("ps"))], "ps")
        st[("pe", j)] = P.n("pt")
        st[("ev", j)] = P.n("ps")

    xw = lambda sl: wsB[:, 2048 + sl * 1024:2048 + (sl + 1) * 1024]

    obufT_f = obufT.bitcast(F32)
    qkbuf_f = qkbuf2.bitcast(F32)
    wqk2_f = wqk2.bitcast(F32)

    def tdst(c):
        if c < 16:
            return obufT_f[:, c * 512:(c + 1) * 512]
        if c < 24:
            return qkbuf_f[:, (c - 16) * 512:(c - 15) * 512]
        if c < 28:
            return wqk2_f[:, (c - 24) * 512:(c - 23) * 512]
        return wsB[:, (c - 28) * 512:(c - 27) * 512]

    lnd8 = big[:, QB0:QB0 + 2048].bitcast(I8)
    wd8 = big[:, R10:R10 + 1024].bitcast(I8)

    st_cur = {}
    for j in range(4):
        emit_F0_dve(0, j, st_cur,
                    pw=list(g_xw_free) + prep_stoks if j == 0 else ())
        emit_F0_pe(0, j, st_cur)
    f0_ready = (st_cur[("ev", 3)], st_cur[("vc", 3)])
    g_prev_f2 = {}
    crep_v = [0]

    def emit_crep(frps, frpv):
        nonlocal g_cin_t
        for j in range(4):
            mm(pT[:, j * 128:(j + 1) * 128], cin(j)[:], ident[:], True, True,
               [("ps", frps), ("pv", frpv)] if j == 0 else ())
        tci = P.n("pt")
        OP("v", lambda e: e.tensor_copy(big[:, CI0 + 512:CI0 + 1024], pT[:, 0:512]),
           [("pt", tci)], "pv")
        vci = P.n("pv")
        for cls in range(2):
            basep = cls * 32
            for j in range(4):
                mm(pU[:, j * 128:(j + 1) * 128], onesb[basep:basep + 1, :],
                   big[basep:basep + 1, CI0 + 512 + j * 128: CI0 + 512 + (j + 1) * 128],
                   True, True, [("pv", vci)] if j == 0 else (),
                   tp=(basep, 0) if basep else None)
            tcr = P.n("pt")
            OP("v", lambda e, cls=cls: e.tensor_copy(crepf[:, cls * 512:(cls + 1) * 512],
                                                     pU[:, 0:512]), [("pt", tcr)], "pv")
            vci = P.n("pv")
        g_cin_t = P.n("pt")
        crep_v[0] = P.n("pv")

    for b in range(NB):
        if b == 0:
            emit_crep(f0_ready[0], f0_ready[1])

        OP("v", lambda e: e.memset(ln2f[:, 0:512], 0.0),
           [("pv", crep_v[0]), ("pg", g_f1pg[0])], "pv")
        OP("v", lambda e: e.memset(ln2f[:, 512:1024], 0.0), [("pv", P.n("pv"))], "pv")
        vM1 = P.n("pv")
        g_f1 = {}
        g_f1c = {}

        def f1_wsa_guard(c):
            if c >= 4:
                return [("pt", g_f1[c - 4][0])]
            if g_prev_f2:
                return [("pt", g_prev_f2[24 + 2 * c]), ("pt", g_prev_f2[25 + 2 * c])]
            return [("pt", P.n("pt"))]

        def f1_load(c):
            sl8 = c % 2
            wl = [("ps", g_f1c[c - 2])] if c >= 2 else [("ps", P.n("ps"))]
            return (DMA("y", lambda e, c=c, sl8=sl8: e.dma_start(
                        lnd8[:, sl8 * 2048: sl8 * 2048 + 1024],
                        s_wg[c * 128:(c + 1) * 128, :]), wl),
                    DMA("y", lambda e, c=c, sl8=sl8: e.dma_start(
                        lnd8[:, sl8 * 2048 + 1024: sl8 * 2048 + 2048],
                        s_wu[c * 128:(c + 1) * 128, :]), ()))

        def f1_cast(c, toks):
            sl8 = c % 2
            slw = c % 4
            OP("s", lambda e, slw=slw, sl8=sl8: e.activation(
                wsA[:, slw * 2048: slw * 2048 + 1024],
                lnd8[:, sl8 * 2048: sl8 * 2048 + 1024], AF.Copy),
               [toks[0]] + f1_wsa_guard(c), "ps")
            OP("s", lambda e, slw=slw, sl8=sl8: e.activation(
                wsA[:, slw * 2048 + 1024: slw * 2048 + 2048],
                lnd8[:, sl8 * 2048 + 1024: sl8 * 2048 + 2048], AF.Copy),
               [toks[1]], "ps")
            g_f1c[c] = P.n("ps")

        ldq = {0: f1_load(0), 1: f1_load(1)}
        f1_cast(0, ldq.pop(0))
        for c in range(32):
            slw = c % 4
            if c + 2 < 32:
                ldq[c + 2] = f1_load(c + 2)
            if c + 1 < 32:
                f1_cast(c + 1, ldq.pop(c + 1))
            pb = pS if c % 2 == 0 else pS2
            w0 = [("ps", g_f1c[c])]
            if c >= 2:
                w0.append(("pv", g_f1[c - 2][1]))
            else:
                w0.append(("pv", vM1))
            for wsel in range(2):
                for hc in range(8):
                    mm(pb[:, wsel * 512:(wsel + 1) * 512],
                       wsA[:, slw * 2048 + wsel * 1024 + hc * 128:
                           slw * 2048 + wsel * 1024 + (hc + 1) * 128],
                       xqf(hc), hc == 0, hc == 7, w0 if (wsel == 0 and hc == 0) else ())
            tf1 = P.n("pt")
            wcm = [("pt", tf1)]
            if (c - 1, "vt") in g_f1:
                wcm.append(("pv", g_f1[(c - 1, "vt")]))
            OP("v", lambda e, pb=pb: e.tensor_tensor(jF32[:, 0:512], pb[:, 0:512],
                                                     crepf[:, 0:512], ALU.mult),
               wcm, "pv")
            vg1 = P.n("pv")
            OP("s", lambda e: e.activation(jF32[:, 0:512], jF32[:, 0:512], AF.Silu),
               [("pv", vg1)], "ps")
            sg2 = P.n("ps")
            OP("v", lambda e, pb=pb: e.tensor_tensor(jF32[:, 512:1024], pb[:, 512:1024],
                                                     crepf[:, 512:1024], ALU.mult),
               [("pt", tf1)], "pv")
            wtw = [("ps", sg2), ("pv", P.n("pv"))]
            if c < 4:
                wtw += list(g_strep_w)
            OP("v", lambda e, c=c: e.tensor_tensor(tdst(c), jF32[:, 0:512],
                                                   jF32[:, 512:1024], ALU.mult),
               wtw, "pv")
            vt = P.n("pv")
            sqb = lambda c: ln2f[:, 1024 + (c % 2) * 512: 1536 + (c % 2) * 512]
            wsq = [("pv", vt)]
            if (c - 2, "vmax") in g_f1:
                wsq.append(("pv", g_f1[(c - 2, "vmax")]))
            OP("s", lambda e, c=c: e.activation(sqb(c), tdst(c), AF.Square),
               wsq, "ps")
            OP("v", lambda e, c=c: e.tensor_tensor(ln2f[:, 0:512], ln2f[:, 0:512],
                                                   sqb(c), ALU.max),
               [("ps", P.n("ps"))], "pv")
            g_f1[(c, "vmax")] = P.n("pv")
            g_f1[(c, "vt")] = P.n("pv")
            g_f1[c] = (P.n("pt"), P.n("pv"))
        g_xqf_t = P.n("pt")
        g_f1pg[0] = P.n("pg")
        f1v = P.n("pv")
        OP("v", lambda e: e.tensor_copy(junkA[:, 0:512], ln2f[:, 0:512]),
           [("pv", f1v)], "pv")
        vMb = P.n("pv")
        for j in range(4):
            mm(pT[:, j * 128:(j + 1) * 128], junkA[:, j * 128:j * 128 + 128],
               ident[:], True, True, [("pv", vMb)] if j == 0 else ())
        tM = P.n("pt")
        OP("v", lambda e: e.reduce_max(smal[:, 100:104], pT[:, 0:512].rearrange(
            "p (a x) -> p a x", x=128), AX.X), [("pt", tM)], "pv")
        OP("v", lambda e: e.tensor_scalar(smal[:, 104:108], smal[:, 100:104], 1e-10, None,
                                          ALU.max), [("pv", P.n("pv"))], "pv")
        rsq(smal[:, 108:112], smal[:, 104:108], [("pv", P.n("pv"))])
        sMr = P.n("ps")
        OP("v", lambda e: e.tensor_scalar(smal[:, 112:116], smal[:, 108:112], 127.0, None,
                                          ALU.mult), [("ps", sMr)], "pv")
        v127 = P.n("pv")
        sttoks = []
        for j in range(4):
            sttoks.append(DMA("g", lambda e, j=j: e.dma_start(
                rrow[0:1, j * 128:(j + 1) * 128], smal[:, 112 + j:113 + j]),
                [("pv", v127)] + (list(g_strep_w) if j == 0 else []), queue="g"))
        OP("v", lambda e: e.reciprocal(smal[:, 116:120], smal[:, 104:108]), (), "pv")
        OP("v", lambda e: e.tensor_tensor(smal[:, 116:120], smal[:, 116:120],
                                          kapp[:].broadcast_to([128, 4]), ALU.mult),
           [("pv", P.n("pv"))], "pv")
        vst0 = P.n("pv")
        OP("v", lambda e: e.tensor_copy(junkA[0:1, 0:512], rrow[0:1, 0:512]), sttoks, "pv")
        OP("v", lambda e: e.tensor_sub(junkA[0:1, 512:1024], rrow[0:1, 0:512],
                                       junkA[0:1, 0:512]), [("pv", P.n("pv"))], "pv")
        vhl = P.n("pv")
        for j in range(4):
            mm(pU[:, j * 128:(j + 1) * 128], onesb[0:1, :],
               junkA[0:1, j * 128:(j + 1) * 128], True, False,
               [("pv", vhl)] if j == 0 else ())
            mm(pU[:, j * 128:(j + 1) * 128], onesb[0:1, :],
               junkA[0:1, 512 + j * 128:512 + (j + 1) * 128], False, True, ())
        tstr = P.n("pt")
        OP("v", lambda e: e.tensor_copy(crepf[:, 1024:1536], pU[:, 0:512]), [("pt", tstr)], "pv")
        v_strep = P.n("pv")

        g_q = {}
        g_jf = {}
        g_half_done = {}
        g_wd2 = {}
        have_next = b + 1 < NB
        st_new = {}

        def emit_epi(half, tdn, st_new, have_next):
            epi_v = []
            rel = 0
            for j in range(2):
                tt = half * 2 + j
                pb = pS if j == 0 else pS2
                sl = j % 2
                scr = wsB[:, sl * 1024:(sl + 1) * 1024]
                wx2 = [("pt", tdn)]
                if have_next:
                    wx2 += list(st_new[("ld", half * 2 + sl)])
                x1tok2 = DMA("y", lambda e, sl=sl, src=s_x1[(b * 4 + tt) * 128:
                                                            (b * 4 + tt + 1) * 128, :]:
                             e.dma_start(xw(sl), src), wx2)
                OP("v", lambda e, pb=pb, scr=scr: e.tensor_copy(scr, pb[:]),
                   [("pt", tdn)], "pv")
                rel = P.n("pv")
                OP("v", lambda e, scr=scr: e.bn_stats(bn6[:, 0:6], scr[:, 0:512]), (), "pv")
                OP("v", lambda e, scr=scr: e.bn_stats(bn6[:, 6:12], scr[:, 512:1024]),
                   [("pv", P.n("pv"))], "pv")
                OP("v", lambda e: e.bn_aggr(mv2[:, 0:2], bn6[:, 0:12]), [("pv", P.n("pv"))], "pv")
                OP("v", lambda e, tt=tt: e.tensor_tensor(mv2[:, 2:3], mv2[:, 1:2],
                                                         smal[:, 116 + tt:117 + tt], ALU.add),
                   [("pv", P.n("pv"))], "pv")
                rsq(mv2[:, 3:4], mv2[:, 2:3], [("pv", P.n("pv"))])
                srj = P.n("ps")
                OP("v", lambda e, scr=scr: e.tensor_scalar(jF32[:], scr, mv2[:, 0:1],
                                                           mv2[:, 3:4], ALU.subtract, ALU.mult),
                   [("ps", srj)], "pv")
                slo = tt % 2
                OP("v", lambda e, slo=slo, sl=sl: e.tensor_tensor(
                    outst(slo), xw(sl), jF32[:], ALU.add),
                   [("pv", P.n("pv")), x1tok2] + g_ost[slo], "pv")
                vfin = P.n("pv")
                g_ost[slo] = [DMA("g", lambda e, b=b, tt=tt, slo=slo: e.dma_start(
                    out_hbm[(b * 4 + tt) * 128:(b * 4 + tt + 1) * 128, :], outst(slo)),
                    [("pv", vfin)], queue="pg" if False else "g")]
                epi_v.append(("pv", P.n("pv")))
            return epi_v, rel

        g_wdc = {}

        def wd_load(k):
            sl8 = k % 2
            i2 = k % 32
            wl = [("ps", g_wdc[k - 2])] if k >= 2 else [("ps", P.n("ps"))]
            return DMA("y", lambda e, i2=i2, sl8=sl8: e.dma_start(
                wd8[:, sl8 * 1024:(sl8 + 1) * 1024],
                s_wd[i2 * 128:(i2 + 1) * 128, :]), wl)

        def wd_cast(k, tok, guard):
            sl8 = k % 2
            slw = k % 8
            OP("s", lambda e, slw=slw, sl8=sl8: e.activation(
                wsA[:, slw * 1024:(slw + 1) * 1024],
                wd8[:, sl8 * 1024:(sl8 + 1) * 1024], AF.Copy),
               [tok] + guard, "ps")
            g_wdc[k] = P.n("ps")

        for i2 in range(32):
            slw = i2 % 8
            sl2 = i2 % 2
            if i2 >= 8:
                w = [("pt", g_wd2[i2 - 8])]
            else:
                w = [("pt", g_f1[28 + i2 // 2][0])]
            wd_cast(i2, wd_load(i2), w)
            jh = (i2 % 2) * 512
            OP("v", lambda e, i2=i2, jh=jh: e.tensor_tensor(jF32[:, jh:jh + 512], tdst(i2),
                                                            crepf[:, 1024:1536], ALU.mult),
               [("pv", v_strep)], "pv")
            OP("v", lambda e, sl2=sl2, jh=jh: e.tensor_scalar(i16d[sl2][:, 0:512],
                                                              jF32[:, jh:jh + 512], 1.0,
                                                              shiftp[:], ALU.mult, ALU.add),
               [("pv", P.n("pv"))], "pv")
            g_jf[i2] = P.n("pv")
            OP("v", lambda e, i2=i2, sl2=sl2: e.tensor_scalar(
                tblk(i2), i16d[sl2][:, 0:512], nshift2[:], None, ALU.add),
               [("pv", P.n("pv"))], "pv")
            g_q[i2] = [("pv", P.n("pv"))]
            wmm = [("ps", g_wdc[i2])] + g_q[i2]
            for j in range(2):
                for oc in range(2):
                    mm((pS if j == 0 else pS2)[:, oc * 512:(oc + 1) * 512],
                       tblk(i2)[:, j * 128:(j + 1) * 128],
                       wsA[:, slw * 1024 + oc * 512: slw * 1024 + (oc + 1) * 512],
                       i2 == 0, i2 == 31, wmm if (j == 0 and oc == 0) else ())
            g_wd2[i2] = P.n("pt")
            if have_next and i2 == 12:
                emit_F0_dve(b + 1, 0, st_new, pw=list(g_xw_free))
            elif have_next and i2 == 20:
                emit_F0_dve(b + 1, 1, st_new)
        tdn = P.n("pt")
        g_wd2_h0 = dict(g_wd2)
        wd_cast(32, wd_load(32), [("pt", g_wd2_h0[24])])
        wd_cast(33, wd_load(33), [("pt", g_wd2_h0[25])])
        g_half_done[0], rel0 = emit_epi(0, tdn, st_new, have_next)

        for i2 in range(32):
            slw = i2 % 8
            k = 32 + i2
            if k not in g_wdc:
                if i2 >= 8:
                    w = [("pt", g_wd2[i2 - 8])]
                else:
                    w = [("pt", g_wd2_h0[24 + i2])]
                wd_cast(k, wd_load(k), w)
            wmm = [("ps", g_wdc[k])]
            if i2 == 0:
                wmm.append(("pv", rel0))
            for j in range(2):
                tt = 2 + j
                for oc in range(2):
                    mm((pS if j == 0 else pS2)[:, oc * 512:(oc + 1) * 512],
                       tblk(i2)[:, tt * 128:(tt + 1) * 128],
                       wsA[:, slw * 1024 + oc * 512: slw * 1024 + (oc + 1) * 512],
                       i2 == 0, i2 == 31, wmm if (j == 0 and oc == 0) else ())
            g_wd2[i2] = P.n("pt")
            if have_next:
                if i2 == 2:
                    emit_F0_pe(b + 1, 0, st_new)
                elif i2 == 6:
                    emit_F0_pe(b + 1, 1, st_new)
                elif i2 == 8:
                    emit_F0_dve(b + 1, 2, st_new, pw=list(g_half_done[0]))
                elif i2 == 13:
                    emit_F0_dve(b + 1, 3, st_new)
                elif i2 == 14:
                    emit_F0_pe(b + 1, 2, st_new)
                elif i2 == 22:
                    emit_F0_pe(b + 1, 3, st_new)
                elif i2 == 24:
                    emit_crep(st_new[("ev", 3)], st_new[("vc", 3)])
        tdn = P.n("pt")
        g_half_done[1], _rel1 = emit_epi(1, tdn, st_new, have_next)
        if have_next:
            f0_ready = (st_new[("ev", 3)], st_new[("vc", 3)])
            st_cur = st_new
        g_prev_f2 = dict(g_wd2)
        g_strep_w = [("pv", P.n("pv"))]
        g_xw_free = list(g_half_done[1])

    dma_finals = [(sn, P.n(sn)) for sn in semnames if sn.startswith("d") and P.n(sn) > 0]

    eng_ops = {e: [] for e in ("y", "s", "v", "t", "g")}
    for (e, fn, waits, incs) in P.ops:
        eng_ops[e].append((fn, waits, incs))

    with es, nc.Block() as block:
        def runner(name):
            def run(eng):
                for (fn, waits, incs) in eng_ops[name]:
                    for (sn, val) in waits:
                        eng.wait_ge(sems[sn], val)
                    inst = fn(eng)
                    for (sn, k) in incs:
                        inst.then_inc(sems[sn], k)
                if name == "y":
                    for sn, v in dma_finals:
                        eng.wait_ge(sems[sn], v)
            return run
        block.sync(runner("y"))
        block.scalar(runner("s"))
        block.vector(runner("v"))
        block.tensor(runner("t"))
        block.gpsimd(runner("g"))

    return nc


_NC_CACHE = []
TAP = {}


def kernel(**inputs):
    x = np.ascontiguousarray(np.asarray(inputs["L0_attn_inputs"], dtype=np.float32))
    shards = x.reshape(NC, SEQ * L, D)
    base = {
        "wqk": np.ascontiguousarray(np.asarray(inputs["L0_attn_W_qk"], np.float32)),
        "wv": np.ascontiguousarray(np.asarray(inputs["L0_attn_W_v"], np.float32)),
        "wout": np.ascontiguousarray(np.asarray(inputs["L0_attn_W_out"], np.float32)),
        "qs": np.asarray(inputs["L0_attn_q_scale"], np.float32).reshape(1, HD),
        "qb": np.asarray(inputs["L0_attn_q_bias"], np.float32).reshape(1, HD),
        "ks": np.asarray(inputs["L0_attn_k_scale"], np.float32).reshape(1, HD),
        "kb": np.asarray(inputs["L0_attn_k_bias"], np.float32).reshape(1, HD),
        "wg": np.ascontiguousarray(np.asarray(inputs["L0_ffn_w_gate"], np.float32)),
        "wu": np.ascontiguousarray(np.asarray(inputs["L0_ffn_w_up"], np.float32)),
        "wd": np.ascontiguousarray(np.asarray(inputs["L0_ffn_w_down"], np.float32)),
    }
    nc = build()
    in_maps = [dict(base, x_in=np.ascontiguousarray(shards[c])) for c in range(NC)]
    res = run_bass_kernel_spmd(nc, in_maps, core_ids=list(range(NC)))
    out = np.stack([res.results[c]["out"] for c in range(NC)], axis=0)
    return out.reshape(B, L, D).astype(np.float32)


# version 27
# speedup vs baseline: 1.0819x; 1.0819x over previous
import numpy as np
from contextlib import ExitStack

import concourse.bass as bass
import concourse.mybir as mybir
from concourse.bass_utils import run_bass_kernel_spmd

F32 = mybir.dt.float32
BF16 = mybir.dt.bfloat16
I16 = mybir.dt.int16
I8 = mybir.dt.int8
I32 = mybir.dt.int32
AF = mybir.ActivationFunctionType
ALU = mybir.AluOpType
AX = mybir.AxisListType

B, L, D, H, HD, INNER = 16, 2048, 1024, 16, 64, 4096
NC = 8
SEQ = 2
NT = L // 128
NG = L // 512
NP = H // 2
TB = 512
NB = SEQ * L // TB
EPS_QK = float(1024 * 1e-6)
EPS_LN = 1e-5
EPS_OP = float(1024 * 1e-5)

VB0 = 0
LF0 = 16640
QB0 = 33024
KB0 = 35072
R10 = 37120
OS0 = 39168
ZB0 = 39680
JK0 = 41728
BIGN = 42752
WV0 = QB0
XQ0 = LF0
L2T0 = LF0 + 4096
XT0 = LF0 + 6144
M10 = LF0 + 8192
CI0 = LF0 + 8704
CR0 = LF0 + 9728
JF0 = LF0 + 11264
OT0 = LF0 + 12288
PW0 = 0
PO0 = 8192
PT0 = 12288


class Plan:
    def __init__(self):
        self.ops = []
        self.cnt = {}
        self.pending = {}

    def n(self, sem):
        return self.cnt.get(sem, 0)

    def bar(self, engines, sems_):
        snap = [(s, self.n(s)) for s in sems_ if self.n(s) > 0]
        for e in engines:
            self.pending.setdefault(e, []).extend(snap)

    def op(self, eng, fn, waits=(), inc=None, k=1):
        incs = ()
        if inc is not None:
            self.cnt[inc] = self.cnt.get(inc, 0) + k
            incs = ((inc, k),)
        w = list(self.pending.pop(eng, []))
        w += [(s, v) for (s, v) in waits if v > 0]
        best = {}
        for s, v in w:
            best[s] = max(best.get(s, 0), v)
        self.ops.append((eng, fn, tuple(best.items()), incs))
        return self.cnt.get(inc, 0)

    def dma(self, eng, fn, waits=(), queue="y", nrot=8):
        key = ("dmaidx", queue)
        i = self.cnt.get(key, 0)
        self.cnt[key] = i + 1
        semn = "d%s%d" % (queue, i % nrot)
        v = self.n(semn) + 16
        w = list(waits)
        if v > 16:
            w.append((semn, v - 16))
        self.op(eng, fn, w, inc=semn, k=16)
        return (semn, v)


def build():
    nc = bass.Bass("TRN2", target_bir_lowering=False)
    es = ExitStack()

    dtn = lambda n, s, d, k: nc.dram_tensor(n, s, d, kind=k).ap()
    x_in = dtn("x_in", [SEQ * L, D], F32, "ExternalInput")
    wqk = dtn("wqk", [D, 2 * D], F32, "ExternalInput")
    wv = dtn("wv", [D, D], F32, "ExternalInput")
    wout = dtn("wout", [D, D], F32, "ExternalInput")
    qs_in = dtn("qs", [1, HD], F32, "ExternalInput")
    qb_in = dtn("qb", [1, HD], F32, "ExternalInput")
    ks_in = dtn("ks", [1, HD], F32, "ExternalInput")
    kb_in = dtn("kb", [1, HD], F32, "ExternalInput")
    wg_in = dtn("wg", [INNER, D], F32, "ExternalInput")
    wu_in = dtn("wu", [INNER, D], F32, "ExternalInput")
    wd_in = dtn("wd", [D, INNER], F32, "ExternalInput")
    out_hbm = dtn("out", [SEQ * L, D], F32, "ExternalOutput")
    s_wqk = dtn("s_wqk", [D, 2 * D], BF16, "Internal")
    s_wg = dtn("s_wg", [INNER, D], BF16, "Internal")
    s_wu = dtn("s_wu", [INNER, D], BF16, "Internal")
    s_wd = dtn("s_wd", [INNER, D], BF16, "Internal")
    s_x1 = dtn("s_x1", [SEQ * L, D], F32, "Internal")

    sb = lambda n, shape, d: es.enter_context(nc.sbuf_tensor(n, shape, d))
    ident = sb("ident", [128, 128], BF16)
    onesb = sb("onesb", [128, 128], BF16)
    opair = sb("opair", [128, 128], BF16)
    cst = sb("cst", [128, 16], BF16)
    ltri = sb("ltri", [128, 128], BF16)
    masks = sb("masks", [128, 1024], BF16)
    iot = sb("iot", [128, 512], I32)
    iotb = iot.bitcast(BF16)
    sQ = sb("sQ", [128, 1], F32); bQ = sb("bQ", [128, 1], F32)
    sK = sb("sK", [128, 1], F32); bK = sb("bK", [128, 1], F32)
    shiftp = sb("shiftp", [128, 1], F32)
    mbias = sb("mbias", [128, 1], F32)
    nshift2 = sb("nshift2", [128, 1], F32)
    shiftW = sb("shiftW", [128, 1], F32)
    nshiftW = sb("nshiftW", [128, 1], F32)
    iswg = sb("iswg", [128, 1], F32)
    iswu = sb("iswu", [128, 1], F32)
    kapp = sb("kapp", [128, 1], F32)
    swq = {"wg": sb("swgq", [128, 1], F32), "wu": sb("swuq", [128, 1], F32),
           "wd": sb("swdq", [128, 1], F32)}
    smal = sb("smal", [128, 128], F32)
    statq = sb("statq", [128, 32], F32)
    bn6 = sb("bn6", [128, 16], F32)
    mv2 = sb("mv2", [128, 8], F32)
    big = sb("big", [128, BIGN], BF16)
    wsA = sb("wsA", [128, 8192], BF16)
    wsB = sb("wsB", [128, 4096], F32)
    _i16 = sb("i16a", [128, 1024], I16)
    i16d = (_i16, _i16)
    obufT = sb("obufT", [128, 16384], BF16)
    wqk2 = sb("wqk2", [128, 4096], BF16)
    qkbuf2 = sb("qkbuf2", [128, 8192], BF16)
    jF32 = sb("jF32", [128, 1024], F32)
    ln2f = sb("ln2f", [128, 2048], F32)
    crepf = sb("crepf", [128, 1536], F32)
    rrow = sb("rrow", [128, 1536], F32)
    lstag = rrow[:, 1024:1536]
    l_all = sb("l_all", [128, 512], F32)
    rlb = sb("rlb", [128, 512], BF16)
    rl2 = sb("rl2", [128, 512], BF16)

    ps_ = lambda n, shape: es.enter_context(nc.psum_tensor(n, shape, F32))
    pS = ps_("pS", [128, 1024])
    pS2 = ps_("pS2", [128, 1024])
    pT = ps_("pT", [128, 1024])
    pU = ps_("pU", [128, 1024])

    semnames = ["pt", "pv", "ps", "pg"]
    for q in ("y", "s", "g"):
        semnames += ["d%s%d" % (q, i) for i in range(8)]
    sems = {s: es.enter_context(nc.semaphore(s)) for s in semnames}

    P = Plan()
    OP = P.op
    DMA = P.dma
    ALL_ENG = ("y", "s", "v", "t", "g")
    COMPUTE_SEMS = tuple(semnames)

    def BAR():
        P.bar(ALL_ENG, COMPUTE_SEMS)

    def mm(out_ap, lhsT, rhs, start, stop, waits=(), tp=None):
        return OP("t", lambda e, o=out_ap, l=lhsT, r=rhs, a=start, b=stop, t_=tp:
                  e.matmul(o, l, r, start=a, stop=b, tile_position=t_,
                           skip_group_check=True),
                  waits, "pt")

    def rsq(dst, src, waits, scale_half=True):
        OP("s", lambda e, d=dst, s_=src: e.activation(d, s_, AF.Ln), waits, "ps")
        sc = -0.5 if scale_half else -1.0
        OP("s", lambda e, d=dst, sc=sc: e.activation(d, d, AF.Exp, scale=sc),
           [("ps", P.n("ps"))], "ps")

    tap_outs = []

    def tap(name, ap_fn, shape, dtype=BF16):
        if not TAP.get(name):
            return
        t = nc.dram_tensor("tap_" + name, shape, dtype, kind="ExternalOutput").ap()
        P.bar(("y",), COMPUTE_SEMS)
        tok = DMA("y", lambda e, t=t, ap_fn=ap_fn: e.dma_start(t, ap_fn()), ())
        for e_ in ALL_ENG:
            P.pending.setdefault(e_, []).append(tok)
        tap_outs.append(name)

    bigf = lambda o, n: big[:, o:o + 2 * n].bitcast(F32)
    vbuf = lambda c: big[:, VB0 + c * 1040: VB0 + (c + 1) * 1040]
    ln1f = lambda ic: big[:, LF0 + ic * 2048: LF0 + (ic + 1) * 2048]
    qbp = qkbuf2[:, 0:4096]
    kbp = qkbuf2[:, 4096:8192]
    wqkp = big[:, R10:R10 + 2048]
    pbuf = lambda j: big[:, R10 + j * 1024: R10 + (j + 1) * 1024]
    ostag = big[:, OS0:OS0 + 512]
    lnt = lambda j: big[:, ZB0 + j * 1024: ZB0 + (j + 1) * 1024]
    junkA = big[:, JK0:JK0 + 1024]
    wvo = lambda ic: big[:, WV0 + ic * 1024: WV0 + (ic + 1) * 1024]
    tblk = lambda c: big[:, VB0 + c * 512: VB0 + (c + 1) * 512]
    xqf = lambda hc: big[:, XQ0 + hc * 512: XQ0 + (hc + 1) * 512]
    ln2t = lambda j: big[:, L2T0 + j * 1024: L2T0 + (j + 1) * 1024]
    xqtm = lambda j: big[:, XT0 + j * 1024: XT0 + (j + 1) * 1024]
    M1 = big[:, M10:M10 + 512]
    cin = lambda j: big[:, CI0 + j * 128: CI0 + (j + 1) * 128]
    cstage = big[:, CI0 + 512:CI0 + 640]
    stin = big[:, CI0 + 640:CI0 + 768]
    ststage = big[:, CI0 + 768:CI0 + 896]
    crep = lambda j: big[:, CR0 + j * 512: CR0 + (j + 1) * 512]
    junkF = big[:, JF0:JF0 + 1024]
    outst = lambda j: big[:, OT0 + j * 2048: OT0 + (j + 1) * 2048].bitcast(F32)

    obuf = lambda s, p: obufT[:, p * 2048:(p + 1) * 2048]

    OP("g", lambda e: e.iota(iot[:], [[-1, 512]], base=0, channel_multiplier=1), (), "pg")
    gi = P.n("pg")
    OP("v", lambda e: e.tensor_scalar(ident[:], iot[:, 0:128], 0.0, None, ALU.is_equal),
       [("pg", gi)], "pv")
    OP("v", lambda e: e.tensor_scalar(ltri[:], iot[:, 0:128], 0.0, None, ALU.is_le),
       [("pg", gi)], "pv")
    OP("v", lambda e: e.tensor_scalar(masks[:, 512:1024], iot[:], 1.0, -1e30,
                                      ALU.is_equal, ALU.mult), [("pg", gi)], "pv")
    OP("v", lambda e: e.memset(masks[:, 0:512], 0.0), (), "pv")
    OP("v", lambda e: e.memset(masks[0:1, 0:512], -1e30), [("pv", P.n("pv"))], "pv")
    OP("v", lambda e: e.memset(onesb[:], 1.0), (), "pv")
    OP("v", lambda e: e.memset(opair[:], 0.0), (), "pv")
    OP("v", lambda e: e.memset(opair[0:1, 0:64], 1.0), [("pv", P.n("pv"))], "pv")
    OP("v", lambda e: e.memset(big[0:1, 0:64], 0.0), (), "pv")
    OP("v", lambda e: e.memset(big[0:1, 64:128], 1.0), [("pv", P.n("pv"))], "pv")
    vop = P.n("pv")
    DMA("y", lambda e: e.dma_start(opair[1:2, 0:128], big[0:1, 0:128]),
        [("pv", vop)])
    OP("v", lambda e: e.memset(smal[:, 0:1], 1.5), (), "pv")
    OP("v", lambda e: e.tensor_scalar(i16d[0][:, 0:1], smal[:, 0:1], 1.0, None, ALU.mult),
       [("pv", P.n("pv"))], "pv")
    OP("v", lambda e: e.tensor_scalar(shiftp[:], i16d[0][:, 0:1], -0.5, 1.0, ALU.mult, ALU.add),
       [("pv", P.n("pv"))], "pv")
    OP("v", lambda e: e.tensor_scalar(nshiftW[:], shiftp[:], -32.0, None, ALU.mult),
       [("pv", P.n("pv"))], "pv")
    OP("v", lambda e: e.tensor_scalar(shiftW[:], shiftp[:], 33.0, None, ALU.mult),
       [("pv", P.n("pv"))], "pv")
    OP("v", lambda e: e.tensor_scalar(nshift2[:], shiftp[:], -512.0, None, ALU.mult),
       [("pv", P.n("pv"))], "pv")
    OP("v", lambda e: e.tensor_scalar(shiftp[:], shiftp[:], 513.0, None, ALU.mult),
       [("pv", P.n("pv"))], "pv")
    toks = []
    for r, src in enumerate((qs_in, qb_in, ks_in, kb_in)):
        toks.append(DMA("y", lambda e, r=r, src=src: e.dma_start(smal[r:r + 1, 64:128], src)))
    for dst, src in ((sQ, qs_in), (bQ, qb_in), (sK, ks_in), (bK, kb_in)):
        toks.append(DMA("y", lambda e, d=dst, s_=src: e.dma_start(d[0:64, 0:1], s_.transpose([1, 0]))))
        toks.append(DMA("y", lambda e, d=dst, s_=src: e.dma_start(d[64:128, 0:1], s_.transpose([1, 0]))))
    OP("v", lambda e: e.reduce_max(smal[0:4, 1:2], smal[0:4, 64:128], AX.X,
                                   apply_absolute_value=True), toks, "pv")
    OP("v", lambda e: e.tensor_copy(cst[0:4, 0:1], smal[0:4, 1:2]), [("pv", P.n("pv"))], "pv")
    mm(pU[0:1, 0:1], cst[0:4, 0:1], onesb[0:4, 0:1], True, True, [("pv", P.n("pv"))])
    OP("s", lambda e: e.activation(smal[0:1, 2:3], pU[0:1, 0:1], AF.Square, scale=0.5),
       [("pt", P.n("pt"))], "ps")
    OP("v", lambda e: e.tensor_scalar(cst[0:1, 1:2], smal[0:1, 2:3], -1.0, None, ALU.mult),
       [("ps", P.n("ps"))], "pv")
    mm(pU[:, 1:2], onesb[0:1, :], cst[0:1, 1:2], True, True, [("pv", P.n("pv"))])
    OP("v", lambda e: e.tensor_copy(mbias[:], pU[:, 1:2]), [("pt", P.n("pt"))], "pv")
    BAR()

    pwf = lambda sl: bigf(PW0 + sl * 4096, 2048)
    pwo = lambda sl: big[:, PO0 + sl * 2048: PO0 + (sl + 1) * 2048]
    g_rd = {}
    g_st = {}
    for ic in range(8):
        sl = ic % 2
        w = [("pv", g_rd[ic - 2])] if ic >= 2 else []
        tok = DMA("y", lambda e, ic=ic, sl=sl: e.dma_start(pwf(sl),
                                                           wqk[ic * 128:(ic + 1) * 128, :]), w)
        OP("v", lambda e, sl=sl: e.reduce_sum(
            smal[:, 32:64], pwf(sl).rearrange("p (b j) -> p b j", j=64), AX.X),
           [tok, ("pv", P.n("pv"))], "pv")
        OP("v", lambda e: e.tensor_scalar(smal[:, 32:64], smal[:, 32:64], 1.0 / 64, None,
                                          ALU.mult), [("pv", P.n("pv"))], "pv")
        vme = P.n("pv")
        wst = [g_st[ic - 2]] if ic >= 2 else []
        for e2 in range(2):
            for half in range(2):
                def fprm(e, sl=sl, e2=e2, half=half):
                    i_ap = pwf(sl).rearrange("p (pp e h j) -> p pp e h j",
                                             e=2, h=2, j=64)[:, :, e2, half, :]
                    o_ap = pwo(sl).rearrange("p (pp h e j) -> p pp h e j",
                                             h=2, e=2, j=64)[:, :, half, e2, :]
                    m_ap = smal[:, 32:64].rearrange("p (pp e h) -> p pp e h",
                                                    e=2, h=2)[:, :, e2, half]
                    m_ap = m_ap.unsqueeze(2).broadcast_to([128, 8, 64])
                    return e.tensor_tensor(o_ap, i_ap, m_ap, ALU.subtract)
                OP("v", fprm, [("pv", vme)] + wst, "pv")
        g_rd[ic] = P.n("pv")
        g_st[ic] = DMA("g", lambda e, ic=ic, sl=sl: e.dma_start(
            s_wqk[ic * 128:(ic + 1) * 128, :], pwo(sl)), [("pv", P.n("pv"))], queue="g")
    wqk_stoks = dict(g_st)

    w1f = lambda sl: wsB[:, sl * 1024:(sl + 1) * 1024]
    p2o = lambda sl: wsA[:, sl * 1024:(sl + 1) * 1024]
    ptst = lambda sl: wsA[:, 4096 + sl * 1024: 4096 + (sl + 1) * 1024]
    dstb_wg = lambda rc, cc: s_wg[rc * 128:(rc + 1) * 128, :].rearrange(
        "p (a x) -> p a x", x=128)
    dstb_wu = lambda rc, cc: s_wu[rc * 128:(rc + 1) * 128, :].rearrange(
        "p (a x) -> p a x", x=128)
    dstb_wd = lambda rc, cc: s_wd[cc * 1024:(cc + 1) * 1024, rc * 128:(rc + 1) * 128].rearrange(
        "(a p) x -> p a x", p=128)
    wmeta = []
    for name, w_hbm, dstb in (("wg", wg_in, dstb_wg), ("wu", wu_in, dstb_wu),
                              ("wd", wd_in, dstb_wd)):
        R, C = w_hbm.shape
        wmeta.append((name, w_hbm, dstb, R // 128, C // 1024))
    gslot = [0]
    cons = {}

    def slot_load(src, eng, queue, extra=()):
        gi_ = gslot[0]
        sl = gi_ % 4
        w = list(extra)
        if gi_ >= 4:
            w.append(("pv", cons[gi_ - 4]))
        tok = DMA(eng, lambda e, sl=sl, src=src: e.dma_start(w1f(sl), src), w, queue=queue)
        gslot[0] = gi_ + 1
        return gi_, sl, tok

    pcol = {"wg": 64, "wu": 32, "wd": 0}
    for (name, w_hbm, dstb, nr, ncc) in wmeta:
        base = pcol[name]
        for idx in range(nr * ncc):
            rc, cc = divmod(idx, ncc)
            gi_, sl, tok = slot_load(
                w_hbm[rc * 128:(rc + 1) * 128, cc * 1024:(cc + 1) * 1024], "s", "s")
            OP("v", lambda e, sl=sl, co=base + idx: e.reduce_sum(
                smal[:, co:co + 1], w1f(sl), AX.X, apply_absolute_value=True), [tok], "pv")
            cons[gi_] = P.n("pv")
    for (name, w_hbm, dstb, nr, ncc) in wmeta:
        base = pcol[name]
        R, C = w_hbm.shape
        OP("v", lambda e, base=base: e.reduce_sum(smal[:, 96:97], smal[:, base:base + 32],
                                                  AX.X), [("pv", P.n("pv"))], "pv")
        tokg = DMA("g", lambda e: e.dma_start(rrow[0:1, 0:128], smal[:, 96:97]),
                   [("pv", P.n("pv"))], queue="g")
        OP("v", lambda e: e.reduce_sum(smal[0:1, 97:98], rrow[0:1, 0:128], AX.X),
           [tokg], "pv")
        OP("v", lambda e, sc=1.0 / (R * C): e.tensor_scalar(
            smal[0:1, 97:98], smal[0:1, 97:98], sc, 1e-5, ALU.mult, ALU.max),
           [("pv", P.n("pv"))], "pv")
        OP("v", lambda e: e.reciprocal(smal[0:1, 98:99], smal[0:1, 97:98]),
           [("pv", P.n("pv"))], "pv")
        OP("v", lambda e: e.tensor_copy(rrow[0:1, 0:128],
                                        smal[0:1, 98:99].broadcast_to([1, 128])),
           [("pv", P.n("pv"))], "pv")
        tokr = DMA("g", lambda e, d=swq[name]: e.dma_start(d[:], rrow[0:1, 0:128]),
                   [("pv", P.n("pv"))], queue="g")
        for e_ in ALL_ENG:
            P.pending.setdefault(e_, []).append(tokr)
        if name in ("wg", "wu"):
            dstc = iswg if name == "wg" else iswu
            OP("v", lambda e: e.tensor_scalar(smal[0:1, 99:100], smal[0:1, 97:98], 1.0 / 127,
                                              None, ALU.mult), [("pv", P.n("pv"))], "pv")
        else:
            dstc = kapp
            OP("v", lambda e: e.tensor_tensor(smal[0:1, 99:100], smal[0:1, 98:99],
                                              smal[0:1, 98:99], ALU.mult),
               [("pv", P.n("pv"))], "pv")
            OP("v", lambda e: e.tensor_scalar(smal[0:1, 99:100], smal[0:1, 99:100],
                                              1e-5 * 127.0 * 127.0, None, ALU.mult),
               [("pv", P.n("pv"))], "pv")
        OP("v", lambda e: e.tensor_copy(rrow[0:1, 0:128],
                                        smal[0:1, 99:100].broadcast_to([1, 128])),
           [("pv", P.n("pv"))], "pv")
        tokc = DMA("g", lambda e, d=dstc: e.dma_start(d[:], rrow[0:1, 0:128]),
                   [("pv", P.n("pv"))], queue="g")
        for e_ in ALL_ENG:
            P.pending.setdefault(e_, []).append(tokc)
    evacv = {}
    trv = {}
    stok = {}
    p2i = [0]
    for (name, w_hbm, dstb, nr, ncc) in wmeta:
        for idx in range(nr * ncc):
            rc, cc = divmod(idx, ncc)
            gi_, sl, tok = slot_load(
                w_hbm[rc * 128:(rc + 1) * 128, cc * 1024:(cc + 1) * 1024], "y", "y")
            i2_ = p2i[0]
            p2i[0] += 1
            OP("v", lambda e, sl=sl, n_=name: e.tensor_scalar(
                i16d[0][:], w1f(sl), swq[n_][:], shiftW[:], ALU.mult, ALU.add), [tok], "pv")
            wp2 = [("pv", P.n("pv"))]
            if (i2_ - 4) in trv:
                wp2.append(("pt", trv[i2_ - 4]))
            OP("v", lambda e, sl=sl: e.tensor_scalar(
                p2o(sl), i16d[0][:], nshiftW[:], -1.0, ALU.add, ALU.max),
               wp2, "pv")
            OP("v", lambda e, sl=sl: e.tensor_scalar(
                p2o(sl), p2o(sl), 1.0, None, ALU.min), [("pv", P.n("pv"))], "pv")
            vq2 = P.n("pv")
            cons[gi_] = vq2
            poff = (i2_ % 2) * 512
            sl2 = i2_ % 2
            for jb in range(2):
                pb = pT if jb == 0 else pU
                wtr = [("pv", vq2)] if jb == 0 else []
                if jb == 0 and (i2_ - 2) in evacv:
                    wtr.append(("ps", evacv[i2_ - 2]))
                for j in range(4):
                    jj = jb * 4 + j
                    mm(pb[:, poff + j * 128: poff + (j + 1) * 128],
                       p2o(sl)[:, jj * 128:(jj + 1) * 128],
                       ident[:], True, True, wtr if j == 0 else ())
                ttx = P.n("pt")
                ws_ = [("pt", ttx)]
                if jb == 0 and sl2 in stok:
                    ws_.append(stok[sl2])
                OP("s", lambda e, jb=jb, pb=pb, sl2=sl2, poff=poff: e.activation(
                    ptst(sl2)[:, jb * 512:(jb + 1) * 512], pb[:, poff:poff + 512], AF.Copy),
                   ws_, "ps")
            evacv[i2_] = P.n("ps")
            trv[i2_] = P.n("pt")

            def fst(e, rc=rc, cc=cc, sl2=sl2, dstb=dstb):
                return e.dma_start(dstb(rc, cc),
                                   ptst(sl2).rearrange("p (a x) -> p a x", x=128))
            stok[sl2] = DMA("g", fst, [("ps", evacv[i2_])], queue="g")
    g_prep_v = P.n("pv")
    g_prep_ps = P.n("ps")
    prep_stoks = [stok[0], stok[1]]
    tap("mbias", lambda: mbias[:, :], [128, 1], F32)
    tap("shiftp", lambda: shiftp[:, :], [128, 1], F32)
    tap("iswg", lambda: iswg[:, :], [128, 1], F32)
    tap("kapp", lambda: kapp[:, :], [128, 1], F32)

    for s in range(SEQ):
        xs = x_in[s * L:(s + 1) * L, :]
        g_a1 = {}
        for tt in range(NT):
            sl = tt % 2
            if tt >= 2:
                w = [("pv", g_a1[tt - 2])]
            else:
                w = [("pv", g_prep_v)] if s == 0 else []
            tok = DMA("y", lambda e, sl=sl, src=xs[tt * 128:(tt + 1) * 128, :]: e.dma_start(
                wsB[:, sl * 1024:(sl + 1) * 1024], src), w)
            OP("v", lambda e, sl=sl: e.bn_stats(bn6[:, 0:6], wsB[:, sl * 1024:sl * 1024 + 512]),
               [tok], "pv")
            OP("v", lambda e, sl=sl: e.bn_stats(bn6[:, 6:12],
                                                wsB[:, sl * 1024 + 512:(sl + 1) * 1024]),
               [("pv", P.n("pv"))], "pv")
            OP("v", lambda e: e.bn_aggr(mv2[:, 0:2], bn6[:, 0:12]), [("pv", P.n("pv"))], "pv")
            OP("v", lambda e: e.tensor_scalar(mv2[:, 2:3], mv2[:, 1:2], 1.0, EPS_LN,
                                              ALU.mult, ALU.add), [("pv", P.n("pv"))], "pv")
            rsq(mv2[:, 3:4], mv2[:, 2:3], [("pv", P.n("pv"))])
            sv = P.n("ps")
            OP("v", lambda e, sl=sl: e.tensor_scalar(lnt(sl), wsB[:, sl * 1024:(sl + 1) * 1024],
                                                     mv2[:, 0:1], mv2[:, 3:4],
                                                     ALU.subtract, ALU.mult), [("ps", sv)], "pv")
            vln = P.n("pv")
            for jb in range(2):
                pb = pT if jb == 0 else pU
                for j in range(4):
                    ic = jb * 4 + j
                    wtr0 = [("pv", vln)]
                    if s == 0 and tt == 0 and jb == 0:
                        wtr0.append(("ps", g_prep_ps))
                    mm(pb[:, j * 128:(j + 1) * 128], lnt(sl)[:, ic * 128:(ic + 1) * 128],
                       ident[:], True, True, wtr0 if ic == 0 else ())
                tv = P.n("pt")
                def fev(e, tt=tt, jb=jb, pb=pb):
                    reg = big[:, LF0 + jb * 8192: LF0 + (jb + 1) * 8192]
                    dst = reg.rearrange("p (a x) -> p a x", x=2048)[
                        :, :, tt * 128:(tt + 1) * 128]
                    return e.activation(dst, pb[:, 0:512].rearrange("p (a x) -> p a x", x=128),
                                        AF.Copy)
                OP("s", fev, [("pt", tv), ("ps", P.n("ps"))], "ps")
            g_a1[tt] = P.n("pv")
            if tt == 0:
                P.bar(("y",), COMPUTE_SEMS)
                tap("ln1f_t0_s%d" % s, lambda: big[:, LF0:LF0 + 128], [128, 128])
                tap("lnt0_s%d" % s, lambda: lnt(0), [128, 1024])
                tap("wsB0_s%d" % s, lambda: wsB[:, 0:1024], [128, 1024], F32)
                tap("mv20_s%d" % s, lambda: mv2[:, 0:4], [128, 4], F32)
                tap("bn60_s%d" % s, lambda: bn6[:, 0:12], [128, 12], F32)
        BAR()
        tap("ln1f_s%d" % s, lambda: big[:, LF0:LF0 + 16384], [128, 16384])

        wvtoks = []
        for ic in range(8):
            wvtoks.append(DMA("g", lambda e, ic=ic: e.dma_start(
                wvo(ic), wv[ic * 128:(ic + 1) * 128, :]), (), queue="g"))
        g_v = {}
        for tt in range(NT):
            pb = pS if tt % 2 == 0 else pS2
            w0 = list(wvtoks)
            if tt >= 2:
                w0.append(("ps", g_v[tt - 2]))
            for oc in range(2):
                for ic in range(8):
                    mm(pb[:, oc * 512:(oc + 1) * 512], ln1f(ic)[:, tt * 128:(tt + 1) * 128],
                       wvo(ic)[:, oc * 512:(oc + 1) * 512], ic == 0, ic == 7,
                       w0 if (oc == 0 and ic == 0) else ())
            tv = P.n("pt")
            OP("v", lambda e, tt=tt: e.memset(
                vbuf(tt).rearrange("p (h x) -> p h x", x=65)[:, :, 64:65], 1.0), (), "pv")
            vo = P.n("pv")
            for oc in range(2):
                def fvev(e, tt=tt, oc=oc, pb=pb):
                    dst = vbuf(tt).rearrange("p (h x) -> p h x", x=65)[
                        :, oc * 8:(oc + 1) * 8, 0:64]
                    return e.activation(dst, pb[:, oc * 512:(oc + 1) * 512].rearrange(
                        "p (h x) -> p h x", x=64), AF.Copy, scale=1.0 / 32)
                OP("s", fvev, [("pt", tv), ("pv", vo)], "ps")
            g_v[tt] = P.n("ps")
        BAR()
        tap("vbuf_s%d" % s, lambda: big[:, VB0:VB0 + 16640], [128, 16640])

        g_omove = []
        g_lmove = []
        ltoks = []
        g_attn_done = {}
        g_qk_ready = {}
        g_exp_ps = {(0, 0): 0, (0, 1): 0, (1, 0): 0, (1, 1): 0}
        g_pbuf_pt = {(0, 0): 0, (0, 1): 0, (1, 0): 0, (1, 1): 0}
        g_oacc = {0: (0, 0), 1: (0, 0)}
        g_projslot = [0]
        g_proj_state = {}

        def proj_partA(p, rnd):
            par = p % 2
            ttg = rnd // 2
            hf = rnd % 2
            if rnd == 0:
                toks = []
                for ic in range(8):
                    wld = [("pt", g_attn_done.get(p - 2, 0))] if ic == 0 else []
                    if p == 0:
                        wld.append(wqk_stoks[ic])
                    toks.append(DMA("y", lambda e, p=p, ic=ic, par=par: e.dma_start(
                        wqk2[:, par * 2048 + ic * 256: par * 2048 + (ic + 1) * 256],
                        s_wqk[ic * 128:(ic + 1) * 128, p * 256:(p + 1) * 256]), wld))
                g_qk_ready[(p, "wtok")] = toks
            toks = g_qk_ready[(p, "wtok")]
            slots = (pS[:, 0:256], pS[:, 512:768])
            w0 = list(toks) + [("pv", g_projslot[0])]
            if rnd == 0 and p >= 2:
                w0.append(("pt", g_attn_done.get(p - 2, 0)))
            for ic in range(8):
                for tj in range(2):
                    tt = ttg * 4 + hf * 2 + tj
                    mm(slots[tj], ln1f(ic)[:, tt * 128:(tt + 1) * 128],
                       wqk2[:, par * 2048 + ic * 256: par * 2048 + (ic + 1) * 256],
                       ic == 0, ic == 7, w0 if (ic == 0 and tj == 0) else ())
            tprj = P.n("pt")
            for tj in range(2):
                OP("s", lambda e, tj=tj, slots=slots: e.activation(
                    junkA[:, tj * 256:(tj + 1) * 256], slots[tj], AF.Square),
                   [("pt", tprj), ("ps", P.n("ps"))], "ps")
            g_proj_state[(p, rnd, "sq")] = P.n("ps")

        def proj_partA2(p, rnd):
            slots = (pS[:, 0:256], pS[:, 512:768])
            ssq = g_proj_state[(p, rnd, "sq")]
            OP("v", lambda e: e.reduce_sum(statq[:, 0:8],
                                           junkA[:, 0:512].rearrange("p (b j) -> p b j", j=64),
                                           AX.X), [("ps", ssq)], "pv")
            OP("v", lambda e: e.tensor_scalar(statq[:, 8:16], statq[:, 0:8], 1.0 / 64,
                                              EPS_QK, ALU.mult, ALU.add),
               [("pv", P.n("pv"))], "pv")
            rsq(statq[:, 16:24], statq[:, 8:16], [("pv", P.n("pv"))])
            srr = P.n("ps")
            zsl = rnd % 2
            for tj in range(2):
                OP("v", lambda e, tj=tj, zsl=zsl, slots=slots: e.tensor_tensor(
                    lnt(zsl)[:, tj * 256:(tj + 1) * 256].rearrange("p (b j) -> p b j", j=64),
                    slots[tj].rearrange("p (b j) -> p b j", j=64),
                    statq[:, 16 + tj * 4:16 + (tj + 1) * 4].unsqueeze(2).broadcast_to(
                        [128, 4, 64]), ALU.mult), [("ps", srr)], "pv")
            g_proj_state[(p, rnd)] = P.n("pv")

        def proj_partB(p, rnd):
            par = p % 2
            ttg = rnd // 2
            hf = rnd % 2
            vz1 = g_proj_state[(p, rnd)]
            zsl = rnd % 2
            for cls in range(2):
                for tj in range(2):
                    mm(pS[:, cls * 512 + tj * 128: cls * 512 + (tj + 1) * 128],
                       lnt(zsl)[:, (2 * tj + cls) * 128:(2 * tj + cls + 1) * 128],
                       ident[:], True, True,
                       [("pv", vz1)] if (cls == 0 and tj == 0) else ())
            ttr = P.n("pt")
            for cls in range(2):
                buf = qbp if cls == 0 else kbp
                sc = sQ if cls == 0 else sK
                bi = bQ if cls == 0 else bK
                OP("v", lambda e, buf=buf, sc=sc, bi=bi, par=par, ttg=ttg, hf=hf, cls=cls:
                   e.tensor_scalar(buf[:, par * 2048 + ttg * 512 + hf * 256:
                                       par * 2048 + ttg * 512 + (hf + 1) * 256],
                                   pS[:, cls * 512:cls * 512 + 256], sc[:], bi[:],
                                   ALU.mult, ALU.add), [("pt", ttr)], "pv")
            g_projslot[0] = P.n("pv")
            if rnd == 7:
                g_qk_ready[p] = P.n("pv")

        def emit_attn_pairgroup(p, g, hooks, cb0):
            par = p % 2
            nchunk = 4 * (g + 1)
            bufs = (pT, pS2)

            def emit_pv(ci):
                par2 = ci % 2
                for e2 in range(2):
                    h = 2 * p + e2
                    pOb = pU[:, e2 * 512:(e2 + 1) * 512]
                    wPV = [("ps", g_exp_ps[(e2, par2)])]
                    if ci == 0:
                        wPV.append(("pv", g_oacc[e2][0]))
                        wPV.append(("ps", g_oacc[e2][1]))
                    mm(pOb[0:65, :], vbuf(ci)[:, h * 65:(h + 1) * 65],
                       pbuf(e2)[:, par2 * 512:(par2 + 1) * 512],
                       ci == 0, ci == nchunk - 1, wPV)
                    g_pbuf_pt[(e2, par2)] = P.n("pt")

            for ci in range(nchunk):
                par2 = ci % 2
                off = 512 * g - 128 * ci
                diag = -384 <= off <= 0
                for e2 in range(2):
                    base = e2 * 64
                    tpos = (64, 0) if e2 == 1 else None
                    pb = bufs[e2]
                    wS = [("ps", g_exp_ps[(e2, par2)])]
                    if ci == 0 and e2 == 0:
                        wS.append(("pv", g_qk_ready[p]))
                    mm(pb[:, par2 * 512:(par2 + 1) * 512],
                       kbp[base:base + 64, par * 2048 + ci * 128:
                           par * 2048 + (ci + 1) * 128],
                       qbp[base:base + 64, par * 2048 + g * 512:
                           par * 2048 + (g + 1) * 512],
                       True, not diag, wS, tp=tpos)
                    if diag:
                        jm = (-off) // 128
                        mm(pb[:, par2 * 512:(par2 + 1) * 512], ltri[:],
                           masks[:, 512 - 128 * jm: 1024 - 128 * jm], False, True, ())
                    texp = P.n("pt")
                    OP("s", lambda e, pb=pb, e2=e2, par2=par2: e.activation(
                        pbuf(e2)[:, par2 * 512:(par2 + 1) * 512],
                        pb[:, par2 * 512:(par2 + 1) * 512],
                        AF.Exp, bias=mbias[:], scale=1.0 / 64),
                       [("pt", texp), ("pt", g_pbuf_pt[(e2, par2)])], "ps")
                    g_exp_ps[(e2, par2)] = P.n("ps")
                if ci > 0:
                    emit_pv(ci - 1)
                if ci % 2 == 1:
                    for fn in hooks.get(cb0 + ci // 2, ()):
                        fn()
            emit_pv(nchunk - 1)
            tPV = P.n("pt")
            for e2 in range(2):
                h = 2 * p + e2
                pOb = pU[:, e2 * 512:(e2 + 1) * 512]
                r = h * 4 + g
                if e2 == 0:
                    OP("v", lambda e, s=s, p=p, g=g, pOb=pOb: e.tensor_copy(
                        obuf(s, p)[0:64, g * 512:(g + 1) * 512], pOb[0:64, 0:512]),
                       [("pt", tPV)], "pv")
                else:
                    OP("v", lambda e, pOb=pOb: e.tensor_copy(ostag[0:64, :], pOb[0:64, 0:512]),
                       [("pt", tPV)] + g_omove, "pv")
                    g_omove[:] = [DMA("g", lambda e, s=s, p=p, g=g: e.dma_start(
                        obuf(s, p)[64:128, g * 512:(g + 1) * 512], ostag[0:64, :]),
                        [("pv", P.n("pv"))], queue="g")]
                OP("s", lambda e, pOb=pOb: e.copy(lstag[64:65, :], pOb[64:65, 0:512]),
                   [("pt", tPV)] + g_lmove, "ps")
                g_lmove[:] = [DMA("g", lambda e, r=r: e.dma_start(
                    l_all[r:r + 1, :], lstag[64:65, :]),
                    [("ps", P.n("ps"))], queue="g")]
                ltoks.extend(g_lmove)
                g_oacc[e2] = (P.n("pv"), P.n("ps"))
            g_attn_done[p] = P.n("pt")

        for rnd in range(8):
            proj_partA(0, rnd)
            proj_partA2(0, rnd)
            proj_partB(0, rnd)
        A_SLOTS = (0, 3, 5, 8, 10, 12, 15, 17)
        for p in range(NP):
            hooks = {}
            if p + 1 < NP:
                for rnd in range(8):
                    hooks.setdefault(A_SLOTS[rnd], []).append(
                        lambda rnd=rnd: proj_partA(p + 1, rnd))
                    hooks.setdefault(A_SLOTS[rnd] + 1, []).append(
                        lambda rnd=rnd: proj_partA2(p + 1, rnd))
                    hooks.setdefault(A_SLOTS[rnd] + 2, []).append(
                        lambda rnd=rnd: proj_partB(p + 1, rnd))
            cb0 = 0
            for g in range(NG):
                emit_attn_pairgroup(p, g, hooks, cb0)
                cb0 += 2 * (g + 1)
        BAR()
        OP("s", lambda e: e.activation(l_all[0:64, :], l_all[0:64, :], AF.Ln), ltoks, "ps")
        OP("s", lambda e: e.activation(rlb[0:64, :], l_all[0:64, :], AF.Exp, scale=-1.0),
           [("ps", P.n("ps"))], "ps")
        srl = P.n("ps")
        rlslot = (rl2[0:2, 0:512], iotb[0:2, 0:512], iotb[0:2, 512:1024])
        rl_mm = {}
        rl_mult = {}
        for idx in range(NP * NG):
            p, g = divmod(idx, NG)
            r0 = 8 * p + g
            sl3 = rlslot[idx % 3]
            wga = [("ps", srl)]
            if idx - 3 in rl_mm:
                wga.append(("pt", rl_mm[idx - 3]))
            tokA = DMA("g", lambda e, r0=r0, sl3=sl3: e.dma_start(sl3[0:1, :],
                                                                 rlb[r0:r0 + 1, :]),
                       wga, queue="g")
            tokB = DMA("g", lambda e, r0=r0, sl3=sl3: e.dma_start(sl3[1:2, :],
                                                                  rlb[r0 + 4:r0 + 5, :]),
                       (), queue="g")
            poff = (idx % 2) * 512
            wmm_ = [tokA, tokB]
            if idx - 2 in rl_mult:
                wmm_.append(("pv", rl_mult[idx - 2]))
            mm(pS[:, poff:poff + 512], opair[0:2, :], sl3[0:2, :], True, True, wmm_)
            rl_mm[idx] = P.n("pt")
            OP("v", lambda e, s=s, p=p, g=g, poff=poff: e.tensor_tensor(
                obuf(s, p)[:, g * 512:(g + 1) * 512], obuf(s, p)[:, g * 512:(g + 1) * 512],
                pS[:, poff:poff + 512], ALU.mult), [("pt", rl_mm[idx])], "pv")
            rl_mult[idx] = P.n("pv")
        BAR()
        tap("lall_s%d" % s, lambda: l_all[0:64, :], [64, 512], F32)
        tap("onorm_s%d" % s, lambda s=s: obuf(s, 0), [128, 2048])

        wotoks = []
        g_x1st = {0: [], 1: []}
        for ic in range(8):
            wotoks.append(DMA("g", lambda e, ic=ic: e.dma_start(
                wvo(ic), wout[ic * 128:(ic + 1) * 128, :]), (), queue="g"))
        g_a4 = {}
        for tt in range(NT):
            slx = tt % 2
            wx = [("pv", g_a4[tt - 2][1])] if tt >= 2 else []
            dyx = DMA("y", lambda e, slx=slx, src=xs[tt * 128:(tt + 1) * 128, :]: e.dma_start(
                wsB[:, slx * 1024:(slx + 1) * 1024], src), wx)
            pb = pS if tt % 2 == 0 else pS2
            w0 = list(wotoks)
            if tt >= 2:
                w0.append(("pv", g_a4[tt - 2][0]))
            for oc in range(2):
                for ic in range(8):
                    mm(pb[:, oc * 512:(oc + 1) * 512],
                       obuf(s, ic)[:, tt * 128:(tt + 1) * 128],
                       wvo(ic)[:, oc * 512:(oc + 1) * 512], ic == 0, ic == 7,
                       w0 if (oc == 0 and ic == 0) else ())
            ta4 = P.n("pt")
            OP("v", lambda e, pb=pb: e.bn_stats(bn6[:, 0:6], pb[:, 0:512]), [("pt", ta4)], "pv")
            OP("v", lambda e, pb=pb: e.bn_stats(bn6[:, 6:12], pb[:, 512:1024]),
               [("pv", P.n("pv"))], "pv")
            OP("v", lambda e: e.bn_aggr(mv2[:, 0:2], bn6[:, 0:12]), [("pv", P.n("pv"))], "pv")
            OP("v", lambda e: e.tensor_scalar(mv2[:, 2:3], mv2[:, 1:2], 1.0, EPS_OP,
                                              ALU.mult, ALU.add), [("pv", P.n("pv"))], "pv")
            rsq(mv2[:, 3:4], mv2[:, 2:3], [("pv", P.n("pv"))])
            sr4 = P.n("ps")
            OP("v", lambda e, pb=pb: e.tensor_scalar(jF32[:], pb[:], mv2[:, 0:1], mv2[:, 3:4],
                                                     ALU.subtract, ALU.mult), [("ps", sr4)], "pv")
            vstt = P.n("pv")
            slo = tt % 2
            OP("v", lambda e, slo=slo, slx=slx: e.tensor_tensor(
                outst(slo), wsB[:, slx * 1024:(slx + 1) * 1024], jF32[:], ALU.add),
               [("pv", vstt), dyx] + g_x1st[slo], "pv")
            g_x1st[slo] = [DMA("g", lambda e, dst=s_x1[(s * 16 + tt) * 128:
                                                       (s * 16 + tt + 1) * 128, :],
                               slo=slo: e.dma_start(dst, outst(slo)),
                               [("pv", P.n("pv"))], queue="g")]
            g_a4[tt] = (P.n("pv"), P.n("pv"))
        BAR()


    g_ost = {0: [], 1: []}
    g_strep_w = [("pv", P.n("pv")), ("pt", P.n("pt")), ("ps", P.n("ps"))]
    g_xw_free = list(g_strep_w)
    g_xqf_t = 0
    g_f1pg = [0]
    g_cin_t = 0
    f0_state = {}

    def emit_F0_dve(b, j, st, pw=()):
        sl = j % 2
        if True:
            w0 = list(st.get(("ld", j - 2), [])) + list(pw)
            x1tok = DMA(
                "y", lambda e, sl=sl, src=s_x1[(b * 4 + j) * 128:(b * 4 + j + 1) * 128, :]:
                e.dma_start(xw(sl), src), w0)
            OP("v", lambda e, sl=sl: e.bn_stats(bn6[:, 0:6], xw(sl)[:, 0:512]),
               [x1tok] + ([("pt", st[("pe", j - 2)])] if ("pe", j - 2) in st else []), "pv")
            OP("v", lambda e, sl=sl: e.bn_stats(bn6[:, 6:12], xw(sl)[:, 512:1024]),
               [("pv", P.n("pv"))], "pv")
            OP("v", lambda e: e.bn_aggr(mv2[:, 0:2], bn6[:, 0:12]), [("pv", P.n("pv"))], "pv")
            OP("v", lambda e: e.tensor_scalar(mv2[:, 2:3], mv2[:, 1:2], 1.0, EPS_LN,
                                              ALU.mult, ALU.add), [("pv", P.n("pv"))], "pv")
            rsq(mv2[:, 3:4], mv2[:, 2:3], [("pv", P.n("pv"))])
            s0 = P.n("ps")
            OP("v", lambda e, sl=sl: e.tensor_scalar(
                ln2f[:, sl * 1024:(sl + 1) * 1024], xw(sl),
                mv2[:, 0:1], mv2[:, 3:4], ALU.subtract, ALU.mult),
               [("ps", s0), ("pg", g_f1pg[0])], "pv")
            OP("v", lambda e, sl=sl: e.reduce_max(mv2[:, 4:5], ln2f[:, sl * 1024:(sl + 1) * 1024],
                                                  AX.X, apply_absolute_value=True),
               [("pv", P.n("pv"))], "pv")
            OP("v", lambda e: e.tensor_scalar(mv2[:, 5:6], mv2[:, 4:5], 1e-5, None, ALU.max),
               [("pv", P.n("pv"))], "pv")
            OP("v", lambda e: e.reciprocal(mv2[:, 6:7], mv2[:, 5:6]), [("pv", P.n("pv"))], "pv")
            OP("v", lambda e: e.tensor_scalar(mv2[:, 7:8], mv2[:, 6:7], 127.0, None, ALU.mult),
               [("pv", P.n("pv"))], "pv")
            vq0 = P.n("pv")
            OP("v", lambda e, sl=sl: e.tensor_scalar(i16d[sl][:],
                                                     ln2f[:, sl * 1024:(sl + 1) * 1024],
                                                     mv2[:, 7:8], shiftp[:],
                                                     ALU.mult, ALU.add), [("pv", vq0)], "pv")
            OP("v", lambda e, sl=sl: e.tensor_scalar(xqtm(sl), i16d[sl][:], nshift2[:], None,
                                                     ALU.add), [("pv", P.n("pv"))], "pv")
            wcin = [("pv", vq0)]
            if j == 0:
                wcin.append(("pt", g_cin_t))
            OP("v", lambda e, j=j: e.memset(cin(j)[:], 0.0), wcin, "pv")
            OP("v", lambda e, j=j: e.tensor_tensor(cin(j)[:, 0:1], mv2[:, 5:6], iswg[:],
                                                   ALU.mult), [("pv", P.n("pv"))], "pv")
            OP("v", lambda e, j=j: e.tensor_tensor(cin(j)[:, 32:33], mv2[:, 5:6], iswu[:],
                                                   ALU.mult), [("pv", P.n("pv"))], "pv")
            st[("vc", j)] = P.n("pv")
            st[("ld", j)] = [("pv", P.n("pv"))]

    def emit_F0_pe(b, j, st):
        sl = j % 2
        for jb in range(2):
            pb = pT if jb == 0 else pU
            wtr = [("pv", st[("vc", j)])]
            if jb == 0 and ("ev", j - 1) in st:
                wtr.append(("ps", st[("ev", j - 1)]))
            if j == 0 and jb == 0:
                wtr.append(("pt", g_xqf_t))
            for jj in range(4):
                hc = jb * 4 + jj
                mm(pb[:, jj * 128:(jj + 1) * 128], xqtm(sl)[:, hc * 128:(hc + 1) * 128],
                   ident[:], True, True, wtr if jj == 0 else ())
            ttx = P.n("pt")
            def fex(e, j=j, jb=jb, pb=pb):
                reg = big[:, XQ0 + jb * 2048: XQ0 + (jb + 1) * 2048]
                dst = reg.rearrange("p (a x) -> p a x", x=512)[
                    :, :, j * 128:(j + 1) * 128]
                return e.activation(dst, pb[:, 0:512].rearrange("p (a x) -> p a x", x=128),
                                    AF.Copy)
            OP("s", fex, [("pt", ttx), ("ps", P.n("ps"))], "ps")
        st[("pe", j)] = P.n("pt")
        st[("ev", j)] = P.n("ps")

    xw = lambda sl: wsB[:, 2048 + sl * 1024:2048 + (sl + 1) * 1024]

    obufT_f = obufT.bitcast(F32)
    qkbuf_f = qkbuf2.bitcast(F32)
    wqk2_f = wqk2.bitcast(F32)

    def tdst(c):
        if c < 16:
            return obufT_f[:, c * 512:(c + 1) * 512]
        if c < 24:
            return qkbuf_f[:, (c - 16) * 512:(c - 15) * 512]
        if c < 28:
            return wqk2_f[:, (c - 24) * 512:(c - 23) * 512]
        return wsB[:, (c - 28) * 512:(c - 27) * 512]

    def tdst2(p):
        c = 2 * p
        if c < 16:
            return obufT_f[:, c * 512:(c + 2) * 512]
        if c < 24:
            return qkbuf_f[:, (c - 16) * 512:(c - 14) * 512]
        if c < 28:
            return wqk2_f[:, (c - 24) * 512:(c - 22) * 512]
        return wsB[:, (c - 28) * 512:(c - 26) * 512]

    st_cur = {}
    for j in range(4):
        emit_F0_dve(0, j, st_cur, pw=list(g_xw_free) if j == 0 else ())
        emit_F0_pe(0, j, st_cur)
    f0_ready = (st_cur[("ev", 3)], st_cur[("vc", 3)])
    g_prev_f2 = {}
    crep_v = [0]

    def emit_crep(frps, frpv):
        nonlocal g_cin_t
        for j in range(4):
            mm(pT[:, j * 128:(j + 1) * 128], cin(j)[:], ident[:], True, True,
               [("ps", frps), ("pv", frpv)] if j == 0 else ())
        tci = P.n("pt")
        OP("v", lambda e: e.tensor_copy(big[:, CI0 + 512:CI0 + 1024], pT[:, 0:512]),
           [("pt", tci)], "pv")
        vci = P.n("pv")
        for cls in range(2):
            basep = cls * 32
            for j in range(4):
                mm(pU[:, j * 128:(j + 1) * 128], onesb[basep:basep + 1, :],
                   big[basep:basep + 1, CI0 + 512 + j * 128: CI0 + 512 + (j + 1) * 128],
                   True, True, [("pv", vci)] if j == 0 else (),
                   tp=(basep, 0) if basep else None)
            tcr = P.n("pt")
            OP("v", lambda e, cls=cls: e.tensor_copy(crepf[:, cls * 512:(cls + 1) * 512],
                                                     pU[:, 0:512]), [("pt", tcr)], "pv")
            vci = P.n("pv")
        g_cin_t = P.n("pt")
        crep_v[0] = P.n("pv")

    for b in range(NB):
        if b == 0:
            emit_crep(f0_ready[0], f0_ready[1])

        OP("v", lambda e: e.memset(ln2f[:, 0:512], 0.0),
           [("pv", crep_v[0]), ("pg", g_f1pg[0])], "pv")
        OP("v", lambda e: e.memset(ln2f[:, 512:1024], 0.0), [("pv", P.n("pv"))], "pv")
        vM1 = P.n("pv")
        g_f1 = {}
        for c in range(32):
            slw = c % 4
            if c >= 4:
                w = [("pt", g_f1[c - 4][0])]
            elif g_prev_f2:
                w = [("pt", g_prev_f2[24 + 2 * c]), ("pt", g_prev_f2[25 + 2 * c])]
            else:
                w = [("pt", P.n("pt"))]
                if c == 2:
                    w += prep_stoks
            t1 = DMA("y", lambda e, c=c, slw=slw: e.dma_start(
                wsA[:, slw * 2048: slw * 2048 + 1024], s_wg[c * 128:(c + 1) * 128, :]), w)
            t2 = DMA("y", lambda e, c=c, slw=slw: e.dma_start(
                wsA[:, slw * 2048 + 1024: slw * 2048 + 2048], s_wu[c * 128:(c + 1) * 128, :]), ())
            pb = pS if c % 2 == 0 else pS2
            w0 = [t1, t2]
            if c >= 2:
                w0.append(("pv", g_f1[c - 2][1]))
            else:
                w0.append(("pv", vM1))
            for wsel in range(2):
                for hc in range(8):
                    mm(pb[:, wsel * 512:(wsel + 1) * 512],
                       wsA[:, slw * 2048 + wsel * 1024 + hc * 128:
                           slw * 2048 + wsel * 1024 + (hc + 1) * 128],
                       xqf(hc), hc == 0, hc == 7, w0 if (wsel == 0 and hc == 0) else ())
            tf1 = P.n("pt")
            wcm = [("pt", tf1)]
            if (c - 1, "vt") in g_f1:
                wcm.append(("pv", g_f1[(c - 1, "vt")]))
            OP("v", lambda e, pb=pb: e.tensor_tensor(jF32[:, 0:512], pb[:, 0:512],
                                                     crepf[:, 0:512], ALU.mult),
               wcm, "pv")
            vg1 = P.n("pv")
            OP("s", lambda e: e.activation(jF32[:, 0:512], jF32[:, 0:512], AF.Silu),
               [("pv", vg1)], "ps")
            sg2 = P.n("ps")
            OP("v", lambda e, pb=pb: e.tensor_tensor(jF32[:, 512:1024], pb[:, 512:1024],
                                                     crepf[:, 512:1024], ALU.mult),
               [("pt", tf1)], "pv")
            wtw = [("ps", sg2), ("pv", P.n("pv"))]
            if c < 4:
                wtw += list(g_strep_w)
            OP("v", lambda e, c=c: e.tensor_tensor(tdst(c), jF32[:, 0:512],
                                                   jF32[:, 512:1024], ALU.mult),
               wtw, "pv")
            vt = P.n("pv")
            sqb = lambda c: ln2f[:, 1024 + (c % 2) * 512: 1536 + (c % 2) * 512]
            wsq = [("pv", vt)]
            if (c - 2, "vmax") in g_f1:
                wsq.append(("pv", g_f1[(c - 2, "vmax")]))
            OP("s", lambda e, c=c: e.activation(sqb(c), tdst(c), AF.Square),
               wsq, "ps")
            OP("v", lambda e, c=c: e.tensor_tensor(ln2f[:, 0:512], ln2f[:, 0:512],
                                                   sqb(c), ALU.max),
               [("ps", P.n("ps"))], "pv")
            g_f1[(c, "vmax")] = P.n("pv")
            g_f1[(c, "vt")] = P.n("pv")
            g_f1[c] = (P.n("pt"), P.n("pv"))
        g_xqf_t = P.n("pt")
        g_f1pg[0] = P.n("pg")
        f1v = P.n("pv")
        OP("v", lambda e: e.tensor_copy(junkA[:, 0:512], ln2f[:, 0:512]),
           [("pv", f1v)], "pv")
        vMb = P.n("pv")
        for j in range(4):
            mm(pT[:, j * 128:(j + 1) * 128], junkA[:, j * 128:j * 128 + 128],
               ident[:], True, True, [("pv", vMb)] if j == 0 else ())
        tM = P.n("pt")
        OP("v", lambda e: e.reduce_max(smal[:, 100:104], pT[:, 0:512].rearrange(
            "p (a x) -> p a x", x=128), AX.X), [("pt", tM)], "pv")
        OP("v", lambda e: e.tensor_scalar(smal[:, 104:108], smal[:, 100:104], 1e-10, None,
                                          ALU.max), [("pv", P.n("pv"))], "pv")
        rsq(smal[:, 108:112], smal[:, 104:108], [("pv", P.n("pv"))])
        sMr = P.n("ps")
        OP("v", lambda e: e.tensor_scalar(smal[:, 112:116], smal[:, 108:112], 127.0, None,
                                          ALU.mult), [("ps", sMr)], "pv")
        v127 = P.n("pv")
        sttoks = []
        for j in range(4):
            sttoks.append(DMA("g", lambda e, j=j: e.dma_start(
                rrow[0:1, j * 128:(j + 1) * 128], smal[:, 112 + j:113 + j]),
                [("pv", v127)] + (list(g_strep_w) if j == 0 else []), queue="g"))
        OP("v", lambda e: e.reciprocal(smal[:, 116:120], smal[:, 104:108]), (), "pv")
        OP("v", lambda e: e.tensor_tensor(smal[:, 116:120], smal[:, 116:120],
                                          kapp[:].broadcast_to([128, 4]), ALU.mult),
           [("pv", P.n("pv"))], "pv")
        vst0 = P.n("pv")
        OP("v", lambda e: e.tensor_copy(junkA[0:1, 0:512], rrow[0:1, 0:512]), sttoks, "pv")
        OP("v", lambda e: e.tensor_sub(junkA[0:1, 512:1024], rrow[0:1, 0:512],
                                       junkA[0:1, 0:512]), [("pv", P.n("pv"))], "pv")
        vhl = P.n("pv")
        for j in range(4):
            mm(pU[:, j * 128:(j + 1) * 128], onesb[0:1, :],
               junkA[0:1, j * 128:(j + 1) * 128], True, False,
               [("pv", vhl)] if j == 0 else ())
            mm(pU[:, j * 128:(j + 1) * 128], onesb[0:1, :],
               junkA[0:1, 512 + j * 128:512 + (j + 1) * 128], False, True, ())
        tstr = P.n("pt")
        OP("v", lambda e: e.tensor_copy(crepf[:, 1024:1536], pU[:, 0:512]), [("pt", tstr)], "pv")
        v_strep = P.n("pv")

        g_half_done = {}
        g_wd2 = {}
        have_next = b + 1 < NB
        st_new = {}

        def emit_epi(half, tdn, st_new, have_next):
            epi_v = []
            rel = 0
            for j in range(2):
                tt = half * 2 + j
                pb = pS if j == 0 else pS2
                sl = j % 2
                scr = wsB[:, sl * 1024:(sl + 1) * 1024]
                wx2 = [("pt", tdn)]
                if have_next:
                    wx2 += list(st_new[("ld", half * 2 + sl)])
                x1tok2 = DMA("y", lambda e, sl=sl, src=s_x1[(b * 4 + tt) * 128:
                                                            (b * 4 + tt + 1) * 128, :]:
                             e.dma_start(xw(sl), src), wx2)
                OP("v", lambda e, pb=pb, scr=scr: e.tensor_copy(scr, pb[:]),
                   [("pt", tdn)], "pv")
                rel = P.n("pv")
                OP("v", lambda e, scr=scr: e.bn_stats(bn6[:, 0:6], scr[:, 0:512]), (), "pv")
                OP("v", lambda e, scr=scr: e.bn_stats(bn6[:, 6:12], scr[:, 512:1024]),
                   [("pv", P.n("pv"))], "pv")
                OP("v", lambda e: e.bn_aggr(mv2[:, 0:2], bn6[:, 0:12]), [("pv", P.n("pv"))], "pv")
                OP("v", lambda e, tt=tt: e.tensor_tensor(mv2[:, 2:3], mv2[:, 1:2],
                                                         smal[:, 116 + tt:117 + tt], ALU.add),
                   [("pv", P.n("pv"))], "pv")
                rsq(mv2[:, 3:4], mv2[:, 2:3], [("pv", P.n("pv"))])
                srj = P.n("ps")
                OP("v", lambda e, scr=scr: e.tensor_scalar(jF32[:], scr, mv2[:, 0:1],
                                                           mv2[:, 3:4], ALU.subtract, ALU.mult),
                   [("ps", srj)], "pv")
                slo = tt % 2
                OP("v", lambda e, slo=slo, sl=sl: e.tensor_tensor(
                    outst(slo), xw(sl), jF32[:], ALU.add),
                   [("pv", P.n("pv")), x1tok2] + g_ost[slo], "pv")
                vfin = P.n("pv")
                g_ost[slo] = [DMA("g", lambda e, b=b, tt=tt, slo=slo: e.dma_start(
                    out_hbm[(b * 4 + tt) * 128:(b * 4 + tt + 1) * 128, :], outst(slo)),
                    [("pv", vfin)], queue="pg" if False else "g")]
                epi_v.append(("pv", P.n("pv")))
            return epi_v, rel

        g_q2 = {}
        g_qs = {}
        for p in range(16):
            dy2 = []
            for i2 in (2 * p, 2 * p + 1):
                slw = i2 % 8
                if i2 >= 8:
                    w = [("pt", g_wd2[i2 - 8])]
                else:
                    w = [("pt", g_f1[28 + i2 // 2][0])]
                dy2.append(DMA("y", lambda e, i2=i2, slw=slw: e.dma_start(
                    wsA[:, slw * 1024:(slw + 1) * 1024],
                    s_wd[i2 * 128:(i2 + 1) * 128, :]), w))
            w1 = [("pv", v_strep)]
            if p - 1 in g_qs:
                w1.append(("ps", g_qs[p - 1]))
            OP("v", lambda e, p=p: e.tensor_tensor(
                jF32[:].rearrange("q (a x) -> q a x", x=512),
                tdst2(p).rearrange("q (a x) -> q a x", x=512),
                crepf[:, 1024:1536].unsqueeze(1).broadcast_to([128, 2, 512]),
                ALU.mult), w1, "pv")
            vq1 = P.n("pv")
            wqs = [("pv", vq1)]
            if p - 1 in g_q2:
                wqs.append(("pv", g_q2[p - 1]))
            OP("s", lambda e: e.activation(i16d[0][:], jF32[:], AF.Identity,
                                           bias=shiftp[:]), wqs, "ps")
            g_qs[p] = P.n("ps")
            OP("v", lambda e, p=p: e.tensor_scalar(
                big[:, VB0 + p * 1024: VB0 + (p + 1) * 1024], i16d[0][:],
                nshift2[:], None, ALU.add), [("ps", g_qs[p])], "pv")
            g_q2[p] = P.n("pv")
            for i2 in (2 * p, 2 * p + 1):
                slw = i2 % 8
                wmm = [dy2[i2 - 2 * p], ("pv", g_q2[p])]
                for j in range(2):
                    for oc in range(2):
                        mm((pS if j == 0 else pS2)[:, oc * 512:(oc + 1) * 512],
                           tblk(i2)[:, j * 128:(j + 1) * 128],
                           wsA[:, slw * 1024 + oc * 512: slw * 1024 + (oc + 1) * 512],
                           i2 == 0, i2 == 31, wmm if (j == 0 and oc == 0) else ())
                g_wd2[i2] = P.n("pt")
            if have_next and p == 6:
                emit_F0_dve(b + 1, 0, st_new, pw=list(g_xw_free))
            elif have_next and p == 10:
                emit_F0_dve(b + 1, 1, st_new)
        tdn = P.n("pt")
        g_wd2_h0 = dict(g_wd2)
        g_half_done[0], rel0 = emit_epi(0, tdn, st_new, have_next)

        for i2 in range(32):
            slw = i2 % 8
            if i2 >= 8:
                w = [("pt", g_wd2[i2 - 8])]
            else:
                w = [("pt", g_wd2_h0[24 + i2])]
            dyd = DMA("y", lambda e, i2=i2, slw=slw: e.dma_start(
                wsA[:, slw * 1024:(slw + 1) * 1024], s_wd[i2 * 128:(i2 + 1) * 128, :]), w)
            wmm = [dyd]
            if i2 == 0:
                wmm.append(("pv", rel0))
            for j in range(2):
                tt = 2 + j
                for oc in range(2):
                    mm((pS if j == 0 else pS2)[:, oc * 512:(oc + 1) * 512],
                       tblk(i2)[:, tt * 128:(tt + 1) * 128],
                       wsA[:, slw * 1024 + oc * 512: slw * 1024 + (oc + 1) * 512],
                       i2 == 0, i2 == 31, wmm if (j == 0 and oc == 0) else ())
            g_wd2[i2] = P.n("pt")
            if have_next:
                if i2 == 2:
                    emit_F0_pe(b + 1, 0, st_new)
                elif i2 == 6:
                    emit_F0_pe(b + 1, 1, st_new)
                elif i2 == 8:
                    emit_F0_dve(b + 1, 2, st_new, pw=list(g_half_done[0]))
                elif i2 == 13:
                    emit_F0_dve(b + 1, 3, st_new)
                elif i2 == 14:
                    emit_F0_pe(b + 1, 2, st_new)
                elif i2 == 22:
                    emit_F0_pe(b + 1, 3, st_new)
                elif i2 == 24:
                    emit_crep(st_new[("ev", 3)], st_new[("vc", 3)])
        tdn = P.n("pt")
        g_half_done[1], _rel1 = emit_epi(1, tdn, st_new, have_next)
        if have_next:
            f0_ready = (st_new[("ev", 3)], st_new[("vc", 3)])
            st_cur = st_new
        g_prev_f2 = dict(g_wd2)
        g_strep_w = [("pv", P.n("pv"))]
        g_xw_free = list(g_half_done[1])

    dma_finals = [(sn, P.n(sn)) for sn in semnames if sn.startswith("d") and P.n(sn) > 0]

    eng_ops = {e: [] for e in ("y", "s", "v", "t", "g")}
    for (e, fn, waits, incs) in P.ops:
        eng_ops[e].append((fn, waits, incs))

    with es, nc.Block() as block:
        def runner(name):
            def run(eng):
                for (fn, waits, incs) in eng_ops[name]:
                    for (sn, val) in waits:
                        eng.wait_ge(sems[sn], val)
                    inst = fn(eng)
                    for (sn, k) in incs:
                        inst.then_inc(sems[sn], k)
                if name == "y":
                    for sn, v in dma_finals:
                        eng.wait_ge(sems[sn], v)
            return run
        block.sync(runner("y"))
        block.scalar(runner("s"))
        block.vector(runner("v"))
        block.tensor(runner("t"))
        block.gpsimd(runner("g"))

    return nc


_NC_CACHE = []
TAP = {}


def kernel(**inputs):
    x = np.ascontiguousarray(np.asarray(inputs["L0_attn_inputs"], dtype=np.float32))
    shards = x.reshape(NC, SEQ * L, D)
    base = {
        "wqk": np.ascontiguousarray(np.asarray(inputs["L0_attn_W_qk"], np.float32)),
        "wv": np.ascontiguousarray(np.asarray(inputs["L0_attn_W_v"], np.float32)),
        "wout": np.ascontiguousarray(np.asarray(inputs["L0_attn_W_out"], np.float32)),
        "qs": np.asarray(inputs["L0_attn_q_scale"], np.float32).reshape(1, HD),
        "qb": np.asarray(inputs["L0_attn_q_bias"], np.float32).reshape(1, HD),
        "ks": np.asarray(inputs["L0_attn_k_scale"], np.float32).reshape(1, HD),
        "kb": np.asarray(inputs["L0_attn_k_bias"], np.float32).reshape(1, HD),
        "wg": np.ascontiguousarray(np.asarray(inputs["L0_ffn_w_gate"], np.float32)),
        "wu": np.ascontiguousarray(np.asarray(inputs["L0_ffn_w_up"], np.float32)),
        "wd": np.ascontiguousarray(np.asarray(inputs["L0_ffn_w_down"], np.float32)),
    }
    nc = build()
    in_maps = [dict(base, x_in=np.ascontiguousarray(shards[c])) for c in range(NC)]
    res = run_bass_kernel_spmd(nc, in_maps, core_ids=list(range(NC)))
    out = np.stack([res.results[c]["out"] for c in range(NC)], axis=0)
    return out.reshape(B, L, D).astype(np.float32)


# version 31 (final)
# speedup vs baseline: 1.0852x; 1.0031x over previous
import numpy as np
from contextlib import ExitStack

import concourse.bass as bass
import concourse.mybir as mybir
from concourse.bass_utils import run_bass_kernel_spmd

F32 = mybir.dt.float32
BF16 = mybir.dt.bfloat16
I16 = mybir.dt.int16
I8 = mybir.dt.int8
I32 = mybir.dt.int32
AF = mybir.ActivationFunctionType
ALU = mybir.AluOpType
AX = mybir.AxisListType

B, L, D, H, HD, INNER = 16, 2048, 1024, 16, 64, 4096
NC = 8
SEQ = 2
NT = L // 128
NG = L // 512
NP = H // 2
TB = 512
NB = SEQ * L // TB
EPS_QK = float(1024 * 1e-6)
EPS_LN = 1e-5
EPS_OP = float(1024 * 1e-5)

VB0 = 0
LF0 = 16640
QB0 = 33024
KB0 = 35072
R10 = 37120
OS0 = 39168
ZB0 = 39680
JK0 = 41728
BIGN = 42752
WV0 = QB0
XQ0 = LF0
L2T0 = LF0 + 4096
XT0 = LF0 + 6144
M10 = LF0 + 8192
CI0 = LF0 + 8704
CR0 = LF0 + 9728
JF0 = LF0 + 11264
OT0 = LF0 + 12288
PW0 = 0
PO0 = 8192
PT0 = 12288


class Plan:
    def __init__(self):
        self.ops = []
        self.cnt = {}
        self.pending = {}

    def n(self, sem):
        return self.cnt.get(sem, 0)

    def bar(self, engines, sems_):
        snap = [(s, self.n(s)) for s in sems_ if self.n(s) > 0]
        for e in engines:
            self.pending.setdefault(e, []).extend(snap)

    def op(self, eng, fn, waits=(), inc=None, k=1):
        incs = ()
        if inc is not None:
            self.cnt[inc] = self.cnt.get(inc, 0) + k
            incs = ((inc, k),)
        w = list(self.pending.pop(eng, []))
        w += [(s, v) for (s, v) in waits if v > 0]
        best = {}
        for s, v in w:
            best[s] = max(best.get(s, 0), v)
        self.ops.append((eng, fn, tuple(best.items()), incs))
        return self.cnt.get(inc, 0)

    def dma(self, eng, fn, waits=(), queue="y", nrot=8):
        key = ("dmaidx", queue)
        i = self.cnt.get(key, 0)
        self.cnt[key] = i + 1
        semn = "d%s%d" % (queue, i % nrot)
        v = self.n(semn) + 16
        w = list(waits)
        if v > 16:
            w.append((semn, v - 16))
        self.op(eng, fn, w, inc=semn, k=16)
        return (semn, v)


def build():
    nc = bass.Bass("TRN2", target_bir_lowering=False)
    es = ExitStack()

    dtn = lambda n, s, d, k: nc.dram_tensor(n, s, d, kind=k).ap()
    x_in = dtn("x_in", [SEQ * L, D], F32, "ExternalInput")
    wqk = dtn("wqk", [D, 2 * D], F32, "ExternalInput")
    wv = dtn("wv", [D, D], F32, "ExternalInput")
    wout = dtn("wout", [D, D], F32, "ExternalInput")
    qs_in = dtn("qs", [1, HD], F32, "ExternalInput")
    qb_in = dtn("qb", [1, HD], F32, "ExternalInput")
    ks_in = dtn("ks", [1, HD], F32, "ExternalInput")
    kb_in = dtn("kb", [1, HD], F32, "ExternalInput")
    wg_in = dtn("wg", [INNER, D], F32, "ExternalInput")
    wu_in = dtn("wu", [INNER, D], F32, "ExternalInput")
    wd_in = dtn("wd", [D, INNER], F32, "ExternalInput")
    out_hbm = dtn("out", [SEQ * L, D], F32, "ExternalOutput")
    s_wqk = dtn("s_wqk", [D, 2 * D], BF16, "Internal")
    s_wg = dtn("s_wg", [INNER, D], BF16, "Internal")
    s_wu = dtn("s_wu", [INNER, D], BF16, "Internal")
    s_wd = dtn("s_wd", [INNER, D], BF16, "Internal")
    s_x1 = dtn("s_x1", [SEQ * L, D], F32, "Internal")

    sb = lambda n, shape, d: es.enter_context(nc.sbuf_tensor(n, shape, d))
    ident = sb("ident", [128, 128], BF16)
    onesb = sb("onesb", [128, 128], BF16)
    opair = sb("opair", [128, 128], BF16)
    cst = sb("cst", [128, 16], BF16)
    ltri = sb("ltri", [128, 128], BF16)
    masks = sb("masks", [128, 1024], BF16)
    iot = sb("iot", [128, 512], I32)
    iotb = iot.bitcast(BF16)
    sQ = sb("sQ", [128, 1], F32); bQ = sb("bQ", [128, 1], F32)
    sK = sb("sK", [128, 1], F32); bK = sb("bK", [128, 1], F32)
    shiftp = sb("shiftp", [128, 1], F32)
    mbias = sb("mbias", [128, 1], F32)
    nshift2 = sb("nshift2", [128, 1], F32)
    shiftW = sb("shiftW", [128, 1], F32)
    nshiftW = sb("nshiftW", [128, 1], F32)
    iswg = sb("iswg", [128, 1], F32)
    iswu = sb("iswu", [128, 1], F32)
    kapp = sb("kapp", [128, 1], F32)
    swq = {"wg": sb("swgq", [128, 1], F32), "wu": sb("swuq", [128, 1], F32),
           "wd": sb("swdq", [128, 1], F32)}
    smal = sb("smal", [128, 128], F32)
    statq = sb("statq", [128, 32], F32)
    bn6 = sb("bn6", [128, 16], F32)
    mv2 = sb("mv2", [128, 8], F32)
    big = sb("big", [128, BIGN], BF16)
    wsA = sb("wsA", [128, 8192], BF16)
    wsB = sb("wsB", [128, 4096], F32)
    _i16 = sb("i16a", [128, 1024], I16)
    i16d = (_i16, _i16)
    obufT = sb("obufT", [128, 16384], BF16)
    wqk2 = sb("wqk2", [128, 4096], BF16)
    qkbuf2 = sb("qkbuf2", [128, 8192], BF16)
    jF32 = sb("jF32", [128, 1024], F32)
    ln2f = sb("ln2f", [128, 2048], F32)
    crepf = sb("crepf", [128, 1536], F32)
    rrow = sb("rrow", [128, 1536], F32)
    lstag = rrow[:, 1024:1536]
    l_all = sb("l_all", [128, 512], F32)
    rlb = sb("rlb", [128, 512], BF16)
    rl2 = sb("rl2", [128, 512], BF16)

    ps_ = lambda n, shape: es.enter_context(nc.psum_tensor(n, shape, F32))
    pS = ps_("pS", [128, 1024])
    pS2 = ps_("pS2", [128, 1024])
    pT = ps_("pT", [128, 1024])
    pU = ps_("pU", [128, 1024])

    semnames = ["pt", "pv", "ps", "pg"]
    for q in ("y", "s", "g"):
        semnames += ["d%s%d" % (q, i) for i in range(8)]
    sems = {s: es.enter_context(nc.semaphore(s)) for s in semnames}

    P = Plan()
    OP = P.op
    DMA = P.dma
    ALL_ENG = ("y", "s", "v", "t", "g")
    COMPUTE_SEMS = tuple(semnames)

    def BAR():
        P.bar(ALL_ENG, COMPUTE_SEMS)

    def mm(out_ap, lhsT, rhs, start, stop, waits=(), tp=None):
        return OP("t", lambda e, o=out_ap, l=lhsT, r=rhs, a=start, b=stop, t_=tp:
                  e.matmul(o, l, r, start=a, stop=b, tile_position=t_,
                           skip_group_check=True),
                  waits, "pt")

    def rsq(dst, src, waits, scale_half=True):
        OP("s", lambda e, d=dst, s_=src: e.activation(d, s_, AF.Ln), waits, "ps")
        sc = -0.5 if scale_half else -1.0
        OP("s", lambda e, d=dst, sc=sc: e.activation(d, d, AF.Exp, scale=sc),
           [("ps", P.n("ps"))], "ps")

    tap_outs = []

    def tap(name, ap_fn, shape, dtype=BF16):
        if not TAP.get(name):
            return
        t = nc.dram_tensor("tap_" + name, shape, dtype, kind="ExternalOutput").ap()
        P.bar(("y",), COMPUTE_SEMS)
        tok = DMA("y", lambda e, t=t, ap_fn=ap_fn: e.dma_start(t, ap_fn()), ())
        for e_ in ALL_ENG:
            P.pending.setdefault(e_, []).append(tok)
        tap_outs.append(name)

    bigf = lambda o, n: big[:, o:o + 2 * n].bitcast(F32)
    vbuf = lambda c: big[:, VB0 + c * 1040: VB0 + (c + 1) * 1040]
    ln1f = lambda ic: big[:, LF0 + ic * 2048: LF0 + (ic + 1) * 2048]
    qbp = qkbuf2[:, 0:4096]
    kbp = qkbuf2[:, 4096:8192]
    wqkp = big[:, R10:R10 + 2048]
    pbuf = lambda j: big[:, R10 + j * 1024: R10 + (j + 1) * 1024]
    ostag = big[:, OS0:OS0 + 512]
    lnt = lambda j: big[:, ZB0 + j * 1024: ZB0 + (j + 1) * 1024]
    junkA = big[:, JK0:JK0 + 1024]
    wvo = lambda ic: big[:, WV0 + ic * 1024: WV0 + (ic + 1) * 1024]
    tblk = lambda c: big[:, VB0 + c * 512: VB0 + (c + 1) * 512]
    xqf = lambda hc: big[:, XQ0 + hc * 512: XQ0 + (hc + 1) * 512]
    ln2t = lambda j: big[:, L2T0 + j * 1024: L2T0 + (j + 1) * 1024]
    xqtm = lambda j: big[:, XT0 + j * 1024: XT0 + (j + 1) * 1024]
    M1 = big[:, M10:M10 + 512]
    cin = lambda j: big[:, CI0 + j * 128: CI0 + (j + 1) * 128]
    cstage = big[:, CI0 + 512:CI0 + 640]
    stin = big[:, CI0 + 640:CI0 + 768]
    ststage = big[:, CI0 + 768:CI0 + 896]
    crep = lambda j: big[:, CR0 + j * 512: CR0 + (j + 1) * 512]
    junkF = big[:, JF0:JF0 + 1024]
    outst = lambda j: big[:, OT0 + j * 2048: OT0 + (j + 1) * 2048].bitcast(F32)

    obuf = lambda s, p: obufT[:, p * 2048:(p + 1) * 2048]

    OP("g", lambda e: e.iota(iot[:], [[-1, 512]], base=0, channel_multiplier=1), (), "pg")
    gi = P.n("pg")
    OP("v", lambda e: e.tensor_scalar(ident[:], iot[:, 0:128], 0.0, None, ALU.is_equal),
       [("pg", gi)], "pv")
    OP("v", lambda e: e.tensor_scalar(ltri[:], iot[:, 0:128], 0.0, None, ALU.is_le),
       [("pg", gi)], "pv")
    OP("v", lambda e: e.tensor_scalar(masks[:, 512:1024], iot[:], 1.0, -1e30,
                                      ALU.is_equal, ALU.mult), [("pg", gi)], "pv")
    OP("v", lambda e: e.memset(masks[:, 0:512], 0.0), (), "pv")
    OP("v", lambda e: e.memset(masks[0:1, 0:512], -1e30), [("pv", P.n("pv"))], "pv")
    OP("v", lambda e: e.memset(onesb[:], 1.0), (), "pv")
    OP("v", lambda e: e.memset(opair[:], 0.0), (), "pv")
    OP("v", lambda e: e.memset(opair[0:1, 0:64], 1.0), [("pv", P.n("pv"))], "pv")
    OP("v", lambda e: e.memset(big[0:1, 0:64], 0.0), (), "pv")
    OP("v", lambda e: e.memset(big[0:1, 64:128], 1.0), [("pv", P.n("pv"))], "pv")
    vop = P.n("pv")
    DMA("y", lambda e: e.dma_start(opair[1:2, 0:128], big[0:1, 0:128]),
        [("pv", vop)])
    OP("v", lambda e: e.memset(smal[:, 0:1], 1.5), (), "pv")
    OP("v", lambda e: e.tensor_scalar(i16d[0][:, 0:1], smal[:, 0:1], 1.0, None, ALU.mult),
       [("pv", P.n("pv"))], "pv")
    OP("v", lambda e: e.tensor_scalar(shiftp[:], i16d[0][:, 0:1], -0.5, 1.0, ALU.mult, ALU.add),
       [("pv", P.n("pv"))], "pv")
    OP("v", lambda e: e.tensor_scalar(nshiftW[:], shiftp[:], -32.0, None, ALU.mult),
       [("pv", P.n("pv"))], "pv")
    OP("v", lambda e: e.tensor_scalar(shiftW[:], shiftp[:], 33.0, None, ALU.mult),
       [("pv", P.n("pv"))], "pv")
    OP("v", lambda e: e.tensor_scalar(nshift2[:], shiftp[:], -512.0, None, ALU.mult),
       [("pv", P.n("pv"))], "pv")
    OP("v", lambda e: e.tensor_scalar(shiftp[:], shiftp[:], 513.0, None, ALU.mult),
       [("pv", P.n("pv"))], "pv")
    toks = []
    for r, src in enumerate((qs_in, qb_in, ks_in, kb_in)):
        toks.append(DMA("y", lambda e, r=r, src=src: e.dma_start(smal[r:r + 1, 64:128], src)))
    for dst, src in ((sQ, qs_in), (bQ, qb_in), (sK, ks_in), (bK, kb_in)):
        toks.append(DMA("y", lambda e, d=dst, s_=src: e.dma_start(d[0:64, 0:1], s_.transpose([1, 0]))))
        toks.append(DMA("y", lambda e, d=dst, s_=src: e.dma_start(d[64:128, 0:1], s_.transpose([1, 0]))))
    OP("v", lambda e: e.reduce_max(smal[0:4, 1:2], smal[0:4, 64:128], AX.X,
                                   apply_absolute_value=True), toks, "pv")
    OP("v", lambda e: e.tensor_copy(cst[0:4, 0:1], smal[0:4, 1:2]), [("pv", P.n("pv"))], "pv")
    mm(pU[0:1, 0:1], cst[0:4, 0:1], onesb[0:4, 0:1], True, True, [("pv", P.n("pv"))])
    OP("s", lambda e: e.activation(smal[0:1, 2:3], pU[0:1, 0:1], AF.Square, scale=0.5),
       [("pt", P.n("pt"))], "ps")
    OP("v", lambda e: e.tensor_scalar(cst[0:1, 1:2], smal[0:1, 2:3], -1.0, None, ALU.mult),
       [("ps", P.n("ps"))], "pv")
    mm(pU[:, 1:2], onesb[0:1, :], cst[0:1, 1:2], True, True, [("pv", P.n("pv"))])
    OP("v", lambda e: e.tensor_copy(mbias[:], pU[:, 1:2]), [("pt", P.n("pt"))], "pv")
    BAR()

    pwf = lambda sl: bigf(PW0 + sl * 4096, 2048)
    pwo = lambda sl: big[:, PO0 + sl * 2048: PO0 + (sl + 1) * 2048]
    g_rd = {}
    g_st = {}
    for ic in range(8):
        sl = ic % 2
        w = [("pv", g_rd[ic - 2])] if ic >= 2 else []
        tok = DMA("y", lambda e, ic=ic, sl=sl: e.dma_start(pwf(sl),
                                                           wqk[ic * 128:(ic + 1) * 128, :]), w)
        OP("v", lambda e, sl=sl: e.reduce_sum(
            smal[:, 32:64], pwf(sl).rearrange("p (b j) -> p b j", j=64), AX.X),
           [tok, ("pv", P.n("pv"))], "pv")
        OP("v", lambda e: e.tensor_scalar(smal[:, 32:64], smal[:, 32:64], 1.0 / 64, None,
                                          ALU.mult), [("pv", P.n("pv"))], "pv")
        vme = P.n("pv")
        wst = [g_st[ic - 2]] if ic >= 2 else []
        for e2 in range(2):
            for half in range(2):
                def fprm(e, sl=sl, e2=e2, half=half):
                    i_ap = pwf(sl).rearrange("p (pp e h j) -> p pp e h j",
                                             e=2, h=2, j=64)[:, :, e2, half, :]
                    o_ap = pwo(sl).rearrange("p (pp h e j) -> p pp h e j",
                                             h=2, e=2, j=64)[:, :, half, e2, :]
                    m_ap = smal[:, 32:64].rearrange("p (pp e h) -> p pp e h",
                                                    e=2, h=2)[:, :, e2, half]
                    m_ap = m_ap.unsqueeze(2).broadcast_to([128, 8, 64])
                    return e.tensor_tensor(o_ap, i_ap, m_ap, ALU.subtract)
                OP("v", fprm, [("pv", vme)] + wst, "pv")
        g_rd[ic] = P.n("pv")
        g_st[ic] = DMA("g", lambda e, ic=ic, sl=sl: e.dma_start(
            s_wqk[ic * 128:(ic + 1) * 128, :], pwo(sl)), [("pv", P.n("pv"))], queue="g")
    wqk_stoks = dict(g_st)

    w1f = lambda sl: wsB[:, sl * 1024:(sl + 1) * 1024]
    p2o = lambda sl: wsA[:, sl * 1024:(sl + 1) * 1024]
    ptst = lambda sl: wsA[:, 4096 + sl * 1024: 4096 + (sl + 1) * 1024]
    dstb_wg = lambda rc, cc: s_wg[rc * 128:(rc + 1) * 128, :].rearrange(
        "p (a x) -> p a x", x=128)
    dstb_wu = lambda rc, cc: s_wu[rc * 128:(rc + 1) * 128, :].rearrange(
        "p (a x) -> p a x", x=128)
    dstb_wd = lambda rc, cc: s_wd[cc * 1024:(cc + 1) * 1024, rc * 128:(rc + 1) * 128].rearrange(
        "(a p) x -> p a x", p=128)
    wmeta = []
    for name, w_hbm, dstb in (("wg", wg_in, dstb_wg), ("wu", wu_in, dstb_wu),
                              ("wd", wd_in, dstb_wd)):
        R, C = w_hbm.shape
        wmeta.append((name, w_hbm, dstb, R // 128, C // 1024))
    gslot = [0]
    cons = {}

    def slot_load(src, eng, queue, extra=()):
        gi_ = gslot[0]
        sl = gi_ % 4
        w = list(extra)
        if gi_ >= 4:
            w.append(("pv", cons[gi_ - 4]))
        tok = DMA(eng, lambda e, sl=sl, src=src: e.dma_start(w1f(sl), src), w, queue=queue)
        gslot[0] = gi_ + 1
        return gi_, sl, tok

    pcol = {"wg": 64, "wu": 32, "wd": 0}
    for (name, w_hbm, dstb, nr, ncc) in wmeta:
        base = pcol[name]
        for idx in range(nr * ncc):
            rc, cc = divmod(idx, ncc)
            gi_, sl, tok = slot_load(
                w_hbm[rc * 128:(rc + 1) * 128, cc * 1024:(cc + 1) * 1024], "s", "s")
            OP("v", lambda e, sl=sl, co=base + idx: e.reduce_sum(
                smal[:, co:co + 1], w1f(sl), AX.X, apply_absolute_value=True), [tok], "pv")
            cons[gi_] = P.n("pv")
    for (name, w_hbm, dstb, nr, ncc) in wmeta:
        base = pcol[name]
        R, C = w_hbm.shape
        OP("v", lambda e, base=base: e.reduce_sum(smal[:, 96:97], smal[:, base:base + 32],
                                                  AX.X), [("pv", P.n("pv"))], "pv")
        tokg = DMA("g", lambda e: e.dma_start(rrow[0:1, 0:128], smal[:, 96:97]),
                   [("pv", P.n("pv"))], queue="g")
        OP("v", lambda e: e.reduce_sum(smal[0:1, 97:98], rrow[0:1, 0:128], AX.X),
           [tokg], "pv")
        OP("v", lambda e, sc=1.0 / (R * C): e.tensor_scalar(
            smal[0:1, 97:98], smal[0:1, 97:98], sc, 1e-5, ALU.mult, ALU.max),
           [("pv", P.n("pv"))], "pv")
        OP("v", lambda e: e.reciprocal(smal[0:1, 98:99], smal[0:1, 97:98]),
           [("pv", P.n("pv"))], "pv")
        OP("v", lambda e: e.tensor_copy(rrow[0:1, 0:128],
                                        smal[0:1, 98:99].broadcast_to([1, 128])),
           [("pv", P.n("pv"))], "pv")
        tokr = DMA("g", lambda e, d=swq[name]: e.dma_start(d[:], rrow[0:1, 0:128]),
                   [("pv", P.n("pv"))], queue="g")
        for e_ in ALL_ENG:
            P.pending.setdefault(e_, []).append(tokr)
        if name in ("wg", "wu"):
            dstc = iswg if name == "wg" else iswu
            OP("v", lambda e: e.tensor_scalar(smal[0:1, 99:100], smal[0:1, 97:98], 1.0 / 127,
                                              None, ALU.mult), [("pv", P.n("pv"))], "pv")
        else:
            dstc = kapp
            OP("v", lambda e: e.tensor_tensor(smal[0:1, 99:100], smal[0:1, 98:99],
                                              smal[0:1, 98:99], ALU.mult),
               [("pv", P.n("pv"))], "pv")
            OP("v", lambda e: e.tensor_scalar(smal[0:1, 99:100], smal[0:1, 99:100],
                                              1e-5 * 127.0 * 127.0, None, ALU.mult),
               [("pv", P.n("pv"))], "pv")
        OP("v", lambda e: e.tensor_copy(rrow[0:1, 0:128],
                                        smal[0:1, 99:100].broadcast_to([1, 128])),
           [("pv", P.n("pv"))], "pv")
        tokc = DMA("g", lambda e, d=dstc: e.dma_start(d[:], rrow[0:1, 0:128]),
                   [("pv", P.n("pv"))], queue="g")
        for e_ in ALL_ENG:
            P.pending.setdefault(e_, []).append(tokc)
    evacv = {}
    trv = {}
    stok = {}
    p2i = [0]
    for (name, w_hbm, dstb, nr, ncc) in wmeta:
        for idx in range(nr * ncc):
            rc, cc = divmod(idx, ncc)
            gi_, sl, tok = slot_load(
                w_hbm[rc * 128:(rc + 1) * 128, cc * 1024:(cc + 1) * 1024], "y", "y")
            i2_ = p2i[0]
            p2i[0] += 1
            OP("v", lambda e, sl=sl, n_=name: e.tensor_scalar(
                i16d[0][:], w1f(sl), swq[n_][:], shiftW[:], ALU.mult, ALU.add), [tok], "pv")
            wp2 = [("pv", P.n("pv"))]
            if (i2_ - 4) in trv:
                wp2.append(("pt", trv[i2_ - 4]))
            OP("v", lambda e, sl=sl: e.tensor_scalar(
                p2o(sl), i16d[0][:], nshiftW[:], -1.0, ALU.add, ALU.max),
               wp2, "pv")
            OP("v", lambda e, sl=sl: e.tensor_scalar(
                p2o(sl), p2o(sl), 1.0, None, ALU.min), [("pv", P.n("pv"))], "pv")
            vq2 = P.n("pv")
            cons[gi_] = vq2
            poff = (i2_ % 2) * 512
            sl2 = i2_ % 2
            for jb in range(2):
                pb = pT if jb == 0 else pU
                wtr = [("pv", vq2)] if jb == 0 else []
                if jb == 0 and (i2_ - 2) in evacv:
                    wtr.append(("ps", evacv[i2_ - 2]))
                for j in range(4):
                    jj = jb * 4 + j
                    mm(pb[:, poff + j * 128: poff + (j + 1) * 128],
                       p2o(sl)[:, jj * 128:(jj + 1) * 128],
                       ident[:], True, True, wtr if j == 0 else ())
                ttx = P.n("pt")
                ws_ = [("pt", ttx)]
                if jb == 0 and sl2 in stok:
                    ws_.append(stok[sl2])
                OP("s", lambda e, jb=jb, pb=pb, sl2=sl2, poff=poff: e.activation(
                    ptst(sl2)[:, jb * 512:(jb + 1) * 512], pb[:, poff:poff + 512], AF.Copy),
                   ws_, "ps")
            evacv[i2_] = P.n("ps")
            trv[i2_] = P.n("pt")

            def fst(e, rc=rc, cc=cc, sl2=sl2, dstb=dstb):
                return e.dma_start(dstb(rc, cc),
                                   ptst(sl2).rearrange("p (a x) -> p a x", x=128))
            stok[sl2] = DMA("g", fst, [("ps", evacv[i2_])], queue="g")
    g_prep_v = P.n("pv")
    g_prep_ps = P.n("ps")
    prep_stoks = [stok[0], stok[1]]
    tap("mbias", lambda: mbias[:, :], [128, 1], F32)
    tap("shiftp", lambda: shiftp[:, :], [128, 1], F32)
    tap("iswg", lambda: iswg[:, :], [128, 1], F32)
    tap("kapp", lambda: kapp[:, :], [128, 1], F32)

    for s in range(SEQ):
        xs = x_in[s * L:(s + 1) * L, :]
        wvtoks = []
        wv_guard = [("pt", P.n("pt")), ("ps", P.n("ps"))] + (prep_stoks if s == 0 else [])
        for ic in range(8):
            wvtoks.append(DMA("g", lambda e, ic=ic: e.dma_start(
                wsA[:, ic * 1024:(ic + 1) * 1024], wv[ic * 128:(ic + 1) * 128, :]),
                wv_guard if ic == 0 else (), queue="g"))
        g_a1 = {}
        a1ev = {}
        a1tr = {}
        g_v = {}

        def emit_vproj(tt):
            pb = pS if tt % 2 == 0 else pS2
            w0 = list(wvtoks) + [("ps", a1ev[tt])]
            if tt >= 2:
                w0.append(("ps", g_v[tt - 2]))
            for oc in range(2):
                for ic in range(8):
                    mm(pb[:, oc * 512:(oc + 1) * 512], ln1f(ic)[:, tt * 128:(tt + 1) * 128],
                       wsA[:, ic * 1024 + oc * 512: ic * 1024 + (oc + 1) * 512],
                       ic == 0, ic == 7, w0 if (oc == 0 and ic == 0) else ())
            tv = P.n("pt")
            wms = [wqk_stoks[6], wqk_stoks[7]] if (s == 0 and tt in (0, 8)) else []
            OP("v", lambda e, tt=tt: e.memset(
                vbuf(tt).rearrange("p (h x) -> p h x", x=65)[:, :, 64:65], 1.0), wms, "pv")
            vo = P.n("pv")
            for oc in range(2):
                def fvev(e, tt=tt, oc=oc, pb=pb):
                    dst = vbuf(tt).rearrange("p (h x) -> p h x", x=65)[
                        :, oc * 8:(oc + 1) * 8, 0:64]
                    return e.activation(dst, pb[:, oc * 512:(oc + 1) * 512].rearrange(
                        "p (h x) -> p h x", x=64), AF.Copy, scale=1.0 / 32)
                OP("s", fvev, [("pt", tv), ("pv", vo)], "ps")
            g_v[tt] = P.n("ps")

        for tt in range(NT):
            sl = tt % 2
            if tt >= 2:
                w = [("pv", g_a1[tt - 2])]
            else:
                w = [("pv", g_prep_v)] if s == 0 else []
            tok = DMA("y", lambda e, sl=sl, src=xs[tt * 128:(tt + 1) * 128, :]: e.dma_start(
                wsB[:, sl * 1024:(sl + 1) * 1024], src), w)
            OP("v", lambda e, sl=sl: e.bn_stats(bn6[:, 0:6], wsB[:, sl * 1024:sl * 1024 + 512]),
               [tok], "pv")
            OP("v", lambda e, sl=sl: e.bn_stats(bn6[:, 6:12],
                                                wsB[:, sl * 1024 + 512:(sl + 1) * 1024]),
               [("pv", P.n("pv"))], "pv")
            OP("v", lambda e: e.bn_aggr(mv2[:, 0:2], bn6[:, 0:12]), [("pv", P.n("pv"))], "pv")
            OP("v", lambda e: e.tensor_scalar(mv2[:, 2:3], mv2[:, 1:2], 1.0, EPS_LN,
                                              ALU.mult, ALU.add), [("pv", P.n("pv"))], "pv")
            rsq(mv2[:, 3:4], mv2[:, 2:3], [("pv", P.n("pv"))])
            sv = P.n("ps")
            wlnt = [("ps", sv)]
            if tt >= 2:
                wlnt.append(("pt", a1tr[tt - 2]))
            OP("v", lambda e, sl=sl: e.tensor_scalar(lnt(sl), wsB[:, sl * 1024:(sl + 1) * 1024],
                                                     mv2[:, 0:1], mv2[:, 3:4],
                                                     ALU.subtract, ALU.mult), wlnt, "pv")
            vln = P.n("pv")
            if tt >= 1:
                emit_vproj(tt - 1)
            for jb in range(2):
                pb = pT if jb == 0 else pU
                for j in range(4):
                    ic = jb * 4 + j
                    wtr0 = [("pv", vln)]
                    if s == 0 and tt == 0 and jb == 0:
                        wtr0.append(("ps", g_prep_ps))
                    mm(pb[:, j * 128:(j + 1) * 128], lnt(sl)[:, ic * 128:(ic + 1) * 128],
                       ident[:], True, True, wtr0 if ic == 0 else ())
                tv = P.n("pt")
                def fev(e, tt=tt, jb=jb, pb=pb):
                    reg = big[:, LF0 + jb * 8192: LF0 + (jb + 1) * 8192]
                    dst = reg.rearrange("p (a x) -> p a x", x=2048)[
                        :, :, tt * 128:(tt + 1) * 128]
                    return e.activation(dst, pb[:, 0:512].rearrange("p (a x) -> p a x", x=128),
                                        AF.Copy)
                OP("s", fev, [("pt", tv), ("ps", P.n("ps"))], "ps")
            g_a1[tt] = P.n("pv")
            a1ev[tt] = P.n("ps")
            a1tr[tt] = P.n("pt")
        emit_vproj(NT - 1)
        BAR()
        tap("ln1f_s%d" % s, lambda: big[:, LF0:LF0 + 16384], [128, 16384])
        tap("vbuf_s%d" % s, lambda: big[:, VB0:VB0 + 16640], [128, 16640])

        g_omove = []
        g_lmove = []
        ltoks = []
        g_attn_done = {}
        g_qk_ready = {}
        g_exp_ps = {(0, 0): 0, (0, 1): 0, (1, 0): 0, (1, 1): 0}
        g_pbuf_pt = {(0, 0): 0, (0, 1): 0, (1, 0): 0, (1, 1): 0}
        g_oacc = {0: (0, 0), 1: (0, 0)}
        g_projslot = [0]
        g_proj_state = {}

        def proj_partA(p, rnd):
            par = p % 2
            ttg = rnd // 2
            hf = rnd % 2
            if rnd == 0:
                toks = []
                for ic in range(8):
                    wld = [("pt", g_attn_done.get(p - 2, 0))] if ic == 0 else []
                    if p == 0:
                        wld.append(wqk_stoks[ic])
                    toks.append(DMA("y", lambda e, p=p, ic=ic, par=par: e.dma_start(
                        wqk2[:, par * 2048 + ic * 256: par * 2048 + (ic + 1) * 256],
                        s_wqk[ic * 128:(ic + 1) * 128, p * 256:(p + 1) * 256]), wld))
                g_qk_ready[(p, "wtok")] = toks
            toks = g_qk_ready[(p, "wtok")]
            slots = (pS[:, 0:256], pS[:, 512:768])
            w0 = list(toks) + [("pv", g_projslot[0])]
            if rnd == 0 and p >= 2:
                w0.append(("pt", g_attn_done.get(p - 2, 0)))
            for ic in range(8):
                for tj in range(2):
                    tt = ttg * 4 + hf * 2 + tj
                    mm(slots[tj], ln1f(ic)[:, tt * 128:(tt + 1) * 128],
                       wqk2[:, par * 2048 + ic * 256: par * 2048 + (ic + 1) * 256],
                       ic == 0, ic == 7, w0 if (ic == 0 and tj == 0) else ())
            tprj = P.n("pt")
            for tj in range(2):
                OP("s", lambda e, tj=tj, slots=slots: e.activation(
                    junkA[:, tj * 256:(tj + 1) * 256], slots[tj], AF.Square),
                   [("pt", tprj), ("ps", P.n("ps"))], "ps")
            g_proj_state[(p, rnd, "sq")] = P.n("ps")

        def proj_partA2(p, rnd):
            slots = (pS[:, 0:256], pS[:, 512:768])
            ssq = g_proj_state[(p, rnd, "sq")]
            OP("v", lambda e: e.reduce_sum(statq[:, 0:8],
                                           junkA[:, 0:512].rearrange("p (b j) -> p b j", j=64),
                                           AX.X), [("ps", ssq)], "pv")
            OP("v", lambda e: e.tensor_scalar(statq[:, 8:16], statq[:, 0:8], 1.0 / 64,
                                              EPS_QK, ALU.mult, ALU.add),
               [("pv", P.n("pv"))], "pv")
            rsq(statq[:, 16:24], statq[:, 8:16], [("pv", P.n("pv"))])
            srr = P.n("ps")
            zsl = rnd % 2
            for tj in range(2):
                OP("v", lambda e, tj=tj, zsl=zsl, slots=slots: e.tensor_tensor(
                    lnt(zsl)[:, tj * 256:(tj + 1) * 256].rearrange("p (b j) -> p b j", j=64),
                    slots[tj].rearrange("p (b j) -> p b j", j=64),
                    statq[:, 16 + tj * 4:16 + (tj + 1) * 4].unsqueeze(2).broadcast_to(
                        [128, 4, 64]), ALU.mult), [("ps", srr)], "pv")
            g_proj_state[(p, rnd)] = P.n("pv")

        def proj_partB(p, rnd):
            par = p % 2
            ttg = rnd // 2
            hf = rnd % 2
            vz1 = g_proj_state[(p, rnd)]
            zsl = rnd % 2
            for cls in range(2):
                for tj in range(2):
                    mm(pS[:, cls * 512 + tj * 128: cls * 512 + (tj + 1) * 128],
                       lnt(zsl)[:, (2 * tj + cls) * 128:(2 * tj + cls + 1) * 128],
                       ident[:], True, True,
                       [("pv", vz1)] if (cls == 0 and tj == 0) else ())
            ttr = P.n("pt")
            for cls in range(2):
                buf = qbp if cls == 0 else kbp
                sc = sQ if cls == 0 else sK
                bi = bQ if cls == 0 else bK
                OP("v", lambda e, buf=buf, sc=sc, bi=bi, par=par, ttg=ttg, hf=hf, cls=cls:
                   e.tensor_scalar(buf[:, par * 2048 + ttg * 512 + hf * 256:
                                       par * 2048 + ttg * 512 + (hf + 1) * 256],
                                   pS[:, cls * 512:cls * 512 + 256], sc[:], bi[:],
                                   ALU.mult, ALU.add), [("pt", ttr)], "pv")
            g_projslot[0] = P.n("pv")
            if rnd == 7:
                g_qk_ready[p] = P.n("pv")

        def emit_attn_pairgroup(p, g, hooks, cb0):
            par = p % 2
            nchunk = 4 * (g + 1)
            bufs = (pT, pS2)

            def emit_pv(ci):
                par2 = ci % 2
                for e2 in range(2):
                    h = 2 * p + e2
                    pOb = pU[:, e2 * 512:(e2 + 1) * 512]
                    wPV = [("ps", g_exp_ps[(e2, par2)])]
                    if ci == 0:
                        wPV.append(("pv", g_oacc[e2][0]))
                        wPV.append(("ps", g_oacc[e2][1]))
                    mm(pOb[0:65, :], vbuf(ci)[:, h * 65:(h + 1) * 65],
                       pbuf(e2)[:, par2 * 512:(par2 + 1) * 512],
                       ci == 0, ci == nchunk - 1, wPV)
                    g_pbuf_pt[(e2, par2)] = P.n("pt")

            for ci in range(nchunk):
                par2 = ci % 2
                off = 512 * g - 128 * ci
                diag = -384 <= off <= 0
                for e2 in range(2):
                    base = e2 * 64
                    tpos = (64, 0) if e2 == 1 else None
                    pb = bufs[e2]
                    wS = [("ps", g_exp_ps[(e2, par2)])]
                    if ci == 0 and e2 == 0:
                        wS.append(("pv", g_qk_ready[p]))
                    mm(pb[:, par2 * 512:(par2 + 1) * 512],
                       kbp[base:base + 64, par * 2048 + ci * 128:
                           par * 2048 + (ci + 1) * 128],
                       qbp[base:base + 64, par * 2048 + g * 512:
                           par * 2048 + (g + 1) * 512],
                       True, not diag, wS, tp=tpos)
                    if diag:
                        jm = (-off) // 128
                        mm(pb[:, par2 * 512:(par2 + 1) * 512], ltri[:],
                           masks[:, 512 - 128 * jm: 1024 - 128 * jm], False, True, ())
                    texp = P.n("pt")
                    OP("s", lambda e, pb=pb, e2=e2, par2=par2: e.activation(
                        pbuf(e2)[:, par2 * 512:(par2 + 1) * 512],
                        pb[:, par2 * 512:(par2 + 1) * 512],
                        AF.Exp, bias=mbias[:], scale=1.0 / 64),
                       [("pt", texp), ("pt", g_pbuf_pt[(e2, par2)])], "ps")
                    g_exp_ps[(e2, par2)] = P.n("ps")
                if ci > 0:
                    emit_pv(ci - 1)
                if ci % 2 == 1:
                    for fn in hooks.get(cb0 + ci // 2, ()):
                        fn()
            emit_pv(nchunk - 1)
            tPV = P.n("pt")
            for e2 in range(2):
                h = 2 * p + e2
                pOb = pU[:, e2 * 512:(e2 + 1) * 512]
                r = h * 4 + g
                if e2 == 0:
                    OP("v", lambda e, s=s, p=p, g=g, pOb=pOb: e.tensor_copy(
                        obuf(s, p)[0:64, g * 512:(g + 1) * 512], pOb[0:64, 0:512]),
                       [("pt", tPV)], "pv")
                else:
                    OP("v", lambda e, pOb=pOb: e.tensor_copy(ostag[0:64, :], pOb[0:64, 0:512]),
                       [("pt", tPV)] + g_omove, "pv")
                    g_omove[:] = [DMA("g", lambda e, s=s, p=p, g=g: e.dma_start(
                        obuf(s, p)[64:128, g * 512:(g + 1) * 512], ostag[0:64, :]),
                        [("pv", P.n("pv"))], queue="g")]
                OP("s", lambda e, pOb=pOb: e.copy(lstag[64:65, :], pOb[64:65, 0:512]),
                   [("pt", tPV)] + g_lmove, "ps")
                g_lmove[:] = [DMA("g", lambda e, r=r: e.dma_start(
                    l_all[r:r + 1, :], lstag[64:65, :]),
                    [("ps", P.n("ps"))], queue="g")]
                ltoks.extend(g_lmove)
                g_oacc[e2] = (P.n("pv"), P.n("ps"))
            g_attn_done[p] = P.n("pt")

        for rnd in range(8):
            proj_partA(0, rnd)
            proj_partA2(0, rnd)
            proj_partB(0, rnd)
        A_SLOTS = (0, 3, 5, 8, 10, 12, 15, 17)
        for p in range(NP):
            hooks = {}
            if p + 1 < NP:
                for rnd in range(8):
                    hooks.setdefault(A_SLOTS[rnd], []).append(
                        lambda rnd=rnd: proj_partA(p + 1, rnd))
                    hooks.setdefault(A_SLOTS[rnd] + 1, []).append(
                        lambda rnd=rnd: proj_partA2(p + 1, rnd))
                    hooks.setdefault(A_SLOTS[rnd] + 2, []).append(
                        lambda rnd=rnd: proj_partB(p + 1, rnd))
            cb0 = 0
            for g in range(NG):
                emit_attn_pairgroup(p, g, hooks, cb0)
                cb0 += 2 * (g + 1)
        BAR()
        OP("s", lambda e: e.activation(l_all[0:64, :], l_all[0:64, :], AF.Ln), ltoks, "ps")
        OP("s", lambda e: e.activation(rlb[0:64, :], l_all[0:64, :], AF.Exp, scale=-1.0),
           [("ps", P.n("ps"))], "ps")
        srl = P.n("ps")
        rlslot = (rl2[0:2, 0:512], iotb[0:2, 0:512], iotb[0:2, 512:1024])
        rl_mm = {}
        rl_mult = {}
        for idx in range(NP * NG):
            p, g = divmod(idx, NG)
            r0 = 8 * p + g
            sl3 = rlslot[idx % 3]
            wga = [("ps", srl)]
            if idx - 3 in rl_mm:
                wga.append(("pt", rl_mm[idx - 3]))
            tokA = DMA("g", lambda e, r0=r0, sl3=sl3: e.dma_start(sl3[0:1, :],
                                                                 rlb[r0:r0 + 1, :]),
                       wga, queue="g")
            tokB = DMA("g", lambda e, r0=r0, sl3=sl3: e.dma_start(sl3[1:2, :],
                                                                  rlb[r0 + 4:r0 + 5, :]),
                       (), queue="g")
            poff = (idx % 2) * 512
            wmm_ = [tokA, tokB]
            if idx - 2 in rl_mult:
                wmm_.append(("pv", rl_mult[idx - 2]))
            mm(pS[:, poff:poff + 512], opair[0:2, :], sl3[0:2, :], True, True, wmm_)
            rl_mm[idx] = P.n("pt")
            OP("v", lambda e, s=s, p=p, g=g, poff=poff: e.tensor_tensor(
                obuf(s, p)[:, g * 512:(g + 1) * 512], obuf(s, p)[:, g * 512:(g + 1) * 512],
                pS[:, poff:poff + 512], ALU.mult), [("pt", rl_mm[idx])], "pv")
            rl_mult[idx] = P.n("pv")
        BAR()
        tap("lall_s%d" % s, lambda: l_all[0:64, :], [64, 512], F32)
        tap("onorm_s%d" % s, lambda s=s: obuf(s, 0), [128, 2048])

        wotoks = []
        g_x1st = {0: [], 1: []}
        for ic in range(8):
            wotoks.append(DMA("g", lambda e, ic=ic: e.dma_start(
                wvo(ic), wout[ic * 128:(ic + 1) * 128, :]), (), queue="g"))
        g_a4 = {}
        for tt in range(NT):
            slx = tt % 2
            wx = [("pv", g_a4[tt - 2][1])] if tt >= 2 else []
            dyx = DMA("y", lambda e, slx=slx, src=xs[tt * 128:(tt + 1) * 128, :]: e.dma_start(
                wsB[:, slx * 1024:(slx + 1) * 1024], src), wx)
            pb = pS if tt % 2 == 0 else pS2
            w0 = list(wotoks)
            if tt >= 2:
                w0.append(("pv", g_a4[tt - 2][0]))
            for oc in range(2):
                for ic in range(8):
                    mm(pb[:, oc * 512:(oc + 1) * 512],
                       obuf(s, ic)[:, tt * 128:(tt + 1) * 128],
                       wvo(ic)[:, oc * 512:(oc + 1) * 512], ic == 0, ic == 7,
                       w0 if (oc == 0 and ic == 0) else ())
            ta4 = P.n("pt")
            OP("v", lambda e, pb=pb: e.bn_stats(bn6[:, 0:6], pb[:, 0:512]), [("pt", ta4)], "pv")
            OP("v", lambda e, pb=pb: e.bn_stats(bn6[:, 6:12], pb[:, 512:1024]),
               [("pv", P.n("pv"))], "pv")
            OP("v", lambda e: e.bn_aggr(mv2[:, 0:2], bn6[:, 0:12]), [("pv", P.n("pv"))], "pv")
            OP("v", lambda e: e.tensor_scalar(mv2[:, 2:3], mv2[:, 1:2], 1.0, EPS_OP,
                                              ALU.mult, ALU.add), [("pv", P.n("pv"))], "pv")
            rsq(mv2[:, 3:4], mv2[:, 2:3], [("pv", P.n("pv"))])
            sr4 = P.n("ps")
            OP("v", lambda e, pb=pb: e.tensor_scalar(jF32[:], pb[:], mv2[:, 0:1], mv2[:, 3:4],
                                                     ALU.subtract, ALU.mult), [("ps", sr4)], "pv")
            vstt = P.n("pv")
            slo = tt % 2
            OP("v", lambda e, slo=slo, slx=slx: e.tensor_tensor(
                outst(slo), wsB[:, slx * 1024:(slx + 1) * 1024], jF32[:], ALU.add),
               [("pv", vstt), dyx] + g_x1st[slo], "pv")
            g_x1st[slo] = [DMA("g", lambda e, dst=s_x1[(s * 16 + tt) * 128:
                                                       (s * 16 + tt + 1) * 128, :],
                               slo=slo: e.dma_start(dst, outst(slo)),
                               [("pv", P.n("pv"))], queue="g")]
            g_a4[tt] = (P.n("pv"), P.n("pv"))
        BAR()


    g_ost = {0: [], 1: []}
    g_strep_w = [("pv", P.n("pv")), ("pt", P.n("pt")), ("ps", P.n("ps"))]
    g_xw_free = list(g_strep_w)
    g_xwread = {0: 0, 1: 0}
    g_xqf_t = 0
    g_f1pg = [0]
    g_cin_t = 0
    f0_state = {}

    def emit_F0_dve(b, j, st, pw=()):
        sl = j % 2
        if True:
            w0 = (list(st.get(("ld", j - 2), [])) + list(pw)
                  + [("pv", g_xwread[sl])])
            x1tok = DMA(
                "y", lambda e, sl=sl, src=s_x1[(b * 4 + j) * 128:(b * 4 + j + 1) * 128, :]:
                e.dma_start(xw(sl), src), w0)
            OP("v", lambda e, sl=sl: e.bn_stats(bn6[:, 0:6], xw(sl)[:, 0:512]),
               [x1tok] + ([("pt", st[("pe", j - 2)])] if ("pe", j - 2) in st else []), "pv")
            OP("v", lambda e, sl=sl: e.bn_stats(bn6[:, 6:12], xw(sl)[:, 512:1024]),
               [("pv", P.n("pv"))], "pv")
            OP("v", lambda e: e.bn_aggr(mv2[:, 0:2], bn6[:, 0:12]), [("pv", P.n("pv"))], "pv")
            OP("v", lambda e: e.tensor_scalar(mv2[:, 2:3], mv2[:, 1:2], 1.0, EPS_LN,
                                              ALU.mult, ALU.add), [("pv", P.n("pv"))], "pv")
            rsq(mv2[:, 3:4], mv2[:, 2:3], [("pv", P.n("pv"))])
            s0 = P.n("ps")
            OP("v", lambda e, sl=sl: e.tensor_scalar(
                ln2f[:, sl * 1024:(sl + 1) * 1024], xw(sl),
                mv2[:, 0:1], mv2[:, 3:4], ALU.subtract, ALU.mult),
               [("ps", s0), ("pg", g_f1pg[0])], "pv")
            OP("v", lambda e, sl=sl: e.reduce_max(mv2[:, 4:5], ln2f[:, sl * 1024:(sl + 1) * 1024],
                                                  AX.X, apply_absolute_value=True),
               [("pv", P.n("pv"))], "pv")
            OP("v", lambda e: e.tensor_scalar(mv2[:, 5:6], mv2[:, 4:5], 1e-5, None, ALU.max),
               [("pv", P.n("pv"))], "pv")
            OP("v", lambda e: e.reciprocal(mv2[:, 6:7], mv2[:, 5:6]), [("pv", P.n("pv"))], "pv")
            OP("v", lambda e: e.tensor_scalar(mv2[:, 7:8], mv2[:, 6:7], 127.0, None, ALU.mult),
               [("pv", P.n("pv"))], "pv")
            vq0 = P.n("pv")
            OP("v", lambda e, sl=sl: e.tensor_scalar(i16d[sl][:],
                                                     ln2f[:, sl * 1024:(sl + 1) * 1024],
                                                     mv2[:, 7:8], shiftp[:],
                                                     ALU.mult, ALU.add), [("pv", vq0)], "pv")
            OP("v", lambda e, sl=sl: e.tensor_scalar(xqtm(sl), i16d[sl][:], nshift2[:], None,
                                                     ALU.add), [("pv", P.n("pv"))], "pv")
            wcin = [("pv", vq0)]
            if j == 0:
                wcin.append(("pt", g_cin_t))
            OP("v", lambda e, j=j: e.memset(cin(j)[:], 0.0), wcin, "pv")
            OP("v", lambda e, j=j: e.tensor_tensor(cin(j)[:, 0:1], mv2[:, 5:6], iswg[:],
                                                   ALU.mult), [("pv", P.n("pv"))], "pv")
            OP("v", lambda e, j=j: e.tensor_tensor(cin(j)[:, 32:33], mv2[:, 5:6], iswu[:],
                                                   ALU.mult), [("pv", P.n("pv"))], "pv")
            st[("vc", j)] = P.n("pv")
            st[("ld", j)] = [("pv", P.n("pv"))]

    def emit_F0_pe(b, j, st):
        sl = j % 2
        for jb in range(2):
            pb = pT if jb == 0 else pU
            wtr = [("pv", st[("vc", j)])]
            if jb == 0 and ("ev", j - 1) in st:
                wtr.append(("ps", st[("ev", j - 1)]))
            if j == 0 and jb == 0:
                wtr.append(("pt", g_xqf_t))
            for jj in range(4):
                hc = jb * 4 + jj
                mm(pb[:, jj * 128:(jj + 1) * 128], xqtm(sl)[:, hc * 128:(hc + 1) * 128],
                   ident[:], True, True, wtr if jj == 0 else ())
            ttx = P.n("pt")
            def fex(e, j=j, jb=jb, pb=pb):
                reg = big[:, XQ0 + jb * 2048: XQ0 + (jb + 1) * 2048]
                dst = reg.rearrange("p (a x) -> p a x", x=512)[
                    :, :, j * 128:(j + 1) * 128]
                return e.activation(dst, pb[:, 0:512].rearrange("p (a x) -> p a x", x=128),
                                    AF.Copy)
            OP("s", fex, [("pt", ttx), ("ps", P.n("ps"))], "ps")
        st[("pe", j)] = P.n("pt")
        st[("ev", j)] = P.n("ps")

    xw = lambda sl: wsB[:, 2048 + sl * 1024:2048 + (sl + 1) * 1024]

    obufT_f = obufT.bitcast(F32)
    qkbuf_f = qkbuf2.bitcast(F32)
    wqk2_f = wqk2.bitcast(F32)

    def tdst(c):
        if c < 16:
            return obufT_f[:, c * 512:(c + 1) * 512]
        if c < 24:
            return qkbuf_f[:, (c - 16) * 512:(c - 15) * 512]
        if c < 28:
            return wqk2_f[:, (c - 24) * 512:(c - 23) * 512]
        return wsB[:, (c - 28) * 512:(c - 27) * 512]

    def tdst2(p):
        c = 2 * p
        if c < 16:
            return obufT_f[:, c * 512:(c + 2) * 512]
        if c < 24:
            return qkbuf_f[:, (c - 16) * 512:(c - 14) * 512]
        if c < 28:
            return wqk2_f[:, (c - 24) * 512:(c - 22) * 512]
        return wsB[:, (c - 28) * 512:(c - 26) * 512]

    st_cur = {}
    for j in range(4):
        emit_F0_dve(0, j, st_cur, pw=list(g_xw_free) if j == 0 else ())
        emit_F0_pe(0, j, st_cur)
    f0_ready = (st_cur[("ev", 3)], st_cur[("vc", 3)])
    g_prev_f2 = {}
    crep_v = [0]

    def emit_crep(frps, frpv):
        nonlocal g_cin_t
        for j in range(4):
            mm(pT[:, j * 128:(j + 1) * 128], cin(j)[:], ident[:], True, True,
               [("ps", frps), ("pv", frpv)] if j == 0 else ())
        tci = P.n("pt")
        OP("v", lambda e: e.tensor_copy(big[:, CI0 + 512:CI0 + 1024], pT[:, 0:512]),
           [("pt", tci)], "pv")
        vci = P.n("pv")
        for cls in range(2):
            basep = cls * 32
            for j in range(4):
                mm(pU[:, j * 128:(j + 1) * 128], onesb[basep:basep + 1, :],
                   big[basep:basep + 1, CI0 + 512 + j * 128: CI0 + 512 + (j + 1) * 128],
                   True, True, [("pv", vci)] if j == 0 else (),
                   tp=(basep, 0) if basep else None)
            tcr = P.n("pt")
            OP("v", lambda e, cls=cls: e.tensor_copy(crepf[:, cls * 512:(cls + 1) * 512],
                                                     pU[:, 0:512]), [("pt", tcr)], "pv")
            vci = P.n("pv")
        g_cin_t = P.n("pt")
        crep_v[0] = P.n("pv")

    for b in range(NB):
        if b == 0:
            emit_crep(f0_ready[0], f0_ready[1])

        OP("v", lambda e: e.memset(ln2f[:, 0:512], 0.0),
           [("pv", crep_v[0]), ("pg", g_f1pg[0])], "pv")
        OP("v", lambda e: e.memset(ln2f[:, 512:1024], 0.0), [("pv", P.n("pv"))], "pv")
        vM1 = P.n("pv")
        g_f1 = {}
        for c in range(32):
            slw = c % 4
            if c >= 4:
                w = [("pt", g_f1[c - 4][0])]
            elif g_prev_f2:
                w = [("pt", g_prev_f2[24 + 2 * c]), ("pt", g_prev_f2[25 + 2 * c])]
            else:
                w = [("pt", P.n("pt"))]
                if c == 2:
                    w += prep_stoks
            t1 = DMA("y", lambda e, c=c, slw=slw: e.dma_start(
                wsA[:, slw * 2048: slw * 2048 + 1024], s_wg[c * 128:(c + 1) * 128, :]), w)
            t2 = DMA("y", lambda e, c=c, slw=slw: e.dma_start(
                wsA[:, slw * 2048 + 1024: slw * 2048 + 2048], s_wu[c * 128:(c + 1) * 128, :]), ())
            pb = pS if c % 2 == 0 else pS2
            w0 = [t1, t2]
            if c >= 2:
                w0.append(("pv", g_f1[c - 2][1]))
            else:
                w0.append(("pv", vM1))
            for wsel in range(2):
                for hc in range(8):
                    mm(pb[:, wsel * 512:(wsel + 1) * 512],
                       wsA[:, slw * 2048 + wsel * 1024 + hc * 128:
                           slw * 2048 + wsel * 1024 + (hc + 1) * 128],
                       xqf(hc), hc == 0, hc == 7, w0 if (wsel == 0 and hc == 0) else ())
            tf1 = P.n("pt")
            wcm = [("pt", tf1)]
            if (c - 1, "vt") in g_f1:
                wcm.append(("pv", g_f1[(c - 1, "vt")]))
            OP("v", lambda e, pb=pb: e.tensor_tensor(jF32[:, 0:512], pb[:, 0:512],
                                                     crepf[:, 0:512], ALU.mult),
               wcm, "pv")
            vg1 = P.n("pv")
            OP("s", lambda e: e.activation(jF32[:, 0:512], jF32[:, 0:512], AF.Silu),
               [("pv", vg1)], "ps")
            sg2 = P.n("ps")
            OP("v", lambda e, pb=pb: e.tensor_tensor(jF32[:, 512:1024], pb[:, 512:1024],
                                                     crepf[:, 512:1024], ALU.mult),
               [("pt", tf1)], "pv")
            wtw = [("ps", sg2), ("pv", P.n("pv"))]
            if c < 4:
                wtw += list(g_strep_w)
            OP("v", lambda e, c=c: e.tensor_tensor(tdst(c), jF32[:, 0:512],
                                                   jF32[:, 512:1024], ALU.mult),
               wtw, "pv")
            vt = P.n("pv")
            sqb = lambda c: ln2f[:, 1024 + (c % 2) * 512: 1536 + (c % 2) * 512]
            wsq = [("pv", vt)]
            if (c - 2, "vmax") in g_f1:
                wsq.append(("pv", g_f1[(c - 2, "vmax")]))
            OP("s", lambda e, c=c: e.activation(sqb(c), tdst(c), AF.Square),
               wsq, "ps")
            OP("v", lambda e, c=c: e.tensor_tensor(ln2f[:, 0:512], ln2f[:, 0:512],
                                                   sqb(c), ALU.max),
               [("ps", P.n("ps"))], "pv")
            g_f1[(c, "vmax")] = P.n("pv")
            g_f1[(c, "vt")] = P.n("pv")
            g_f1[c] = (P.n("pt"), P.n("pv"))
        g_xqf_t = P.n("pt")
        g_f1pg[0] = P.n("pg")
        f1v = P.n("pv")
        OP("v", lambda e: e.tensor_copy(junkA[:, 0:512], ln2f[:, 0:512]),
           [("pv", f1v)], "pv")
        vMb = P.n("pv")
        for j in range(4):
            mm(pT[:, j * 128:(j + 1) * 128], junkA[:, j * 128:j * 128 + 128],
               ident[:], True, True, [("pv", vMb)] if j == 0 else ())
        tM = P.n("pt")
        OP("v", lambda e: e.reduce_max(smal[:, 100:104], pT[:, 0:512].rearrange(
            "p (a x) -> p a x", x=128), AX.X), [("pt", tM)], "pv")
        OP("v", lambda e: e.tensor_scalar(smal[:, 104:108], smal[:, 100:104], 1e-10, None,
                                          ALU.max), [("pv", P.n("pv"))], "pv")
        rsq(smal[:, 108:112], smal[:, 104:108], [("pv", P.n("pv"))])
        sMr = P.n("ps")
        OP("v", lambda e: e.tensor_scalar(smal[:, 112:116], smal[:, 108:112], 127.0, None,
                                          ALU.mult), [("ps", sMr)], "pv")
        v127 = P.n("pv")
        sttoks = []
        for j in range(4):
            sttoks.append(DMA("g", lambda e, j=j: e.dma_start(
                rrow[0:1, j * 128:(j + 1) * 128], smal[:, 112 + j:113 + j]),
                [("pv", v127)] + (list(g_strep_w) if j == 0 else []), queue="g"))
        OP("v", lambda e: e.reciprocal(smal[:, 116:120], smal[:, 104:108]),
           [("pv", P.n("pv"))], "pv")
        OP("v", lambda e: e.tensor_tensor(smal[:, 116:120], smal[:, 116:120],
                                          kapp[:].broadcast_to([128, 4]), ALU.mult),
           [("pv", P.n("pv"))], "pv")
        vst0 = P.n("pv")
        OP("v", lambda e: e.tensor_copy(junkA[0:1, 0:512], rrow[0:1, 0:512]), sttoks, "pv")
        OP("v", lambda e: e.tensor_sub(junkA[0:1, 512:1024], rrow[0:1, 0:512],
                                       junkA[0:1, 0:512]), [("pv", P.n("pv"))], "pv")
        vhl = P.n("pv")
        for j in range(4):
            mm(pU[:, j * 128:(j + 1) * 128], onesb[0:1, :],
               junkA[0:1, j * 128:(j + 1) * 128], True, False,
               [("pv", vhl)] if j == 0 else ())
            mm(pU[:, j * 128:(j + 1) * 128], onesb[0:1, :],
               junkA[0:1, 512 + j * 128:512 + (j + 1) * 128], False, True, ())
        tstr = P.n("pt")
        OP("v", lambda e: e.tensor_copy(crepf[:, 1024:1536], pU[:, 0:512]), [("pt", tstr)], "pv")
        v_strep = P.n("pv")

        g_half_done = {}
        g_wd2 = {}
        have_next = b + 1 < NB
        st_new = {}

        def emit_epi(half, tdn, st_new, have_next):
            epi_v = []
            rel = 0
            for j in range(2):
                tt = half * 2 + j
                pb = pS if j == 0 else pS2
                sl = j % 2
                scr = wsB[:, sl * 1024:(sl + 1) * 1024]
                wx2 = [("pt", tdn), ("pv", g_xwread[sl])]
                if have_next:
                    wx2 += list(st_new[("ld", half * 2 + sl)])
                x1tok2 = DMA("y", lambda e, sl=sl, src=s_x1[(b * 4 + tt) * 128:
                                                            (b * 4 + tt + 1) * 128, :]:
                             e.dma_start(xw(sl), src), wx2)
                OP("v", lambda e, pb=pb, scr=scr: e.tensor_copy(scr[:, 0:512],
                                                                pb[:, 0:512]),
                   [("pt", tdn)], "pv")
                OP("v", lambda e, pb=pb, scr=scr: e.tensor_copy(scr[:, 512:1024],
                                                                pb[:, 512:1024]),
                   [("pv", P.n("pv"))], "pv")
                rel = P.n("pv")
                OP("v", lambda e, scr=scr: e.bn_stats(bn6[:, 0:6], scr[:, 0:512]),
                   [("pv", P.n("pv"))], "pv")
                OP("v", lambda e, scr=scr: e.bn_stats(bn6[:, 6:12], scr[:, 512:1024]),
                   [("pv", P.n("pv"))], "pv")
                OP("v", lambda e: e.bn_aggr(mv2[:, 0:2], bn6[:, 0:12]), [("pv", P.n("pv"))], "pv")
                OP("v", lambda e, tt=tt: e.tensor_tensor(mv2[:, 2:3], mv2[:, 1:2],
                                                         smal[:, 116 + tt:117 + tt], ALU.add),
                   [("pv", P.n("pv"))], "pv")
                rsq(mv2[:, 3:4], mv2[:, 2:3], [("pv", P.n("pv"))])
                srj = P.n("ps")
                OP("v", lambda e, scr=scr: e.tensor_scalar(jF32[:], scr, mv2[:, 0:1],
                                                           mv2[:, 3:4], ALU.subtract, ALU.mult),
                   [("ps", srj)], "pv")
                slo = tt % 2
                OP("v", lambda e, slo=slo, sl=sl: e.tensor_tensor(
                    outst(slo), xw(sl), jF32[:], ALU.add),
                   [("pv", P.n("pv")), x1tok2] + g_ost[slo], "pv")
                vfin = P.n("pv")
                g_xwread[sl] = vfin
                g_ost[slo] = [DMA("g", lambda e, b=b, tt=tt, slo=slo: e.dma_start(
                    out_hbm[(b * 4 + tt) * 128:(b * 4 + tt + 1) * 128, :], outst(slo)),
                    [("pv", vfin)], queue="pg" if False else "g")]
                epi_v.append(("pv", P.n("pv")))
            return epi_v, rel

        g_q2 = {}
        g_qs = {}
        for p in range(16):
            dy2 = []
            for i2 in (2 * p, 2 * p + 1):
                slw = i2 % 8
                if i2 >= 8:
                    w = [("pt", g_wd2[i2 - 8])]
                else:
                    w = [("pt", g_f1[28 + i2 // 2][0])]
                dy2.append(DMA("y", lambda e, i2=i2, slw=slw: e.dma_start(
                    wsA[:, slw * 1024:(slw + 1) * 1024],
                    s_wd[i2 * 128:(i2 + 1) * 128, :]), w))
            w1 = [("pv", v_strep)]
            if p - 1 in g_qs:
                w1.append(("ps", g_qs[p - 1]))
            OP("v", lambda e, p=p: e.tensor_tensor(
                jF32[:].rearrange("q (a x) -> q a x", x=512),
                tdst2(p).rearrange("q (a x) -> q a x", x=512),
                crepf[:, 1024:1536].unsqueeze(1).broadcast_to([128, 2, 512]),
                ALU.mult), w1, "pv")
            vq1 = P.n("pv")
            wqs = [("pv", vq1)]
            if p - 1 in g_q2:
                wqs.append(("pv", g_q2[p - 1]))
            OP("s", lambda e: e.activation(i16d[0][:], jF32[:], AF.Identity,
                                           bias=shiftp[:]), wqs, "ps")
            g_qs[p] = P.n("ps")
            OP("v", lambda e, p=p: e.tensor_scalar(
                big[:, VB0 + p * 1024: VB0 + (p + 1) * 1024], i16d[0][:],
                nshift2[:], None, ALU.add), [("ps", g_qs[p])], "pv")
            g_q2[p] = P.n("pv")
            for i2 in (2 * p, 2 * p + 1):
                slw = i2 % 8
                wmm = [dy2[i2 - 2 * p], ("pv", g_q2[p])]
                for j in range(2):
                    for oc in range(2):
                        mm((pS if j == 0 else pS2)[:, oc * 512:(oc + 1) * 512],
                           tblk(i2)[:, j * 128:(j + 1) * 128],
                           wsA[:, slw * 1024 + oc * 512: slw * 1024 + (oc + 1) * 512],
                           i2 == 0, i2 == 31, wmm if (j == 0 and oc == 0) else ())
                g_wd2[i2] = P.n("pt")
            if have_next and p == 6:
                emit_F0_dve(b + 1, 0, st_new, pw=list(g_xw_free))
            elif have_next and p == 10:
                emit_F0_dve(b + 1, 1, st_new)
        tdn = P.n("pt")
        g_wd2_h0 = dict(g_wd2)
        g_half_done[0], rel0 = emit_epi(0, tdn, st_new, have_next)

        for i2 in range(32):
            slw = i2 % 8
            if i2 >= 8:
                w = [("pt", g_wd2[i2 - 8])]
            else:
                w = [("pt", g_wd2_h0[24 + i2])]
            dyd = DMA("y", lambda e, i2=i2, slw=slw: e.dma_start(
                wsA[:, slw * 1024:(slw + 1) * 1024], s_wd[i2 * 128:(i2 + 1) * 128, :]), w)
            wmm = [dyd]
            if i2 == 0:
                wmm.append(("pv", rel0))
            for j in range(2):
                tt = 2 + j
                for oc in range(2):
                    mm((pS if j == 0 else pS2)[:, oc * 512:(oc + 1) * 512],
                       tblk(i2)[:, tt * 128:(tt + 1) * 128],
                       wsA[:, slw * 1024 + oc * 512: slw * 1024 + (oc + 1) * 512],
                       i2 == 0, i2 == 31, wmm if (j == 0 and oc == 0) else ())
            g_wd2[i2] = P.n("pt")
            if have_next:
                if i2 == 2:
                    emit_F0_pe(b + 1, 0, st_new)
                elif i2 == 6:
                    emit_F0_pe(b + 1, 1, st_new)
                elif i2 == 8:
                    emit_F0_dve(b + 1, 2, st_new, pw=list(g_half_done[0]))
                elif i2 == 13:
                    emit_F0_dve(b + 1, 3, st_new)
                elif i2 == 14:
                    emit_F0_pe(b + 1, 2, st_new)
                elif i2 == 22:
                    emit_F0_pe(b + 1, 3, st_new)
                elif i2 == 24:
                    emit_crep(st_new[("ev", 3)], st_new[("vc", 3)])
        tdn = P.n("pt")
        g_half_done[1], _rel1 = emit_epi(1, tdn, st_new, have_next)
        if have_next:
            f0_ready = (st_new[("ev", 3)], st_new[("vc", 3)])
            st_cur = st_new
        g_prev_f2 = dict(g_wd2)
        g_strep_w = [("pv", P.n("pv"))]
        g_xw_free = list(g_half_done[1])

    dma_finals = [(sn, P.n(sn)) for sn in semnames if sn.startswith("d") and P.n(sn) > 0]

    eng_ops = {e: [] for e in ("y", "s", "v", "t", "g")}
    for (e, fn, waits, incs) in P.ops:
        eng_ops[e].append((fn, waits, incs))

    with es, nc.Block() as block:
        def runner(name):
            def run(eng):
                for (fn, waits, incs) in eng_ops[name]:
                    for (sn, val) in waits:
                        eng.wait_ge(sems[sn], val)
                    inst = fn(eng)
                    for (sn, k) in incs:
                        inst.then_inc(sems[sn], k)
                if name == "y":
                    for sn, v in dma_finals:
                        eng.wait_ge(sems[sn], v)
            return run
        block.sync(runner("y"))
        block.scalar(runner("s"))
        block.vector(runner("v"))
        block.tensor(runner("t"))
        block.gpsimd(runner("g"))

    return nc


_NC_CACHE = []
TAP = {}


def kernel(**inputs):
    x = np.ascontiguousarray(np.asarray(inputs["L0_attn_inputs"], dtype=np.float32))
    shards = x.reshape(NC, SEQ * L, D)
    base = {
        "wqk": np.ascontiguousarray(np.asarray(inputs["L0_attn_W_qk"], np.float32)),
        "wv": np.ascontiguousarray(np.asarray(inputs["L0_attn_W_v"], np.float32)),
        "wout": np.ascontiguousarray(np.asarray(inputs["L0_attn_W_out"], np.float32)),
        "qs": np.asarray(inputs["L0_attn_q_scale"], np.float32).reshape(1, HD),
        "qb": np.asarray(inputs["L0_attn_q_bias"], np.float32).reshape(1, HD),
        "ks": np.asarray(inputs["L0_attn_k_scale"], np.float32).reshape(1, HD),
        "kb": np.asarray(inputs["L0_attn_k_bias"], np.float32).reshape(1, HD),
        "wg": np.ascontiguousarray(np.asarray(inputs["L0_ffn_w_gate"], np.float32)),
        "wu": np.ascontiguousarray(np.asarray(inputs["L0_ffn_w_up"], np.float32)),
        "wd": np.ascontiguousarray(np.asarray(inputs["L0_ffn_w_down"], np.float32)),
    }
    nc = build()
    in_maps = [dict(base, x_in=np.ascontiguousarray(shards[c])) for c in range(NC)]
    res = run_bass_kernel_spmd(nc, in_maps, core_ids=list(range(NC)))
    out = np.stack([res.results[c]["out"] for c in range(NC)], axis=0)
    return out.reshape(B, L, D).astype(np.float32)
